# version 11
# baseline (speedup 1.0000x reference)
import sys
sys.path.insert(0, "/opt/trn_rl_repo")
import numpy as np
import concourse.bass as bass
import concourse.mybir as mybir
from concourse.bass_utils import run_bass_kernel_spmd



ENGS = ["tensor", "vector", "scalar", "gpsimd", "sync"]


def _names(args):
    out = []
    for a in args:
        if a is None or isinstance(a, (int, float)):
            continue
        t = getattr(a, "tensor", None)
        if t is not None:
            nm = getattr(t, "name", None)
            if nm is not None:
                if type(t).__name__ == "DRamTensorHandle":
                    nm = f"{nm}@{getattr(a, 'offset', 0)}"
                out.append(nm)
    return out


class _Facade:
    def __init__(self, prog, eng):
        self._p = prog
        self._e = eng

    def _st(self, reads, writes, inc, fn, dma=False):
        self._p.stage(self._e, fn, _names(reads), _names(writes), inc=inc, dma=dma)

    def dma_start(self, out, in_, **kw):
        self._st([in_], [out], True, lambda e: e.dma_start(out, in_, **kw), dma=True)

    def matmul(self, out, lhsT, rhs, start=True, stop=True, **kw):
        self._st([lhsT, rhs], [out], True,
                 lambda e: e.matmul(out, lhsT, rhs, start=start, stop=stop, **kw))

    def transpose(self, out, in_, ident, **kw):
        self._st([in_, ident], [out], True, lambda e: e.transpose(out, in_, ident, **kw))

    def activation(self, out, in_, func, bias=None, scale=None, accum_out=None):
        kw = {}
        if bias is not None:
            kw["bias"] = bias
        if scale is not None:
            kw["scale"] = scale
        if accum_out is not None:
            kw["accum_out"] = accum_out
        self._st([in_, bias, scale], [out, accum_out], True,
                 lambda e: e.activation(out, in_, func, **kw))

    def copy(self, out, in_):
        self._st([in_], [out], True, lambda e: e.copy(out, in_))

    def memzero(self, ap):
        self._st([], [ap], True, lambda e: e.memzero(ap))

    def tensor_copy(self, out, in_):
        self._st([in_], [out], True, lambda e: e.tensor_copy(out, in_))

    def memset(self, ap, c):
        self._st([], [ap], True, lambda e: e.memset(ap, c))

    def iota(self, ap, pattern, **kw):
        self._st([], [ap], True, lambda e: e.iota(ap, pattern, **kw))

    def tensor_tensor(self, out, in0, in1, op):
        self._st([in0, in1], [out], True, lambda e: e.tensor_tensor(out, in0, in1, op))

    def tensor_scalar(self, out, in0, s1, s2, op0, op1=None, accum_out=None):
        kw = dict(accum_out=accum_out) if accum_out is not None else {}
        if op1 is None:
            fn = lambda e: e.tensor_scalar(out, in0, s1, s2, op0, **kw)
        else:
            fn = lambda e: e.tensor_scalar(out, in0, s1, s2, op0, op1, **kw)
        self._st([in0, s1, s2], [out, accum_out], True, fn)

    def scalar_tensor_tensor(self, out, in0, scalar, in1, op0, op1):
        self._st([in0, scalar, in1], [out], True,
                 lambda e: e.scalar_tensor_tensor(out, in0, scalar, in1, op0, op1))

    def bn_stats(self, out, in_):
        self._st([in_], [out], True, lambda e: e.bn_stats(out, in_))

    def bn_aggr(self, out, in_):
        self._st([in_], [out], True, lambda e: e.bn_aggr(out, in_))

    def reciprocal(self, out, in_):
        self._st([in_], [out], True, lambda e: e.reciprocal(out, in_))

    def max(self, out, in_):
        self._st([in_], [out], True, lambda e: e.max(out, in_))

    def reduce_max(self, out, in_, axis, **kw):
        self._st([in_], [out], True, lambda e: e.reduce_max(out, in_, axis, **kw))

    def reduce_sum(self, out, in_, axis, **kw):
        self._st([in_], [out], True, lambda e: e.reduce_sum(out, in_, axis, **kw))

    def select(self, out, mask, a, b):
        self._st([mask, a, b], [out], True, lambda e: e.select(out, mask, a, b))

    def raw(self, fn, reads=(), writes=(), inc=True):
        self._st(list(reads), list(writes), inc, fn)


class Prog:
    def __init__(self, nc):
        self.nc = nc
        self.cnt_sem = {e: nc.alloc_semaphore(name=f"cnt_{e}") for e in ENGS}
        self.pD = nc.alloc_semaphore(name="pDr")
        self.cnt = {e: 0 for e in ENGS}
        self.NDQ = 8
        self.dma_sem = {e: [nc.alloc_semaphore(name=f"dq_{e}_{k}") for k in range(self.NDQ)]
                        for e in ("sync", "gpsimd", "scalar")}
        self.dma_cnt = {e: [0] * self.NDQ for e in ("sync", "gpsimd", "scalar")}
        self.dma_n = {e: 0 for e in ("sync", "gpsimd", "scalar")}
        self.pe_fifo = False
        self.last_w = {}
        self.readers = {}
        self.cur = {e: {} for e in ENGS}
        self.pD_n = 0
        self.ops = {e: [] for e in ENGS}
        for e in ENGS:
            setattr(self, {"tensor": "t", "vector": "v", "scalar": "s",
                           "gpsimd": "g", "sync": "y"}[e], _Facade(self, e))

    def all_sems(self):
        out = list(self.cnt_sem.values()) + [self.pD]
        for sems in self.dma_sem.values():
            out.extend(sems)
        return out

    def _sem_of(self, key):
        kind, ident = key
        if kind == "e":
            return self.cnt_sem[ident]
        e, k = ident
        return self.dma_sem[e][k]

    def _need(self, eng, waits, key, idx):
        kind, ident = key
        if kind == "e" and ident == eng:
            if idx > self.cnt[eng]:
                return
            if eng == "tensor" and self.pe_fifo:
                return
        cur = self.cur[eng].get(key, 0)
        if idx <= cur:
            return
        waits.append((key, idx))
        self.cur[eng][key] = idx

    def stage(self, eng, fn, rn, wn, inc=True, dma=False):
        waits = []
        for nm in rn:
            w = self.last_w.get(nm)
            if w is not None:
                self._need(eng, waits, (w[0], w[1]), w[2])
        for nm in wn:
            w = self.last_w.get(nm)
            if w is not None:
                self._need(eng, waits, (w[0], w[1]), w[2])
            for key, idx in self.readers.get(nm, {}).items():
                self._need(eng, waits, key, idx)
        if dma:
            k = self.dma_n[eng] % self.NDQ
            self.dma_n[eng] += 1
            if self.dma_cnt[eng][k]:
                self._need(eng, waits, ("dm", (eng, k)), self.dma_cnt[eng][k])
            self.dma_cnt[eng][k] += 16
            marker = ("dm", (eng, k), self.dma_cnt[eng][k])
            inc_spec = ("dm", (eng, k), 16)
        elif inc:
            self.cnt[eng] += 1
            marker = ("e", eng, self.cnt[eng])
            inc_spec = ("e", eng, 1)
        else:
            marker = ("e", eng, self.cnt[eng] + 1)
            inc_spec = None
        for nm in wn:
            self.last_w[nm] = marker
            self.readers[nm] = {}
        for nm in rn:
            self.readers.setdefault(nm, {})[(marker[0], marker[1])] = marker[2]
        self.ops[eng].append((waits, fn, inc_spec))

    def raw_stage(self, eng, fn):
        self.ops[eng].append(([], fn, None))

    def rendezvous(self):
        self.pD_n += len(ENGS)
        target = self.pD_n
        for e in ENGS:
            cnt = self.cnt[e]
            dts = []
            if e in self.dma_cnt:
                dts = [(self.dma_sem[e][k], c) for k, c in enumerate(self.dma_cnt[e]) if c]

            def f(eng, e=e, cnt=cnt, dts=dts, target=target):
                if cnt:
                    eng.wait_ge(self.cnt_sem[e], cnt)
                for s, c in dts:
                    eng.wait_ge(s, c)
                eng.sem_inc(self.pD, 1)
                eng.wait_ge(self.pD, target)
            self.ops[e].append((None, f, "RAW"))
        self.last_w.clear()
        self.readers.clear()
        for e in ENGS:
            self.cur[e] = {}

    def flush(self, block):
        for e in ENGS:
            ops = self.ops[e]
            if not ops:
                continue

            def section(eng, ops=ops, e=e):
                for waits, fn, inc_spec in ops:
                    if waits is None:
                        fn(eng)
                        continue
                    for key, val in (waits or ()):
                        eng.wait_ge(self._sem_of(key), val)
                    instr = fn(eng)
                    if inc_spec == "RAW":
                        continue
                    if inc_spec is not None:
                        k, ident, amt = inc_spec
                        sem = self._sem_of((k, ident))
                        instr.then_inc(sem, amt)
            getattr(block, e)(section)
        self.ops = {e: [] for e in ENGS}




F32 = mybir.dt.float32
BF16 = mybir.dt.bfloat16
AF = mybir.ActivationFunctionType
ALU = mybir.AluOpType

NT = 32
TL = 128
D = 1024
NCH = 8
H = 16
HD = 64
GT = 4
EPS = 1e-5

WEIGHT_NAMES = ["wq", "wk", "wv", "wo", "bqw", "bkw", "bvw", "bow", "fr", "fk", "fv"]


def build_program(debug=0):
    nc = bass.Bass("TRN2", target_bir_lowering=False)

    x_h = nc.dram_tensor("x0", [NT * TL, D], F32, kind="ExternalInput")
    win = {}
    for w in WEIGHT_NAMES:
        win[w] = nc.dram_tensor(w + "_t", [D, D], F32, kind="ExternalInput")
    for v in ["bq_c", "bk_c", "qb_c", "kb_c", "mur_c", "muk_c"]:
        win[v] = nc.dram_tensor(v, [128, NCH], F32, kind="ExternalInput")
    for v in ["bv_r", "bo_r", "gam_r", "bet_r", "vb_r", "ob_r"]:
        win[v] = nc.dram_tensor(v, [1, D], F32, kind="ExternalInput")
    bi_h = nc.dram_tensor("bi", [1, 2048], F32, kind="ExternalInput")

    out_h = nc.dram_tensor("out", [NT * TL, D], F32, kind="ExternalOutput")
    dbg = "ExternalOutput" if debug else "Internal"
    ctx_h = nc.dram_tensor("ctxh", [NT, 128, NCH * 128], BF16, kind=dbg)
    x2_h = nc.dram_tensor("x2h", [NT * TL, D], F32, kind=dbg)
    x3_h = nc.dram_tensor("x3h", [NT * TL, D], F32, kind=dbg)
    kdbg_h = vdbg_h = None
    if debug:
        kdbg_h = nc.dram_tensor("kdbg", [NT, 128, D], BF16, kind="ExternalOutput")
        vdbg_h = nc.dram_tensor("vdbg", [NT, 128, 1040], BF16, kind="ExternalOutput")

    pr = Prog(nc)
    from contextlib import ExitStack
    with ExitStack() as top:
        def sb(name, shape, dtype):
            return top.enter_context(nc.sbuf_tensor(name, shape, dtype))

        ident = sb("ident", [128, 128], F32)
        onesf = sb("onesf", [1, 128], F32)
        cm = sb("cm", [128, 16], F32)
        epsc = sb("epsc", [128, 1], F32)
        jt = sb("jt", [128, 512], F32)
        preg = sb("preg", [128, 64], F32)
        prev = sb("prev", [128, 64], F32)
        pres = sb("pres", [128, 64], F32)
        env0 = dict(preg=preg, prev=prev, pres=pres)

        env = dict(x_h=x_h, win=win, bi_h=bi_h, out_h=out_h, ctx_h=ctx_h,
                   x2_h=x2_h, x3_h=x3_h, kdbg_h=kdbg_h, vdbg_h=vdbg_h,
                   ident=ident, onesf=onesf, cm=cm, epsc=epsc, jt=jt, **env0)

        with nc.Block() as block:
            emit_start(nc, pr, env)

            with ExitStack() as p12:
                kres = p12.enter_context(nc.sbuf_tensor("kres", [128, NT * D], BF16))
                vres = p12.enter_context(nc.sbuf_tensor("vres", [128, NT * 1040], BF16))
                env["kres"], env["vres"] = kres, vres

                with ExitStack() as p1s:
                    pr.pe_fifo = True
                    emit_phase1(nc, pr, env, p1s)
                    pr.pe_fifo = False
                pr.rendezvous()

                if debug == 2:
                    for t in range(NT):
                        pr.y.dma_start(kdbg_h.ap()[t], kres[:, t * D:(t + 1) * D])
                        pr.y.dma_start(vdbg_h.ap()[t], vres[:, t * 1040:(t + 1) * 1040])
                    pr.rendezvous()
                    pr.flush(block)
                    return nc
                with ExitStack() as p2s:
                    pr.pe_fifo = True
                    emit_phase2(nc, pr, env, p2s)
                    pr.pe_fifo = False
                pr.rendezvous()
                if debug == 3:
                    pr.flush(block)
                    return nc
            with ExitStack() as p3s:
                pr.pe_fifo = True
                emit_phase3a(nc, pr, env, p3s)
                pr.pe_fifo = False
            pr.rendezvous()
            if debug == 4:
                pr.flush(block)
                return nc
            with ExitStack() as p3bs:
                pr.pe_fifo = True
                emit_phase3b(nc, pr, env, p3bs)
                pr.pe_fifo = False
            pr.rendezvous()
            if debug == 5:
                pr.flush(block)
                return nc
            with ExitStack() as p3cs:
                pr.pe_fifo = True
                emit_phase3c(nc, pr, env, p3cs)
                pr.pe_fifo = False
            pr.rendezvous()
            pr.flush(block)
    return nc


def ln_chain(pr, stat, mv, col, epsc, xb, out_f):
    pr.v.bn_stats(stat[:, 0:6], xb[:, 0:512])
    pr.v.bn_stats(stat[:, 6:12], xb[:, 512:1024])
    pr.v.bn_aggr(mv[:], stat[:])
    pr.s.activation(col[:, 0:1], mv[:, 1:2], AF.Sqrt, bias=epsc[:, 0:1])
    pr.v.reciprocal(col[:, 1:2], col[:, 0:1])
    pr.v.tensor_scalar(out_f[:], xb[:], mv[:, 0:1], col[:, 1:2],
                       ALU.subtract, ALU.mult)


def emit_phase3b(nc, pr, env, st):
    ident, onesf, jt, epsc = (env[k] for k in ["ident", "onesf", "jt", "epsc"])
    win, x2_h, x3_h = env["win"], env["x2_h"], env["x3_h"]

    def sbt(name, shape, dtype):
        return st.enter_context(nc.sbuf_tensor(name, shape, dtype))

    def pst(name, shape, dtype=F32):
        return st.enter_context(nc.psum_tensor(name, shape, dtype))

    wq_sb = sbt("b_wq", [128, NCH * D], BF16)
    wk_sb = sbt("b_wk", [128, NCH * D], BF16)
    wv_sb = sbt("b_wv", [128, NCH * D], BF16)
    wo_sb = sbt("b_wo", [128, NCH * D], BF16)
    qb_c = sbt("b_qbc", [128, NCH], F32)
    kb_c = sbt("b_kbc", [128, NCH], F32)
    rows = sbt("b_rows", [1, 2 * D], F32)
    vbb = sbt("b_vbb", [128, D], F32)
    obb = sbt("b_obb", [128, D], F32)
    x2r = [sbt(f"b_x2_{i}", [128, D], F32) for i in range(3)]
    stat = sbt("b_stat", [128, 12], F32)
    mv = [sbt(f"b_mv{i}", [128, 2], F32) for i in range(2)]
    col = [sbt(f"b_col{i}", [128, 2], F32) for i in range(2)]
    ln_f = [sbt(f"b_ln{i}", [128, D], F32) for i in range(2)]
    lnT = [sbt(f"b_lnT{i}", [128, NCH * 128], BF16) for i in range(2)]
    qbT = [sbt(f"b_qbT{i}", [128, NCH * 128], BF16) for i in range(2)]
    kbT = [sbt(f"b_kbT{i}", [128, NCH * 128], BF16) for i in range(3)]
    kbZ = sbt("b_kbZ", [128, NCH * 128], BF16)
    vbr = [sbt(f"b_vb{i}", [128, H * 65], BF16) for i in range(3)]
    vbZ = sbt("b_vbZ", [128, H * 65], BF16)
    masks = [sbt(f"b_msk{i}", [128, NCH * 128], BF16) for i in range(3)]
    ptb = [sbt(f"b_pt{i}", [128, NCH * 128], BF16) for i in range(3)]
    ctxe = sbt("b_ce", [65, NCH * 128], F32)
    ctxs = sbt("b_cs", [65, NCH * 128], F32)
    rcp = sbt("b_rcp", [1, NCH * 128], F32)
    cnT = sbt("b_cnT", [128, NCH * 128], BF16)
    x3sb = [sbt(f"b_x3_{i}", [128, D], F32) for i in range(2)]
    mt1 = sbt("b_mt1", [128, 128], F32)
    mt2 = sbt("b_mt2", [128, 128], F32)
    mt3 = sbt("b_mt3", [128, 128], F32)
    PA = pst("b_PA", [128, 1024])
    PB = pst("b_PB", [128, 1024])
    PC = pst("b_PC", [128, 1024])
    PD = pst("b_PD", [128, 1024])

    for w_sb, nm in ((wq_sb, "bqw"), (wk_sb, "bkw"), (wv_sb, "bvw"), (wo_sb, "bow")):
        src_ = win[nm].ap().rearrange("(j p) d -> p j d", p=128)
        pr.g.dma_start(w_sb[:].rearrange("p (j d) -> p j d", d=D), src_)
    pr.g.dma_start(qb_c[:], win["qb_c"].ap())
    pr.g.dma_start(kb_c[:], win["kb_c"].ap())
    pr.g.dma_start(rows[0:1, 0:D], win["vb_r"].ap())
    pr.g.dma_start(rows[0:1, D:2 * D], win["ob_r"].ap())
    for bt, off in ((vbb, 0), (obb, D)):
        for half in range(2):
            pr.t.matmul(PA[:, half * 512:(half + 1) * 512], onesf[:],
                        rows[0:1, off + half * 512: off + (half + 1) * 512])
        pr.v.tensor_copy(bt[:], PA[:])
    pr.g.iota(mt1[:], [[1, 128]], channel_multiplier=0,
              allow_small_or_imprecise_dtypes=True)
    pr.g.iota(mt2[:], [[0, 128]], channel_multiplier=1,
              allow_small_or_imprecise_dtypes=True)
    pr.v.tensor_tensor(mt2[:], mt2[:], mt1[:], ALU.subtract)
    for s in range(3):
        pr.v.tensor_scalar(mt1[:], mt2[:], float((s - 1) * 128), None, ALU.add)
        pr.v.tensor_scalar(mt3[:], mt1[:], 32.0, None, ALU.is_le)
        pr.v.tensor_scalar(mt1[:], mt1[:], -32.0, None, ALU.is_ge)
        pr.v.tensor_tensor(mt1[:], mt1[:], mt3[:], ALU.mult)
        for jj in range(NCH):
            pr.v.tensor_copy(masks[s][:, jj * 128:(jj + 1) * 128], mt1[:])
    pr.v.memset(kbZ[:], 0.0)
    pr.v.memset(vbZ[:], 0.0)
    pr.v.memset(vbZ[:].rearrange("p (h o) -> p h o", o=65)[:, :, 64:65], 1.0)
    for i in range(3):
        pr.v.memset(vbr[i][:].rearrange("p (h o) -> p h o", o=65)[:, :, 64:65], 1.0)

    for u in range(NT + 1):
        if u < NT:
            p = u % 2
            xb = x2r[u % 3]
            pr.y.dma_start(xb[:], x2_h.ap()[u * TL:(u + 1) * TL, :])
            ln_chain(pr, stat, mv[p], col[p], epsc, xb, ln_f[p])
            for j in range(NCH):
                pr.t.transpose(PD[:, j * 128:(j + 1) * 128],
                               ln_f[p][:, j * 128:(j + 1) * 128], ident[:])
            pr.v.tensor_copy(lnT[p][:], PD[:])
            for wsb, bias, dst in ((wq_sb, qb_c, qbT[p]), (wk_sb, kb_c, kbT[u % 3])):
                for i in range(NCH):
                    for j in range(NCH):
                        pr.t.matmul(PA[:, i * 128:(i + 1) * 128],
                                    wsb[:, j * D + i * 128: j * D + (i + 1) * 128],
                                    lnT[p][:, j * 128:(j + 1) * 128],
                                    start=(j == 0), stop=(j == NCH - 1))
                for i in range(NCH):
                    pr.s.activation(dst[:, i * 128:(i + 1) * 128],
                                    PA[:, i * 128:(i + 1) * 128],
                                    AF.Identity, bias=bias[:, i:i + 1])
            for half in range(2):
                for j in range(NCH):
                    pr.t.matmul(PB[:, half * 512:(half + 1) * 512],
                                lnT[p][:, j * 128:(j + 1) * 128],
                                wv_sb[:, j * D + half * 512: j * D + (half + 1) * 512],
                                start=(j == 0), stop=(j == NCH - 1))
            for half in range(2):
                pr.v.scalar_tensor_tensor(
                    vbr[u % 3][:].rearrange("p (h o) -> p h o", o=65)
                    [:, half * 8:(half + 1) * 8, 0:64],
                    PB[:, half * 512:(half + 1) * 512]
                    .rearrange("p (h o) -> p h o", o=64),
                    1.0,
                    vbb[:, half * 512:(half + 1) * 512]
                    .rearrange("p (h o) -> p h o", o=64),
                    ALU.mult, ALU.add)
        pr.pe_fifo = False
        if u == 0:
            continue
        w = u - 1
        pw = w % 2
        kslots = [kbZ if w == 0 else kbT[(w - 1) % 3], kbT[w % 3],
                  kbZ if w == NT - 1 else kbT[(w + 1) % 3]]
        vslots = [vbZ if w == 0 else vbr[(w - 1) % 3], vbr[w % 3],
                  vbZ if w == NT - 1 else vbr[(w + 1) % 3]]
        PS = [PA, PB, PC]
        for half in range(2):
            for s in range(3):
                for hp in range(4):
                    jj = 4 * half + hp
                    pr.t.matmul(PS[s][:, hp * 128:(hp + 1) * 128],
                                kslots[s][0:64, jj * 128:(jj + 1) * 128],
                                qbT[pw][0:64, jj * 128:(jj + 1) * 128],
                                tile_position=(0, 0))
                    pr.t.matmul(PS[s][:, 512 + hp * 128:640 + hp * 128],
                                kslots[s][64:128, jj * 128:(jj + 1) * 128],
                                qbT[pw][64:128, jj * 128:(jj + 1) * 128],
                                tile_position=(64, 0))
            for s in range(3):
                pr.s.activation(ptb[s][:], PS[s][:], AF.Exp, scale=0.125)
                pr.v.tensor_tensor(ptb[s][:], ptb[s][:], masks[s][:], ALU.mult)
            for hh in range(8):
                h = half * 8 + hh
                ncol = (hh % 2) * 512 + (hh // 2) * 128
                for s in range(3):
                    pr.t.matmul(PD[0:65, hh * 128:(hh + 1) * 128],
                                vslots[s][0:64, h * 65:(h + 1) * 65],
                                ptb[s][0:64, ncol:ncol + 128],
                                start=(s == 0), stop=(s == 2),
                                tile_position=(0, 0))
                    pr.t.matmul(PA[0:65, hh * 128:(hh + 1) * 128],
                                vslots[s][64:128, h * 65:(h + 1) * 65],
                                ptb[s][64:128, ncol:ncol + 128],
                                start=(s == 0), stop=(s == 2),
                                tile_position=(64, 0))
            pr.v.tensor_copy(ctxs[:], PA[0:65, :])
            pr.v.tensor_tensor(ctxe[:], PD[0:65, :], ctxs[:], ALU.add)
            pr.v.reciprocal(rcp[:], ctxe[64:65, :])
            for hq in range(2):
                pr.t.matmul(PA[0:64, hq * 512:(hq + 1) * 512], onesf[0:1, 0:64],
                            rcp[0:1, hq * 512:(hq + 1) * 512])
            for par in range(2):
                pr.v.tensor_tensor(
                    cnT[par * 64:(par + 1) * 64, :]
                    .rearrange("p (m q) -> p m q", q=128)[:, 4 * half:4 * half + 4, :],
                    ctxe[0:64, :].rearrange("p (m q) -> p m q", q=256)
                    [:, :, par * 128:(par + 1) * 128],
                    PA[0:64, :].rearrange("p (m q) -> p m q", q=256)
                    [:, :, par * 128:(par + 1) * 128],
                    ALU.mult)
        for half in range(2):
            for jj in range(NCH):
                pr.t.matmul(PB[:, half * 512:(half + 1) * 512],
                            cnT[:, jj * 128:(jj + 1) * 128],
                            wo_sb[:, jj * D + half * 512: jj * D + (half + 1) * 512],
                            start=(jj == 0), stop=(jj == NCH - 1))
        pr.v.scalar_tensor_tensor(x3sb[pw][:], PB[:], 1.0, obb[:], ALU.mult, ALU.add)
        pr.v.tensor_tensor(x3sb[pw][:], x3sb[pw][:], x2r[w % 3][:], ALU.add)
        pr.y.dma_start(x3_h.ap()[w * TL:(w + 1) * TL, :], x3sb[pw][:])


def emit_phase3c(nc, pr, env, st):
    ident, onesf, jt, epsc = (env[k] for k in ["ident", "onesf", "jt", "epsc"])
    win, x3_h, out_h = env["win"], env["x3_h"], env["out_h"]

    def sbt(name, shape, dtype):
        return st.enter_context(nc.sbuf_tensor(name, shape, dtype))

    def pst(name, shape, dtype=F32):
        return st.enter_context(nc.psum_tensor(name, shape, dtype))

    fr_sb = sbt("c_fr", [128, NCH * D], BF16)
    fk_sb = sbt("c_fk", [128, NCH * D], BF16)
    fv_sb = sbt("c_fv", [128, NCH * D], BF16)
    mur = sbt("c_mur", [128, NCH], F32)
    muk = sbt("c_muk", [128, NCH], F32)
    x_in = [sbt(f"c_x{i}", [128, D], F32) for i in range(2)]
    stat = sbt("c_stat", [128, 12], F32)
    mv = [sbt(f"c_mv{i}", [128, 2], F32) for i in range(2)]
    col = [sbt(f"c_col{i}", [128, 2], F32) for i in range(2)]
    ln_f = [sbt(f"c_ln{i}", [128, D], F32) for i in range(2)]
    lnT3 = [sbt(f"c_lnT{i}", [128, NCH * 130], BF16) for i in range(2)]
    xdf = sbt("c_xdf", [128, NCH * 128], BF16)
    xrT = sbt("c_xrT", [128, NCH * 128], BF16)
    xkT = sbt("c_xkT", [128, NCH * 128], BF16)
    r_sb = sbt("c_r", [128, D], F32)
    kr = sbt("c_kr", [128, D], F32)
    kaT = sbt("c_kaT", [128, NCH * 128], BF16)
    osb = [sbt(f"c_o{i}", [128, D], F32) for i in range(2)]
    psF = pst("c_psF", [128, 1024])
    psK3 = pst("c_psK", [128, 1024])
    psT3 = pst("c_psT", [128, 1024])

    for w_sb, nm in ((fr_sb, "fr"), (fk_sb, "fk"), (fv_sb, "fv")):
        src_ = win[nm].ap().rearrange("(j p) d -> p j d", p=128)
        pr.g.dma_start(w_sb[:].rearrange("p (j d) -> p j d", d=D), src_)
    pr.g.dma_start(mur[:], win["mur_c"].ap())
    pr.g.dma_start(muk[:], win["muk_c"].ap())

    for t in range(NT):
        p = t % 2
        xb = x_in[p]
        pr.y.dma_start(xb[:], x3_h.ap()[t * TL:(t + 1) * TL, :])
        ln_chain(pr, stat, mv[p], col[p], epsc, xb, ln_f[p])
        for j in range(NCH):
            pr.t.transpose(psT3[:, j * 128:(j + 1) * 128],
                           ln_f[p][:, j * 128:(j + 1) * 128], ident[:])
        lt = lnT3[p]
        ltprev = lnT3[1 - p]
        pr.v.tensor_copy(
            lt[:].rearrange("p (j q) -> p j q", q=130)[:, :, 1:129],
            psT3[:].rearrange("p (j q) -> p j q", q=128))
        if t == 0:
            pr.v.memset(lt[:].rearrange("p (j q) -> p j q", q=130)[:, :, 0:1], 0.0)
        else:
            pr.v.tensor_copy(
                lt[:].rearrange("p (j q) -> p j q", q=130)[:, :, 0:1],
                ltprev[:].rearrange("p (j q) -> p j q", q=130)[:, :, 128:129])
        ltv = lt[:].rearrange("p (j q) -> p j q", q=130)
        pr.v.tensor_tensor(xdf[:].rearrange("p (j q) -> p j q", q=128),
                           ltv[:, :, 1:129], ltv[:, :, 0:128], ALU.subtract)
        for j in range(NCH):
            pr.v.scalar_tensor_tensor(
                xrT[:, j * 128:(j + 1) * 128], xdf[:, j * 128:(j + 1) * 128],
                mur[:, j:j + 1], lt[:, j * 130: j * 130 + 128], ALU.mult, ALU.add)
            pr.v.scalar_tensor_tensor(
                xkT[:, j * 128:(j + 1) * 128], xdf[:, j * 128:(j + 1) * 128],
                muk[:, j:j + 1], lt[:, j * 130: j * 130 + 128], ALU.mult, ALU.add)
        for half in range(2):
            for j in range(NCH):
                pr.t.matmul(psF[:, half * 512:(half + 1) * 512],
                            xrT[:, j * 128:(j + 1) * 128],
                            fr_sb[:, j * D + half * 512: j * D + (half + 1) * 512],
                            start=(j == 0), stop=(j == NCH - 1))
            for j in range(NCH):
                pr.t.matmul(psK3[:, half * 512:(half + 1) * 512],
                            xkT[:, j * 128:(j + 1) * 128],
                            fk_sb[:, j * D + half * 512: j * D + (half + 1) * 512],
                            start=(j == 0), stop=(j == NCH - 1))
        pr.s.activation(r_sb[:], psF[:], AF.Sigmoid)
        pr.s.activation(kr[:], psK3[:], AF.Relu)
        for j in range(NCH):
            pr.t.transpose(psT3[:, j * 128:(j + 1) * 128],
                           kr[:, j * 128:(j + 1) * 128], ident[:])
        pr.s.activation(kaT[:], psT3[:], AF.Square)
        for half in range(2):
            for j in range(NCH):
                pr.t.matmul(psF[:, half * 512:(half + 1) * 512],
                            kaT[:, j * 128:(j + 1) * 128],
                            fv_sb[:, j * D + half * 512: j * D + (half + 1) * 512],
                            start=(j == 0), stop=(j == NCH - 1))
        pr.v.tensor_tensor(r_sb[:], r_sb[:], psF[:], ALU.mult)
        pr.v.tensor_tensor(osb[p][:], r_sb[:], xb[:], ALU.add)
        pr.y.dma_start(env["out_h"].ap()[t * TL:(t + 1) * TL, :], osb[p][:])


def emit_phase3a(nc, pr, env, st):
    ident, onesf, jt, epsc = (env[k] for k in ["ident", "onesf", "jt", "epsc"])
    x_h, win, ctx_h, x2_h = env["x_h"], env["win"], env["ctx_h"], env["x2_h"]

    def sbt(name, shape, dtype):
        return st.enter_context(nc.sbuf_tensor(name, shape, dtype))

    def pst(name, shape, dtype=F32):
        return st.enter_context(nc.psum_tensor(name, shape, dtype))

    wo_sb = sbt("p3_wo", [128, NCH * D], BF16)
    fr_sb = sbt("p3_fr", [128, NCH * D], BF16)
    fk_sb = sbt("p3_fk", [128, NCH * D], BF16)
    fv_sb = sbt("p3_fv", [128, NCH * D], BF16)
    mur = sbt("p3_mur", [128, NCH], F32)
    muk = sbt("p3_muk", [128, NCH], F32)
    rows = sbt("p3_rows", [1, 3 * D], F32)
    bob = sbt("p3_bob", [128, D], F32)
    gab = sbt("p3_gab", [128, D], F32)
    beb = sbt("p3_beb", [128, D], F32)
    x_in = [sbt(f"p3_x{i}", [128, D], F32) for i in range(2)]
    ct_in = [sbt(f"p3_ct{i}", [128, NCH * 128], BF16) for i in range(2)]
    stat = sbt("p3_stat", [128, 12], F32)
    mv = [sbt(f"p3_mv{i}", [128, 2], F32) for i in range(2)]
    col = [sbt(f"p3_col{i}", [128, 2], F32) for i in range(2)]
    ln_f = [sbt(f"p3_ln{i}", [128, D], F32) for i in range(2)]
    ysb = sbt("p3_y", [128, D], F32)
    x1sb = [sbt(f"p3_x1_{i}", [128, D], F32) for i in range(2)]
    lnT3 = [sbt(f"p3_lnT{i}", [128, NCH * 130], BF16) for i in range(2)]
    xdf = sbt("p3_xdf", [128, NCH * 128], BF16)
    xrT = sbt("p3_xrT", [128, NCH * 128], BF16)
    xkT = sbt("p3_xkT", [128, NCH * 128], BF16)
    r_sb = sbt("p3_r", [128, D], F32)
    kr = sbt("p3_kr", [128, D], F32)
    kaT = sbt("p3_kaT", [128, NCH * 128], BF16)
    x2sb = [sbt(f"p3_x2_{i}", [128, D], F32) for i in range(2)]
    psO = pst("p3_psO", [128, 1024])
    psF = pst("p3_psF", [128, 1024])
    psK3 = pst("p3_psK", [128, 1024])
    psT3 = pst("p3_psT", [128, 1024])

    for w_sb, nm in ((wo_sb, "wo"), (fr_sb, "fr"), (fk_sb, "fk"), (fv_sb, "fv")):
        src_ = win[nm].ap().rearrange("(j p) d -> p j d", p=128)
        pr.g.dma_start(w_sb[:].rearrange("p (j d) -> p j d", d=D), src_)
    pr.g.dma_start(mur[:], win["mur_c"].ap())
    pr.g.dma_start(muk[:], win["muk_c"].ap())
    pr.g.dma_start(rows[0:1, 0:D], win["bo_r"].ap())
    pr.g.dma_start(rows[0:1, D:2 * D], win["gam_r"].ap())
    pr.g.dma_start(rows[0:1, 2 * D:3 * D], win["bet_r"].ap())
    for bt, off in ((bob, 0), (gab, D), (beb, 2 * D)):
        for half in range(2):
            pr.t.matmul(psO[:, half * 512:(half + 1) * 512], onesf[:],
                        rows[0:1, off + half * 512: off + (half + 1) * 512])
        pr.v.tensor_copy(bt[:], psO[:])

    def ln_chain(xb, p, out_f):
        pr.v.bn_stats(stat[:, 0:6], xb[:, 0:512])
        pr.v.bn_stats(stat[:, 6:12], xb[:, 512:1024])
        pr.v.bn_aggr(mv[p][:], stat[:])
        pr.s.activation(col[p][:, 0:1], mv[p][:, 1:2], AF.Sqrt, bias=epsc[:, 0:1])
        pr.v.reciprocal(col[p][:, 1:2], col[p][:, 0:1])
        pr.v.tensor_scalar(out_f[:], xb[:], mv[p][:, 0:1], col[p][:, 1:2],
                           ALU.subtract, ALU.mult)

    for t in range(NT):
        p = t % 2
        xb = x_in[p]
        ctb = ct_in[p]
        pr.y.dma_start(xb[:], x_h.ap()[t * TL:(t + 1) * TL, :])
        pr.y.dma_start(ctb[:], ctx_h.ap()[t])
        ln_chain(xb, p, ln_f[p])
        for half in range(2):
            for jj in range(NCH):
                pr.t.matmul(psO[:, half * 512:(half + 1) * 512],
                            ctb[:, jj * 128:(jj + 1) * 128],
                            wo_sb[:, jj * D + half * 512: jj * D + (half + 1) * 512],
                            start=(jj == 0), stop=(jj == NCH - 1))
        pr.v.scalar_tensor_tensor(ysb[:], psO[:], 1.0, bob[:], ALU.mult, ALU.add)
        pr.v.tensor_tensor(ysb[:], ysb[:], ln_f[p][:], ALU.add)
        ln_chain(ysb, p, ln_f[p])
        pr.v.tensor_tensor(ln_f[p][:], ln_f[p][:], gab[:], ALU.mult)
        pr.v.tensor_tensor(ln_f[p][:], ln_f[p][:], beb[:], ALU.add)
        pr.v.tensor_tensor(x1sb[p][:], ln_f[p][:], xb[:], ALU.add)
        ln_chain(x1sb[p], p, ln_f[p])
        for j in range(NCH):
            pr.t.transpose(psT3[:, j * 128:(j + 1) * 128],
                           ln_f[p][:, j * 128:(j + 1) * 128], ident[:])
        lt = lnT3[p]
        ltprev = lnT3[1 - p]
        pr.v.tensor_copy(
            lt[:].rearrange("p (j q) -> p j q", q=130)[:, :, 1:129],
            psT3[:].rearrange("p (j q) -> p j q", q=128))
        if t == 0:
            pr.v.memset(lt[:].rearrange("p (j q) -> p j q", q=130)[:, :, 0:1], 0.0)
        else:
            pr.v.tensor_copy(
                lt[:].rearrange("p (j q) -> p j q", q=130)[:, :, 0:1],
                ltprev[:].rearrange("p (j q) -> p j q", q=130)[:, :, 128:129])
        ltv = lt[:].rearrange("p (j q) -> p j q", q=130)
        pr.v.tensor_tensor(xdf[:].rearrange("p (j q) -> p j q", q=128),
                           ltv[:, :, 1:129], ltv[:, :, 0:128], ALU.subtract)
        for j in range(NCH):
            pr.v.scalar_tensor_tensor(
                xrT[:, j * 128:(j + 1) * 128], xdf[:, j * 128:(j + 1) * 128],
                mur[:, j:j + 1], lt[:, j * 130: j * 130 + 128], ALU.mult, ALU.add)
            pr.v.scalar_tensor_tensor(
                xkT[:, j * 128:(j + 1) * 128], xdf[:, j * 128:(j + 1) * 128],
                muk[:, j:j + 1], lt[:, j * 130: j * 130 + 128], ALU.mult, ALU.add)
        for half in range(2):
            for j in range(NCH):
                pr.t.matmul(psF[:, half * 512:(half + 1) * 512],
                            xrT[:, j * 128:(j + 1) * 128],
                            fr_sb[:, j * D + half * 512: j * D + (half + 1) * 512],
                            start=(j == 0), stop=(j == NCH - 1))
            for j in range(NCH):
                pr.t.matmul(psK3[:, half * 512:(half + 1) * 512],
                            xkT[:, j * 128:(j + 1) * 128],
                            fk_sb[:, j * D + half * 512: j * D + (half + 1) * 512],
                            start=(j == 0), stop=(j == NCH - 1))
        pr.s.activation(r_sb[:], psF[:], AF.Sigmoid)
        pr.s.activation(kr[:], psK3[:], AF.Relu)
        for j in range(NCH):
            pr.t.transpose(psT3[:, j * 128:(j + 1) * 128],
                           kr[:, j * 128:(j + 1) * 128], ident[:])
        pr.s.activation(kaT[:], psT3[:], AF.Square)
        for half in range(2):
            for j in range(NCH):
                pr.t.matmul(psF[:, half * 512:(half + 1) * 512],
                            kaT[:, j * 128:(j + 1) * 128],
                            fv_sb[:, j * D + half * 512: j * D + (half + 1) * 512],
                            start=(j == 0), stop=(j == NCH - 1))
        pr.v.tensor_tensor(r_sb[:], r_sb[:], psF[:], ALU.mult)
        pr.v.tensor_tensor(x2sb[p][:], r_sb[:], x1sb[p][:], ALU.add)
        pr.y.dma_start(x2_h.ap()[t * TL:(t + 1) * TL, :], x2sb[p][:])


def emit_phase2(nc, pr, env, st):
    kres, vres, ident, onesf, jt, epsc = (env[k] for k in
        ["kres", "vres", "ident", "onesf", "jt", "epsc"])
    x_h, win, ctx_h = env["x_h"], env["win"], env["ctx_h"]

    def sbt(name, shape, dtype):
        return st.enter_context(nc.sbuf_tensor(name, shape, dtype))

    def pst(name, shape, dtype=F32):
        return st.enter_context(nc.psum_tensor(name, shape, dtype))

    wq_sb = sbt("p2_wq", [128, NCH * D], BF16)
    bq_c = sbt("p2_bqc", [128, NCH], F32)
    x_in = [sbt(f"p2_x{i}", [128, D], F32) for i in range(2)]
    stat = sbt("p2_stat", [128, 12], F32)
    mv = [sbt(f"p2_mv{i}", [128, 2], F32) for i in range(2)]
    col = [sbt(f"p2_col{i}", [128, 2], F32) for i in range(2)]
    ln_f = [sbt(f"p2_ln{i}", [128, D], F32) for i in range(2)]
    lnT = sbt("p2_lnT", [128, NCH * 512], BF16)
    qt = sbt("p2_qt", [128, NCH * 512], BF16)
    pt = [sbt(f"p2_pt{i}", [128, 2048], BF16) for i in range(2)]
    ctxe = [sbt(f"p2_ce{i}", [65, 512], F32) for i in range(2)]
    ctxs = sbt("p2_cs", [65, 512], F32)
    rcp = sbt("p2_rcp", [1, 512], F32)
    cnT = sbt("p2_cnT", [128, NCH * 512], BF16)
    psS = pst("p2_psS", [128, 2048])
    psC = [pst(f"p2_psC{i}", [65, 512]) for i in range(4)]

    G = NT // GT

    src = win["wq"].ap().rearrange("(j p) d -> p j d", p=128)
    pr.g.dma_start(wq_sb[:].rearrange("p (j d) -> p j d", d=D), src)
    pr.g.dma_start(bq_c[:], win["bq_c"].ap())

    for g in range(G):
        for u in range(GT):
            t = g * GT + u
            p = t % 2
            xb = x_in[p]
            pr.y.dma_start(xb[:], x_h.ap()[t * TL:(t + 1) * TL, :])
            pr.v.bn_stats(stat[:, 0:6], xb[:, 0:512])
            pr.v.bn_stats(stat[:, 6:12], xb[:, 512:1024])
            pr.v.bn_aggr(mv[p][:], stat[:])
            pr.s.activation(col[p][:, 0:1], mv[p][:, 1:2], AF.Sqrt, bias=epsc[:, 0:1])
            pr.v.reciprocal(col[p][:, 1:2], col[p][:, 0:1])
            pr.v.tensor_scalar(ln_f[p][:], xb[:], mv[p][:, 0:1], col[p][:, 1:2],
                               ALU.subtract, ALU.mult)
            for hf in range(2):
                for j in range(4):
                    jj = hf * 4 + j
                    pr.t.transpose(psS[:, j * 128:(j + 1) * 128],
                                   ln_f[p][:, jj * 128:(jj + 1) * 128], ident[:])
                pr.v.tensor_copy(
                    lnT[:].rearrange("p (j q) -> p j q", q=512)
                    [:, hf * 4:(hf + 1) * 4, u * 128:(u + 1) * 128],
                    psS[:, 0:512].rearrange("p (j q) -> p j q", q=128))
        for i in range(NCH):
            psQ = psS[:, 512:1024] if i % 2 == 0 else psS[:, 1536:2048]
            for j in range(NCH):
                pr.t.matmul(psQ, wq_sb[:, j * D + i * 128: j * D + (i + 1) * 128],
                            lnT[:, j * 512:(j + 1) * 512],
                            start=(j == 0), stop=(j == NCH - 1))
            pr.s.activation(qt[:, i * 512:(i + 1) * 512], psQ,
                            AF.Identity, bias=bq_c[:, i:i + 1])
        NB = NT // 2

        def s_mms(jj, b):
            for half in range(2):
                j = 2 * b + half
                pr.t.matmul(psS[:, half * 512:(half + 1) * 512],
                            kres[0:64, j * D + jj * 128: j * D + (jj + 1) * 128],
                            qt[0:64, jj * 512:(jj + 1) * 512],
                            tile_position=(0, 0))
                pr.t.matmul(psS[:, 1024 + half * 512:1536 + half * 512],
                            kres[64:128, j * D + jj * 128: j * D + (jj + 1) * 128],
                            qt[64:128, jj * 512:(jj + 1) * 512],
                            tile_position=(64, 0))

        def ctx_mms(jj, b):
            for ho in range(2):
                h = 2 * jj + ho
                for half in range(2):
                    j = 2 * b + half
                    pcol = ho * 1024 + half * 512
                    pr.t.matmul(psC[2 * ho][:],
                                vres[0:64, j * 1040 + h * 65: j * 1040 + (h + 1) * 65],
                                pt[b % 2][0:64, pcol:pcol + 512],
                                start=(j == 0), stop=(j == NT - 1),
                                tile_position=(0, 0))
                    pr.t.matmul(psC[2 * ho + 1][:],
                                vres[64:128, j * 1040 + h * 65: j * 1040 + (h + 1) * 65],
                                pt[b % 2][64:128, pcol:pcol + 512],
                                start=(j == 0), stop=(j == NT - 1),
                                tile_position=(64, 0))

        for jj in range(NCH):
            for b in range(NB):
                s_mms(jj, b)
                pr.s.activation(pt[b % 2][:], psS[:], AF.Exp, scale=0.125)
                if b >= 1:
                    ctx_mms(jj, b - 1)
            ctx_mms(jj, NB - 1)
            for ho in range(2):
                pbase = 64 * ho
                pr.v.tensor_copy(ctxs[:], psC[2 * ho + 1][:])
                pr.v.tensor_tensor(ctxe[ho][:], psC[2 * ho][:], ctxs[:], ALU.add)
                pr.v.reciprocal(rcp[:], ctxe[ho][64:65, :])
                pr.t.matmul(psC[2 * ho][0:64, :], onesf[0:1, 0:64], rcp[:])
                pr.v.tensor_tensor(
                    cnT[pbase:pbase + 64, jj * 512:(jj + 1) * 512],
                    ctxe[ho][0:64, :], psC[2 * ho][0:64, :], ALU.mult)
        for u in range(GT):
            t = g * GT + u
            pr.y.dma_start(
                ctx_h.ap()[t],
                cnT[:].rearrange("p (j q) -> p j q", q=512)[:, :, u * 128:(u + 1) * 128])


def emit_start(nc, pr, env):
    ident, onesf, jt, epsc = env["ident"], env["onesf"], env["jt"], env["epsc"]

    def s_clear(eng):
        for h_ in pr.all_sems():
            eng.sem_clear(h_)
    pr.ops["sync"].append((None, s_clear, "RAW"))

    def g_pre(eng):
        eng.memset(env["preg"][:], 0)
    pr.ops["gpsimd"].append((None, g_pre, "RAW"))

    def v_pre(eng):
        for i in range(16):
            eng.memset(env["prev"][:, i * 4:(i + 1) * 4], 0.0)
    pr.ops["vector"].append((None, v_pre, "RAW"))

    def c_pre(eng):
        for i in range(16):
            eng.memzero(env["pres"][:, i * 4:(i + 1) * 4])
    pr.ops["scalar"].append((None, c_pre, "RAW"))

    pr.g.iota(jt[:, 0:128], [[1, 128]], channel_multiplier=0,
              allow_small_or_imprecise_dtypes=True)
    pr.g.iota(jt[:, 128:256], [[0, 128]], channel_multiplier=1,
              allow_small_or_imprecise_dtypes=True)
    pr.v.tensor_tensor(ident[:], jt[:, 0:128], jt[:, 128:256], ALU.is_equal)
    pr.v.memset(onesf[:], 1.0)
    pr.v.memset(epsc[:], float(EPS))
    pr.v.memset(jt[:], 0.0)


def emit_phase1(nc, pr, env, st):
    kres, vres, ident, onesf, cm, jt, epsc = (env[k] for k in
        ["kres", "vres", "ident", "onesf", "cm", "jt", "epsc"])
    x_h, win, bi_h = env["x_h"], env["win"], env["bi_h"]

    def sbt(name, shape, dtype):
        return st.enter_context(nc.sbuf_tensor(name, shape, dtype))

    def pst(name, shape, dtype=F32):
        return st.enter_context(nc.psum_tensor(name, shape, dtype))

    wk_sb = sbt("p1_wk", [128, NCH * D], BF16)
    wv_sb = sbt("p1_wv", [128, NCH * D], BF16)
    bk_c = sbt("p1_bkc", [128, NCH], F32)
    bvb = sbt("p1_bvb", [128, D], F32)
    x_in = [sbt(f"p1_x{i}", [128, D], F32) for i in range(2)]
    stat = sbt("p1_stat", [128, 12], F32)
    mv = [sbt(f"p1_mv{i}", [128, 2], F32) for i in range(2)]
    col = [sbt(f"p1_col{i}", [128, 2], F32) for i in range(2)]
    ln_f = [sbt(f"p1_ln{i}", [128, D], F32) for i in range(2)]
    lnT = sbt("p1_lnT", [128, NCH * 512], BF16)
    bi_sb = sbt("p1_bi", [1, 2048], F32)
    row1 = sbt("p1_row", [1, D], F32)
    mx8 = sbt("p1_mx8", [1, 8], F32)
    psT = pst("p1_psT", [128, 1024])
    psK = [pst(f"p1_psK{i}", [128, 512]) for i in range(2)]
    psV = [pst(f"p1_psV{i}", [128, 512]) for i in range(2)]
    psB = pst("p1_psB", [128, 1024])

    G = NT // GT

    for w_sb, nm in ((wk_sb, "wk"), (wv_sb, "wv")):
        src = win[nm].ap().rearrange("(j p) d -> p j d", p=128)
        pr.g.dma_start(w_sb[:].rearrange("p (j d) -> p j d", d=D), src)
    pr.g.dma_start(bk_c[:], win["bk_c"].ap())
    pr.g.dma_start(bi_sb[:], bi_h.ap())
    pr.g.dma_start(row1[:], win["bv_r"].ap())

    pr.v.max(mx8[:], bi_sb[:])
    pr.v.tensor_scalar(bi_sb[:], bi_sb[:], mx8[0:1, 2:3], None, ALU.is_lt)
    for t in range(16):
        pr.t.transpose(psB[:, t:t + 1], bi_sb[0:1, t * 128:(t + 1) * 128],
                       ident[0:1, 0:1])
    pr.v.tensor_copy(cm[:], psB[:, 0:16])
    pr.t.matmul(psB[:, 0:512], onesf[:], row1[0:1, 0:512])
    pr.t.matmul(psB[:, 512:1024], onesf[:], row1[0:1, 512:1024])
    pr.v.tensor_copy(bvb[:], psB[:])
    pr.v.memset(vres[:].rearrange("p (b o) -> p b o", o=65)[:, :, 64:65], 1.0)

    for g in range(G):
        for u in range(GT):
            t = g * GT + u
            p = t % 2
            xb = x_in[p]
            pr.y.dma_start(xb[:], x_h.ap()[t * TL:(t + 1) * TL, :])
            pr.v.bn_stats(stat[:, 0:6], xb[:, 0:512])
            pr.v.bn_stats(stat[:, 6:12], xb[:, 512:1024])
            pr.v.bn_aggr(mv[p][:], stat[:])
            pr.s.activation(col[p][:, 0:1], mv[p][:, 1:2], AF.Sqrt, bias=epsc[:, 0:1])
            pr.v.reciprocal(col[p][:, 1:2], col[p][:, 0:1])
            pr.v.tensor_scalar(ln_f[p][:], xb[:], mv[p][:, 0:1], col[p][:, 1:2],
                               ALU.subtract, ALU.mult)
            for j in range(NCH):
                pr.t.transpose(psT[:, j * 128:(j + 1) * 128],
                               ln_f[p][:, j * 128:(j + 1) * 128], ident[:])
            pr.v.tensor_copy(
                lnT[:].rearrange("p (j q) -> p j q", q=512)[:, :, u * 128:(u + 1) * 128],
                psT[:].rearrange("p (j q) -> p j q", q=128))
        for i in range(NCH):
            pk = psK[i % 2]
            for j in range(NCH):
                pr.t.matmul(pk[:], wk_sb[:, j * D + i * 128: j * D + (i + 1) * 128],
                            lnT[:, j * 512:(j + 1) * 512],
                            start=(j == 0), stop=(j == NCH - 1))
            pr.s.activation(
                kres[:].rearrange("p (c j q) -> p c j q", j=NCH, q=128)
                [:, g * GT:(g + 1) * GT, i, :],
                pk[:].rearrange("p (u q) -> p u q", q=128),
                AF.Identity, bias=bk_c[:, i:i + 1])
        for u in range(GT):
            t = g * GT + u
            for half in range(2):
                pv = psV[half]
                for j in range(NCH):
                    pr.t.matmul(
                        pv[:],
                        lnT[:, j * 512 + u * 128: j * 512 + (u + 1) * 128],
                        wv_sb[:, j * D + half * 512: j * D + (half + 1) * 512],
                        start=(j == 0), stop=(j == NCH - 1))
            vblk = vres[:, t * 1040:(t + 1) * 1040]
            for half in range(2):
                pr.v.scalar_tensor_tensor(
                    vblk.rearrange("p (h o) -> p h o", o=65)[:, half * 8:(half + 1) * 8, 0:64],
                    psV[half][:].rearrange("p (h o) -> p h o", o=64),
                    1.0,
                    bvb[:, half * 512:(half + 1) * 512].rearrange("p (h o) -> p h o", o=64),
                    ALU.mult, ALU.add)
            if t < 16:
                pr.v.tensor_scalar(vblk, vblk, cm[:, t:t + 1], None, ALU.mult)



def stage_inputs(inputs):
    x = np.ascontiguousarray(np.asarray(inputs["L0_attn_a_x"], np.float32))
    tp = lambda a: np.ascontiguousarray(np.asarray(a, np.float32).T)
    chunked = lambda a: np.ascontiguousarray(np.asarray(a, np.float32).reshape(NCH, 128).T)
    row = lambda a: np.ascontiguousarray(np.asarray(a, np.float32)[None, :])

    common = {
        "wq_t": tp(inputs["L0_attn_a_Wq"]), "wk_t": tp(inputs["L0_attn_a_Wk"]),
        "wv_t": tp(inputs["L0_attn_a_Wv"]), "wo_t": tp(inputs["L0_attn_a_Wo"]),
        "bqw_t": tp(inputs["L0_attn_b_q_w"]), "bkw_t": tp(inputs["L0_attn_b_k_w"]),
        "bvw_t": tp(inputs["L0_attn_b_v_w"]), "bow_t": tp(inputs["L0_attn_b_o_w"]),
        "fr_t": tp(inputs["L0_ffn_W_r"]), "fk_t": tp(inputs["L0_ffn_W_k"]),
        "fv_t": tp(inputs["L0_ffn_W_v"]),
        "bq_c": chunked(inputs["L0_attn_a_bq"]), "bk_c": chunked(inputs["L0_attn_a_bk"]),
        "qb_c": chunked(inputs["L0_attn_b_q_b"]), "kb_c": chunked(inputs["L0_attn_b_k_b"]),
        "mur_c": chunked(inputs["L0_ffn_mu_r"]), "muk_c": chunked(inputs["L0_ffn_mu_k"]),
        "bv_r": row(inputs["L0_attn_a_bv"]), "bo_r": row(inputs["L0_attn_a_bo"]),
        "gam_r": row(inputs["L0_attn_a_gamma"]), "bet_r": row(inputs["L0_attn_a_beta"]),
        "vb_r": row(inputs["L0_attn_b_v_b"]), "ob_r": row(inputs["L0_attn_b_o_b"]),
        "bi": np.ascontiguousarray(np.asarray(inputs["L0_attn_a_band_importance"], np.float32)),
    }
    in_maps = []
    for c in range(8):
        m = dict(common)
        m["x0"] = np.ascontiguousarray(x[c])
        in_maps.append(m)
    return in_maps


def kernel(**inputs):
    in_maps = stage_inputs(inputs)
    nc = build_program(debug=0)
    res = run_bass_kernel_spmd(nc, in_maps, core_ids=list(range(8)))
    return np.stack([np.asarray(res.results[c]["out"], np.float32) for c in range(8)])


# version 14
# speedup vs baseline: 1.0713x; 1.0713x over previous
import sys
sys.path.insert(0, "/opt/trn_rl_repo")
import numpy as np
import concourse.bass as bass
import concourse.mybir as mybir
from concourse.bass_utils import run_bass_kernel_spmd



ENGS = ["tensor", "vector", "scalar", "gpsimd", "sync"]


def _names(args):
    out = []
    for a in args:
        if a is None or isinstance(a, (int, float)):
            continue
        t = getattr(a, "tensor", None)
        if t is not None:
            nm = getattr(t, "name", None)
            if nm is not None:
                if type(t).__name__ == "DRamTensorHandle":
                    nm = f"{nm}@{getattr(a, 'offset', 0)}"
                out.append(nm)
    return out


class _Facade:
    def __init__(self, prog, eng):
        self._p = prog
        self._e = eng

    def _st(self, reads, writes, inc, fn, dma=False):
        self._p.stage(self._e, fn, _names(reads), _names(writes), inc=inc, dma=dma)

    def dma_start(self, out, in_, **kw):
        self._st([in_], [out], True, lambda e: e.dma_start(out, in_, **kw), dma=True)

    def matmul(self, out, lhsT, rhs, start=True, stop=True, **kw):
        self._st([lhsT, rhs], [out], True,
                 lambda e: e.matmul(out, lhsT, rhs, start=start, stop=stop, **kw))

    def transpose(self, out, in_, ident, **kw):
        self._st([in_, ident], [out], True, lambda e: e.transpose(out, in_, ident, **kw))

    def activation(self, out, in_, func, bias=None, scale=None, accum_out=None):
        kw = {}
        if bias is not None:
            kw["bias"] = bias
        if scale is not None:
            kw["scale"] = scale
        if accum_out is not None:
            kw["accum_out"] = accum_out
        self._st([in_, bias, scale], [out, accum_out], True,
                 lambda e: e.activation(out, in_, func, **kw))

    def copy(self, out, in_):
        self._st([in_], [out], True, lambda e: e.copy(out, in_))

    def memzero(self, ap):
        self._st([], [ap], True, lambda e: e.memzero(ap))

    def tensor_copy(self, out, in_):
        self._st([in_], [out], True, lambda e: e.tensor_copy(out, in_))

    def memset(self, ap, c):
        self._st([], [ap], True, lambda e: e.memset(ap, c))

    def iota(self, ap, pattern, **kw):
        self._st([], [ap], True, lambda e: e.iota(ap, pattern, **kw))

    def tensor_tensor(self, out, in0, in1, op):
        self._st([in0, in1], [out], True, lambda e: e.tensor_tensor(out, in0, in1, op))

    def tensor_scalar(self, out, in0, s1, s2, op0, op1=None, accum_out=None):
        kw = dict(accum_out=accum_out) if accum_out is not None else {}
        if op1 is None:
            fn = lambda e: e.tensor_scalar(out, in0, s1, s2, op0, **kw)
        else:
            fn = lambda e: e.tensor_scalar(out, in0, s1, s2, op0, op1, **kw)
        self._st([in0, s1, s2], [out, accum_out], True, fn)

    def scalar_tensor_tensor(self, out, in0, scalar, in1, op0, op1):
        self._st([in0, scalar, in1], [out], True,
                 lambda e: e.scalar_tensor_tensor(out, in0, scalar, in1, op0, op1))

    def bn_stats(self, out, in_):
        self._st([in_], [out], True, lambda e: e.bn_stats(out, in_))

    def bn_aggr(self, out, in_):
        self._st([in_], [out], True, lambda e: e.bn_aggr(out, in_))

    def reciprocal(self, out, in_):
        self._st([in_], [out], True, lambda e: e.reciprocal(out, in_))

    def max(self, out, in_):
        self._st([in_], [out], True, lambda e: e.max(out, in_))

    def reduce_max(self, out, in_, axis, **kw):
        self._st([in_], [out], True, lambda e: e.reduce_max(out, in_, axis, **kw))

    def reduce_sum(self, out, in_, axis, **kw):
        self._st([in_], [out], True, lambda e: e.reduce_sum(out, in_, axis, **kw))

    def select(self, out, mask, a, b):
        self._st([mask, a, b], [out], True, lambda e: e.select(out, mask, a, b))

    def raw(self, fn, reads=(), writes=(), inc=True):
        self._st(list(reads), list(writes), inc, fn)


class Prog:
    def __init__(self, nc):
        self.nc = nc
        self.cnt_sem = {e: nc.alloc_semaphore(name=f"cnt_{e}") for e in ENGS}
        self.pD = nc.alloc_semaphore(name="pDr")
        self.cnt = {e: 0 for e in ENGS}
        self.NDQ = 8
        self.dma_sem = {e: [nc.alloc_semaphore(name=f"dq_{e}_{k}") for k in range(self.NDQ)]
                        for e in ("sync", "gpsimd", "scalar")}
        self.dma_cnt = {e: [0] * self.NDQ for e in ("sync", "gpsimd", "scalar")}
        self.dma_n = {e: 0 for e in ("sync", "gpsimd", "scalar")}
        self.pe_fifo = False
        self.last_w = {}
        self.readers = {}
        self.cur = {e: {} for e in ENGS}
        self.pD_n = 0
        self.ops = {e: [] for e in ENGS}
        for e in ENGS:
            setattr(self, {"tensor": "t", "vector": "v", "scalar": "s",
                           "gpsimd": "g", "sync": "y"}[e], _Facade(self, e))

    def all_sems(self):
        out = list(self.cnt_sem.values()) + [self.pD]
        for sems in self.dma_sem.values():
            out.extend(sems)
        return out

    def _sem_of(self, key):
        kind, ident = key
        if kind == "e":
            return self.cnt_sem[ident]
        e, k = ident
        return self.dma_sem[e][k]

    def _need(self, eng, waits, key, idx):
        kind, ident = key
        if kind == "e" and ident == eng:
            if idx > self.cnt[eng]:
                return
            if eng == "tensor" and self.pe_fifo:
                return
        cur = self.cur[eng].get(key, 0)
        if idx <= cur:
            return
        waits.append((key, idx))
        self.cur[eng][key] = idx

    def stage(self, eng, fn, rn, wn, inc=True, dma=False):
        waits = []
        for nm in rn:
            w = self.last_w.get(nm)
            if w is not None:
                self._need(eng, waits, (w[0], w[1]), w[2])
        for nm in wn:
            w = self.last_w.get(nm)
            if w is not None:
                self._need(eng, waits, (w[0], w[1]), w[2])
            for key, idx in self.readers.get(nm, {}).items():
                self._need(eng, waits, key, idx)
        if dma:
            k = self.dma_n[eng] % self.NDQ
            self.dma_n[eng] += 1
            if self.dma_cnt[eng][k]:
                self._need(eng, waits, ("dm", (eng, k)), self.dma_cnt[eng][k])
            self.dma_cnt[eng][k] += 16
            marker = ("dm", (eng, k), self.dma_cnt[eng][k])
            inc_spec = ("dm", (eng, k), 16)
        elif inc:
            self.cnt[eng] += 1
            marker = ("e", eng, self.cnt[eng])
            inc_spec = ("e", eng, 1)
        else:
            marker = ("e", eng, self.cnt[eng] + 1)
            inc_spec = None
        for nm in wn:
            self.last_w[nm] = marker
            self.readers[nm] = {}
        for nm in rn:
            self.readers.setdefault(nm, {})[(marker[0], marker[1])] = marker[2]
        self.ops[eng].append((waits, fn, inc_spec))

    def raw_stage(self, eng, fn):
        self.ops[eng].append(([], fn, None))

    def rendezvous(self):
        self.pD_n += len(ENGS)
        target = self.pD_n
        for e in ENGS:
            cnt = self.cnt[e]
            dts = []
            if e in self.dma_cnt:
                dts = [(self.dma_sem[e][k], c) for k, c in enumerate(self.dma_cnt[e]) if c]

            def f(eng, e=e, cnt=cnt, dts=dts, target=target):
                if cnt:
                    eng.wait_ge(self.cnt_sem[e], cnt)
                for s, c in dts:
                    eng.wait_ge(s, c)
                eng.sem_inc(self.pD, 1)
                eng.wait_ge(self.pD, target)
            self.ops[e].append((None, f, "RAW"))
        self.last_w.clear()
        self.readers.clear()
        for e in ENGS:
            self.cur[e] = {}

    def flush(self, block):
        for e in ENGS:
            ops = self.ops[e]
            if not ops:
                continue

            def section(eng, ops=ops, e=e):
                for waits, fn, inc_spec in ops:
                    if waits is None:
                        fn(eng)
                        continue
                    for key, val in (waits or ()):
                        eng.wait_ge(self._sem_of(key), val)
                    instr = fn(eng)
                    if inc_spec == "RAW":
                        continue
                    if inc_spec is not None:
                        k, ident, amt = inc_spec
                        sem = self._sem_of((k, ident))
                        instr.then_inc(sem, amt)
            getattr(block, e)(section)
        self.ops = {e: [] for e in ENGS}




F32 = mybir.dt.float32
BF16 = mybir.dt.bfloat16
AF = mybir.ActivationFunctionType
ALU = mybir.AluOpType

NT = 32
TL = 128
D = 1024
NCH = 8
H = 16
HD = 64
GT = 4
EPS = 1e-5

WEIGHT_NAMES = ["wq", "wk", "wv", "wo", "bqw", "bkw", "bvw", "bow", "fr", "fk", "fv"]


def build_program(debug=0):
    nc = bass.Bass("TRN2", target_bir_lowering=False)

    x_h = nc.dram_tensor("x0", [NT * TL, D], F32, kind="ExternalInput")
    win = {}
    for w in WEIGHT_NAMES:
        win[w] = nc.dram_tensor(w + "_t", [D, D], F32, kind="ExternalInput")
    for v in ["bq_c", "bk_c", "qb_c", "kb_c", "mur_c", "muk_c"]:
        win[v] = nc.dram_tensor(v, [128, NCH], F32, kind="ExternalInput")
    for v in ["bv_r", "bo_r", "gam_r", "bet_r", "vb_r", "ob_r"]:
        win[v] = nc.dram_tensor(v, [1, D], F32, kind="ExternalInput")
    bi_h = nc.dram_tensor("bi", [1, 2048], F32, kind="ExternalInput")

    out_h = nc.dram_tensor("out", [NT * TL, D], F32, kind="ExternalOutput")
    rcph = [nc.dram_tensor(f"rcph{i}", [1, 1024], F32, kind="Internal")
            for i in range(2)]
    dbg = "ExternalOutput" if debug else "Internal"
    ctx_h = nc.dram_tensor("ctxh", [NT, 128, NCH * 128], BF16, kind=dbg)
    x2_h = nc.dram_tensor("x2h", [NT * TL, D], F32, kind=dbg)
    x3_h = nc.dram_tensor("x3h", [NT * TL, D], F32, kind=dbg)
    kdbg_h = vdbg_h = None
    if debug:
        kdbg_h = nc.dram_tensor("kdbg", [NT, 128, D], BF16, kind="ExternalOutput")
        vdbg_h = nc.dram_tensor("vdbg", [NT, 128, 1040], BF16, kind="ExternalOutput")

    pr = Prog(nc)
    from contextlib import ExitStack
    with ExitStack() as top:
        def sb(name, shape, dtype):
            return top.enter_context(nc.sbuf_tensor(name, shape, dtype))

        ident = sb("ident", [128, 128], F32)
        onesf = sb("onesf", [1, 128], F32)
        cm = sb("cm", [128, 16], F32)
        epsc = sb("epsc", [128, 1], F32)
        jt = sb("jt", [128, 512], F32)
        preg = sb("preg", [128, 64], F32)
        prev = sb("prev", [128, 64], F32)
        pres = sb("pres", [128, 64], F32)
        env0 = dict(preg=preg, prev=prev, pres=pres)

        env = dict(x_h=x_h, win=win, bi_h=bi_h, out_h=out_h, ctx_h=ctx_h,
                   rcph=rcph,
                   x2_h=x2_h, x3_h=x3_h, kdbg_h=kdbg_h, vdbg_h=vdbg_h,
                   ident=ident, onesf=onesf, cm=cm, epsc=epsc, jt=jt, **env0)

        with nc.Block() as block:
            emit_start(nc, pr, env)

            with ExitStack() as p12:
                kres = p12.enter_context(nc.sbuf_tensor("kres", [128, NT * D], BF16))
                vres = p12.enter_context(nc.sbuf_tensor("vres", [128, NT * 1040], BF16))
                env["kres"], env["vres"] = kres, vres

                with ExitStack() as p1s:
                    pr.pe_fifo = True
                    emit_phase1(nc, pr, env, p1s)
                    pr.pe_fifo = False
                pr.rendezvous()

                if debug == 2:
                    for t in range(NT):
                        pr.y.dma_start(kdbg_h.ap()[t], kres[:, t * D:(t + 1) * D])
                        pr.y.dma_start(vdbg_h.ap()[t], vres[:, t * 1040:(t + 1) * 1040])
                    pr.rendezvous()
                    pr.flush(block)
                    return nc
                with ExitStack() as p2s:
                    pr.pe_fifo = True
                    emit_phase2(nc, pr, env, p2s)
                    pr.pe_fifo = False
                pr.rendezvous()
                if debug == 3:
                    pr.flush(block)
                    return nc
            with ExitStack() as p3s:
                pr.pe_fifo = True
                emit_phase3a(nc, pr, env, p3s)
                pr.pe_fifo = False
            pr.rendezvous()
            if debug == 4:
                pr.flush(block)
                return nc
            with ExitStack() as p3bs:
                pr.pe_fifo = True
                emit_phase3b(nc, pr, env, p3bs)
                pr.pe_fifo = False
            pr.rendezvous()
            if debug == 5:
                pr.flush(block)
                return nc
            with ExitStack() as p3cs:
                pr.pe_fifo = True
                emit_phase3c(nc, pr, env, p3cs)
                pr.pe_fifo = False
            pr.rendezvous()
            pr.flush(block)
    return nc


def ln_chain(pr, stat, mv, col, epsc, xb, out_f):
    pr.v.bn_stats(stat[:, 0:6], xb[:, 0:512])
    pr.v.bn_stats(stat[:, 6:12], xb[:, 512:1024])
    pr.v.bn_aggr(mv[:], stat[:])
    pr.s.activation(col[:, 0:1], mv[:, 1:2], AF.Sqrt, bias=epsc[:, 0:1])
    pr.v.reciprocal(col[:, 1:2], col[:, 0:1])
    pr.v.tensor_scalar(out_f[:], xb[:], mv[:, 0:1], col[:, 1:2],
                       ALU.subtract, ALU.mult)


def emit_phase3b(nc, pr, env, st):
    ident, onesf, jt, epsc = (env[k] for k in ["ident", "onesf", "jt", "epsc"])
    win, x2_h, x3_h = env["win"], env["x2_h"], env["x3_h"]
    rcph = env["rcph"]

    def sbt(name, shape, dtype):
        return st.enter_context(nc.sbuf_tensor(name, shape, dtype))

    def pst(name, shape, dtype=F32):
        return st.enter_context(nc.psum_tensor(name, shape, dtype))

    wq_sb = sbt("b_wq", [128, NCH * D], BF16)
    wk_sb = sbt("b_wk", [128, NCH * D], BF16)
    wv_sb = sbt("b_wv", [128, NCH * D], BF16)
    wo_sb = sbt("b_wo", [128, NCH * D], BF16)
    qb_c = sbt("b_qbc", [128, NCH], F32)
    kb_c = sbt("b_kbc", [128, NCH], F32)
    rows = sbt("b_rows", [1, 2 * D], F32)
    vbb = sbt("b_vbb", [128, D], F32)
    obb = sbt("b_obb", [128, D], F32)
    x2r = [sbt(f"b_x2_{i}", [128, D], F32) for i in range(3)]
    stat = sbt("b_stat", [128, 12], F32)
    mv = [sbt(f"b_mv{i}", [128, 2], F32) for i in range(2)]
    col = [sbt(f"b_col{i}", [128, 2], F32) for i in range(2)]
    ln_f = [sbt(f"b_ln{i}", [128, D], F32) for i in range(2)]
    lnT = [sbt(f"b_lnT{i}", [128, NCH * 128], BF16) for i in range(2)]
    qbT = [sbt(f"b_qbT{i}", [128, NCH * 128], BF16) for i in range(2)]
    kbT = [sbt(f"b_kbT{i}", [128, NCH * 128], BF16) for i in range(3)]
    kbZ = sbt("b_kbZ", [128, NCH * 128], BF16)
    vbr = [sbt(f"b_vb{i}", [128, H * 65], BF16) for i in range(3)]
    vbZ = sbt("b_vbZ", [128, H * 65], BF16)
    masks = [sbt(f"b_msk{i}", [128, NCH * 128], BF16) for i in range(3)]
    ptb = [sbt(f"b_pt{i}", [128, NCH * 128], BF16) for i in range(3)]
    ctxe = sbt("b_ce", [65, NCH * 128], F32)
    ctxs = sbt("b_cs", [65, NCH * 128], F32)
    rcpb3 = sbt("b_rb", [64, NCH * 128], F32)
    rcp = sbt("b_rcp", [1, NCH * 128], F32)
    cnT = sbt("b_cnT", [128, NCH * 128], BF16)
    x3sb = [sbt(f"b_x3_{i}", [128, D], F32) for i in range(2)]
    mt1 = sbt("b_mt1", [128, 128], F32)
    mt2 = sbt("b_mt2", [128, 128], F32)
    mt3 = sbt("b_mt3", [128, 128], F32)
    PA = pst("b_PA", [128, 1024])
    PB = pst("b_PB", [128, 1024])
    PC = pst("b_PC", [128, 1024])
    PD = pst("b_PD", [128, 1024])

    for w_sb, nm in ((wq_sb, "bqw"), (wk_sb, "bkw"), (wv_sb, "bvw"), (wo_sb, "bow")):
        src_ = win[nm].ap().rearrange("(j p) d -> p j d", p=128)
        pr.g.dma_start(w_sb[:].rearrange("p (j d) -> p j d", d=D), src_)
    pr.g.dma_start(qb_c[:], win["qb_c"].ap())
    pr.g.dma_start(kb_c[:], win["kb_c"].ap())
    pr.g.dma_start(rows[0:1, 0:D], win["vb_r"].ap())
    pr.g.dma_start(rows[0:1, D:2 * D], win["ob_r"].ap())
    for bt, off in ((vbb, 0), (obb, D)):
        for half in range(2):
            pr.t.matmul(PA[:, half * 512:(half + 1) * 512], onesf[:],
                        rows[0:1, off + half * 512: off + (half + 1) * 512])
        pr.v.tensor_copy(bt[:], PA[:])
    pr.g.iota(mt1[:], [[1, 128]], channel_multiplier=0,
              allow_small_or_imprecise_dtypes=True)
    pr.g.iota(mt2[:], [[0, 128]], channel_multiplier=1,
              allow_small_or_imprecise_dtypes=True)
    pr.v.tensor_tensor(mt2[:], mt2[:], mt1[:], ALU.subtract)
    for s in range(3):
        pr.v.tensor_scalar(mt1[:], mt2[:], float((s - 1) * 128), None, ALU.add)
        pr.v.tensor_scalar(mt3[:], mt1[:], 32.0, None, ALU.is_le)
        pr.v.tensor_scalar(mt1[:], mt1[:], -32.0, None, ALU.is_ge)
        pr.v.tensor_tensor(mt1[:], mt1[:], mt3[:], ALU.mult)
        for jj in range(NCH):
            pr.v.tensor_copy(masks[s][:, jj * 128:(jj + 1) * 128], mt1[:])
    pr.v.memset(kbZ[:], 0.0)
    pr.v.memset(vbZ[:], 0.0)
    pr.v.memset(vbZ[:].rearrange("p (h o) -> p h o", o=65)[:, :, 64:65], 1.0)
    for i in range(3):
        pr.v.memset(vbr[i][:].rearrange("p (h o) -> p h o", o=65)[:, :, 64:65], 1.0)

    for u in range(NT + 1):
        if u < NT:
            p = u % 2
            xb = x2r[u % 3]
            pr.y.dma_start(xb[:], x2_h.ap()[u * TL:(u + 1) * TL, :])
            ln_chain(pr, stat, mv[p], col[p], epsc, xb, ln_f[p])
            for j in range(NCH):
                pr.t.transpose(PD[:, j * 128:(j + 1) * 128],
                               ln_f[p][:, j * 128:(j + 1) * 128], ident[:])
            pr.v.tensor_copy(lnT[p][:], PD[:])
            for wsb, bias, dst in ((wq_sb, qb_c, qbT[p]), (wk_sb, kb_c, kbT[u % 3])):
                for i in range(NCH):
                    for j in range(NCH):
                        pr.t.matmul(PA[:, i * 128:(i + 1) * 128],
                                    wsb[:, j * D + i * 128: j * D + (i + 1) * 128],
                                    lnT[p][:, j * 128:(j + 1) * 128],
                                    start=(j == 0), stop=(j == NCH - 1))
                for i in range(NCH):
                    pr.s.activation(dst[:, i * 128:(i + 1) * 128],
                                    PA[:, i * 128:(i + 1) * 128],
                                    AF.Identity, bias=bias[:, i:i + 1])
            for half in range(2):
                for j in range(NCH):
                    pr.t.matmul(PB[:, half * 512:(half + 1) * 512],
                                lnT[p][:, j * 128:(j + 1) * 128],
                                wv_sb[:, j * D + half * 512: j * D + (half + 1) * 512],
                                start=(j == 0), stop=(j == NCH - 1))
            for half in range(2):
                pr.v.scalar_tensor_tensor(
                    vbr[u % 3][:].rearrange("p (h o) -> p h o", o=65)
                    [:, half * 8:(half + 1) * 8, 0:64],
                    PB[:, half * 512:(half + 1) * 512]
                    .rearrange("p (h o) -> p h o", o=64),
                    1.0,
                    vbb[:, half * 512:(half + 1) * 512]
                    .rearrange("p (h o) -> p h o", o=64),
                    ALU.mult, ALU.add)
        pr.pe_fifo = False
        if u == 0:
            continue
        w = u - 1
        pw = w % 2
        kslots = [kbZ if w == 0 else kbT[(w - 1) % 3], kbT[w % 3],
                  kbZ if w == NT - 1 else kbT[(w + 1) % 3]]
        vslots = [vbZ if w == 0 else vbr[(w - 1) % 3], vbr[w % 3],
                  vbZ if w == NT - 1 else vbr[(w + 1) % 3]]
        PS = [PA, PB, PC]
        for half in range(2):
            for s in range(3):
                for hp in range(4):
                    jj = 4 * half + hp
                    pr.t.matmul(PS[s][:, hp * 128:(hp + 1) * 128],
                                kslots[s][0:64, jj * 128:(jj + 1) * 128],
                                qbT[pw][0:64, jj * 128:(jj + 1) * 128],
                                tile_position=(0, 0))
                    pr.t.matmul(PS[s][:, 512 + hp * 128:640 + hp * 128],
                                kslots[s][64:128, jj * 128:(jj + 1) * 128],
                                qbT[pw][64:128, jj * 128:(jj + 1) * 128],
                                tile_position=(64, 0))
            for s in range(3):
                pr.s.activation(ptb[s][:], PS[s][:], AF.Exp, scale=0.125)
                pr.v.tensor_tensor(ptb[s][:], ptb[s][:], masks[s][:], ALU.mult)
            for hh in range(8):
                h = half * 8 + hh
                ncol = (hh % 2) * 512 + (hh // 2) * 128
                for s in range(3):
                    pr.t.matmul(PD[0:65, hh * 128:(hh + 1) * 128],
                                vslots[s][0:64, h * 65:(h + 1) * 65],
                                ptb[s][0:64, ncol:ncol + 128],
                                start=(s == 0), stop=(s == 2),
                                tile_position=(0, 0))
                    pr.t.matmul(PA[0:65, hh * 128:(hh + 1) * 128],
                                vslots[s][64:128, h * 65:(h + 1) * 65],
                                ptb[s][64:128, ncol:ncol + 128],
                                start=(s == 0), stop=(s == 2),
                                tile_position=(64, 0))
            pr.v.tensor_copy(ctxs[:], PA[0:65, :])
            pr.v.tensor_tensor(ctxe[:], PD[0:65, :], ctxs[:], ALU.add)
            pr.v.reciprocal(rcp[:], ctxe[64:65, :])
            hap = rcph[w % 2].ap()
            pr.g.dma_start(hap, rcp[:])
            bsrc = bass.broadcast_tensor_aps(hap, rcpb3[:])[0]
            pr.g.dma_start(rcpb3[:], bsrc)
            for par in range(2):
                pr.v.tensor_tensor(
                    cnT[par * 64:(par + 1) * 64, :]
                    .rearrange("p (m q) -> p m q", q=128)[:, 4 * half:4 * half + 4, :],
                    ctxe[0:64, :].rearrange("p (m q) -> p m q", q=256)
                    [:, :, par * 128:(par + 1) * 128],
                    rcpb3[:].rearrange("p (m q) -> p m q", q=256)
                    [:, :, par * 128:(par + 1) * 128],
                    ALU.mult)
        for half in range(2):
            for jj in range(NCH):
                pr.t.matmul(PB[:, half * 512:(half + 1) * 512],
                            cnT[:, jj * 128:(jj + 1) * 128],
                            wo_sb[:, jj * D + half * 512: jj * D + (half + 1) * 512],
                            start=(jj == 0), stop=(jj == NCH - 1))
        pr.v.scalar_tensor_tensor(x3sb[pw][:], PB[:], 1.0, obb[:], ALU.mult, ALU.add)
        pr.v.tensor_tensor(x3sb[pw][:], x3sb[pw][:], x2r[w % 3][:], ALU.add)
        pr.y.dma_start(x3_h.ap()[w * TL:(w + 1) * TL, :], x3sb[pw][:])


def emit_phase3c(nc, pr, env, st):
    ident, onesf, jt, epsc = (env[k] for k in ["ident", "onesf", "jt", "epsc"])
    win, x3_h, out_h = env["win"], env["x3_h"], env["out_h"]

    def sbt(name, shape, dtype):
        return st.enter_context(nc.sbuf_tensor(name, shape, dtype))

    def pst(name, shape, dtype=F32):
        return st.enter_context(nc.psum_tensor(name, shape, dtype))

    fr_sb = sbt("c_fr", [128, NCH * D], BF16)
    fk_sb = sbt("c_fk", [128, NCH * D], BF16)
    fv_sb = sbt("c_fv", [128, NCH * D], BF16)
    mur = sbt("c_mur", [128, NCH], F32)
    muk = sbt("c_muk", [128, NCH], F32)
    x_in = [sbt(f"c_x{i}", [128, D], F32) for i in range(2)]
    stat = sbt("c_stat", [128, 12], F32)
    mv = [sbt(f"c_mv{i}", [128, 2], F32) for i in range(2)]
    col = [sbt(f"c_col{i}", [128, 2], F32) for i in range(2)]
    ln_f = [sbt(f"c_ln{i}", [128, D], F32) for i in range(2)]
    lnT3 = [sbt(f"c_lnT{i}", [128, NCH * 130], BF16) for i in range(2)]
    xdf = sbt("c_xdf", [128, NCH * 128], BF16)
    xrT = sbt("c_xrT", [128, NCH * 128], BF16)
    xkT = sbt("c_xkT", [128, NCH * 128], BF16)
    r_sb = sbt("c_r", [128, D], F32)
    kr = sbt("c_kr", [128, D], F32)
    kaT = sbt("c_kaT", [128, NCH * 128], BF16)
    osb = [sbt(f"c_o{i}", [128, D], F32) for i in range(2)]
    psF = pst("c_psF", [128, 1024])
    psK3 = pst("c_psK", [128, 1024])
    psT3 = pst("c_psT", [128, 1024])

    for w_sb, nm in ((fr_sb, "fr"), (fk_sb, "fk"), (fv_sb, "fv")):
        src_ = win[nm].ap().rearrange("(j p) d -> p j d", p=128)
        pr.g.dma_start(w_sb[:].rearrange("p (j d) -> p j d", d=D), src_)
    pr.g.dma_start(mur[:], win["mur_c"].ap())
    pr.g.dma_start(muk[:], win["muk_c"].ap())

    for t in range(NT):
        p = t % 2
        xb = x_in[p]
        pr.y.dma_start(xb[:], x3_h.ap()[t * TL:(t + 1) * TL, :])
        ln_chain(pr, stat, mv[p], col[p], epsc, xb, ln_f[p])
        for j in range(NCH):
            pr.t.transpose(psT3[:, j * 128:(j + 1) * 128],
                           ln_f[p][:, j * 128:(j + 1) * 128], ident[:])
        lt = lnT3[p]
        ltprev = lnT3[1 - p]
        pr.v.tensor_copy(
            lt[:].rearrange("p (j q) -> p j q", q=130)[:, :, 1:129],
            psT3[:].rearrange("p (j q) -> p j q", q=128))
        if t == 0:
            pr.v.memset(lt[:].rearrange("p (j q) -> p j q", q=130)[:, :, 0:1], 0.0)
        else:
            pr.v.tensor_copy(
                lt[:].rearrange("p (j q) -> p j q", q=130)[:, :, 0:1],
                ltprev[:].rearrange("p (j q) -> p j q", q=130)[:, :, 128:129])
        ltv = lt[:].rearrange("p (j q) -> p j q", q=130)
        pr.v.tensor_tensor(xdf[:].rearrange("p (j q) -> p j q", q=128),
                           ltv[:, :, 1:129], ltv[:, :, 0:128], ALU.subtract)
        for j in range(NCH):
            pr.v.scalar_tensor_tensor(
                xrT[:, j * 128:(j + 1) * 128], xdf[:, j * 128:(j + 1) * 128],
                mur[:, j:j + 1], lt[:, j * 130: j * 130 + 128], ALU.mult, ALU.add)
            pr.v.scalar_tensor_tensor(
                xkT[:, j * 128:(j + 1) * 128], xdf[:, j * 128:(j + 1) * 128],
                muk[:, j:j + 1], lt[:, j * 130: j * 130 + 128], ALU.mult, ALU.add)
        for half in range(2):
            for j in range(NCH):
                pr.t.matmul(psF[:, half * 512:(half + 1) * 512],
                            xrT[:, j * 128:(j + 1) * 128],
                            fr_sb[:, j * D + half * 512: j * D + (half + 1) * 512],
                            start=(j == 0), stop=(j == NCH - 1))
            for j in range(NCH):
                pr.t.matmul(psK3[:, half * 512:(half + 1) * 512],
                            xkT[:, j * 128:(j + 1) * 128],
                            fk_sb[:, j * D + half * 512: j * D + (half + 1) * 512],
                            start=(j == 0), stop=(j == NCH - 1))
        pr.s.activation(r_sb[:], psF[:], AF.Sigmoid)
        pr.s.activation(kr[:], psK3[:], AF.Relu)
        for j in range(NCH):
            pr.t.transpose(psT3[:, j * 128:(j + 1) * 128],
                           kr[:, j * 128:(j + 1) * 128], ident[:])
        pr.s.activation(kaT[:], psT3[:], AF.Square)
        for half in range(2):
            for j in range(NCH):
                pr.t.matmul(psF[:, half * 512:(half + 1) * 512],
                            kaT[:, j * 128:(j + 1) * 128],
                            fv_sb[:, j * D + half * 512: j * D + (half + 1) * 512],
                            start=(j == 0), stop=(j == NCH - 1))
        pr.v.tensor_tensor(r_sb[:], r_sb[:], psF[:], ALU.mult)
        pr.v.tensor_tensor(osb[p][:], r_sb[:], xb[:], ALU.add)
        pr.y.dma_start(env["out_h"].ap()[t * TL:(t + 1) * TL, :], osb[p][:])


def emit_phase3a(nc, pr, env, st):
    ident, onesf, jt, epsc = (env[k] for k in ["ident", "onesf", "jt", "epsc"])
    x_h, win, ctx_h, x2_h = env["x_h"], env["win"], env["ctx_h"], env["x2_h"]

    def sbt(name, shape, dtype):
        return st.enter_context(nc.sbuf_tensor(name, shape, dtype))

    def pst(name, shape, dtype=F32):
        return st.enter_context(nc.psum_tensor(name, shape, dtype))

    wo_sb = sbt("p3_wo", [128, NCH * D], BF16)
    fr_sb = sbt("p3_fr", [128, NCH * D], BF16)
    fk_sb = sbt("p3_fk", [128, NCH * D], BF16)
    fv_sb = sbt("p3_fv", [128, NCH * D], BF16)
    mur = sbt("p3_mur", [128, NCH], F32)
    muk = sbt("p3_muk", [128, NCH], F32)
    rows = sbt("p3_rows", [1, 3 * D], F32)
    bob = sbt("p3_bob", [128, D], F32)
    gab = sbt("p3_gab", [128, D], F32)
    beb = sbt("p3_beb", [128, D], F32)
    x_in = [sbt(f"p3_x{i}", [128, D], F32) for i in range(2)]
    ct_in = [sbt(f"p3_ct{i}", [128, NCH * 128], BF16) for i in range(2)]
    stat = sbt("p3_stat", [128, 12], F32)
    mv = [sbt(f"p3_mv{i}", [128, 2], F32) for i in range(2)]
    col = [sbt(f"p3_col{i}", [128, 2], F32) for i in range(2)]
    ln_f = [sbt(f"p3_ln{i}", [128, D], F32) for i in range(2)]
    ysb = sbt("p3_y", [128, D], F32)
    x1sb = [sbt(f"p3_x1_{i}", [128, D], F32) for i in range(2)]
    lnT3 = [sbt(f"p3_lnT{i}", [128, NCH * 130], BF16) for i in range(2)]
    xdf = sbt("p3_xdf", [128, NCH * 128], BF16)
    xrT = sbt("p3_xrT", [128, NCH * 128], BF16)
    xkT = sbt("p3_xkT", [128, NCH * 128], BF16)
    r_sb = sbt("p3_r", [128, D], F32)
    kr = sbt("p3_kr", [128, D], F32)
    kaT = sbt("p3_kaT", [128, NCH * 128], BF16)
    x2sb = [sbt(f"p3_x2_{i}", [128, D], F32) for i in range(2)]
    psO = pst("p3_psO", [128, 1024])
    psF = pst("p3_psF", [128, 1024])
    psK3 = pst("p3_psK", [128, 1024])
    psT3 = pst("p3_psT", [128, 1024])

    for w_sb, nm in ((wo_sb, "wo"), (fr_sb, "fr"), (fk_sb, "fk"), (fv_sb, "fv")):
        src_ = win[nm].ap().rearrange("(j p) d -> p j d", p=128)
        pr.g.dma_start(w_sb[:].rearrange("p (j d) -> p j d", d=D), src_)
    pr.g.dma_start(mur[:], win["mur_c"].ap())
    pr.g.dma_start(muk[:], win["muk_c"].ap())
    pr.g.dma_start(rows[0:1, 0:D], win["bo_r"].ap())
    pr.g.dma_start(rows[0:1, D:2 * D], win["gam_r"].ap())
    pr.g.dma_start(rows[0:1, 2 * D:3 * D], win["bet_r"].ap())
    for bt, off in ((bob, 0), (gab, D), (beb, 2 * D)):
        for half in range(2):
            pr.t.matmul(psO[:, half * 512:(half + 1) * 512], onesf[:],
                        rows[0:1, off + half * 512: off + (half + 1) * 512])
        pr.v.tensor_copy(bt[:], psO[:])

    def ln_chain(xb, p, out_f):
        pr.v.bn_stats(stat[:, 0:6], xb[:, 0:512])
        pr.v.bn_stats(stat[:, 6:12], xb[:, 512:1024])
        pr.v.bn_aggr(mv[p][:], stat[:])
        pr.s.activation(col[p][:, 0:1], mv[p][:, 1:2], AF.Sqrt, bias=epsc[:, 0:1])
        pr.v.reciprocal(col[p][:, 1:2], col[p][:, 0:1])
        pr.v.tensor_scalar(out_f[:], xb[:], mv[p][:, 0:1], col[p][:, 1:2],
                           ALU.subtract, ALU.mult)

    for t in range(NT):
        p = t % 2
        xb = x_in[p]
        ctb = ct_in[p]
        pr.y.dma_start(xb[:], x_h.ap()[t * TL:(t + 1) * TL, :])
        pr.y.dma_start(ctb[:], ctx_h.ap()[t])
        ln_chain(xb, p, ln_f[p])
        for half in range(2):
            for jj in range(NCH):
                pr.t.matmul(psO[:, half * 512:(half + 1) * 512],
                            ctb[:, jj * 128:(jj + 1) * 128],
                            wo_sb[:, jj * D + half * 512: jj * D + (half + 1) * 512],
                            start=(jj == 0), stop=(jj == NCH - 1))
        pr.v.scalar_tensor_tensor(ysb[:], psO[:], 1.0, bob[:], ALU.mult, ALU.add)
        pr.v.tensor_tensor(ysb[:], ysb[:], ln_f[p][:], ALU.add)
        ln_chain(ysb, p, ln_f[p])
        pr.v.tensor_tensor(ln_f[p][:], ln_f[p][:], gab[:], ALU.mult)
        pr.v.tensor_tensor(ln_f[p][:], ln_f[p][:], beb[:], ALU.add)
        pr.v.tensor_tensor(x1sb[p][:], ln_f[p][:], xb[:], ALU.add)
        ln_chain(x1sb[p], p, ln_f[p])
        for j in range(NCH):
            pr.t.transpose(psT3[:, j * 128:(j + 1) * 128],
                           ln_f[p][:, j * 128:(j + 1) * 128], ident[:])
        lt = lnT3[p]
        ltprev = lnT3[1 - p]
        pr.v.tensor_copy(
            lt[:].rearrange("p (j q) -> p j q", q=130)[:, :, 1:129],
            psT3[:].rearrange("p (j q) -> p j q", q=128))
        if t == 0:
            pr.v.memset(lt[:].rearrange("p (j q) -> p j q", q=130)[:, :, 0:1], 0.0)
        else:
            pr.v.tensor_copy(
                lt[:].rearrange("p (j q) -> p j q", q=130)[:, :, 0:1],
                ltprev[:].rearrange("p (j q) -> p j q", q=130)[:, :, 128:129])
        ltv = lt[:].rearrange("p (j q) -> p j q", q=130)
        pr.v.tensor_tensor(xdf[:].rearrange("p (j q) -> p j q", q=128),
                           ltv[:, :, 1:129], ltv[:, :, 0:128], ALU.subtract)
        for j in range(NCH):
            pr.v.scalar_tensor_tensor(
                xrT[:, j * 128:(j + 1) * 128], xdf[:, j * 128:(j + 1) * 128],
                mur[:, j:j + 1], lt[:, j * 130: j * 130 + 128], ALU.mult, ALU.add)
            pr.v.scalar_tensor_tensor(
                xkT[:, j * 128:(j + 1) * 128], xdf[:, j * 128:(j + 1) * 128],
                muk[:, j:j + 1], lt[:, j * 130: j * 130 + 128], ALU.mult, ALU.add)
        for half in range(2):
            for j in range(NCH):
                pr.t.matmul(psF[:, half * 512:(half + 1) * 512],
                            xrT[:, j * 128:(j + 1) * 128],
                            fr_sb[:, j * D + half * 512: j * D + (half + 1) * 512],
                            start=(j == 0), stop=(j == NCH - 1))
            for j in range(NCH):
                pr.t.matmul(psK3[:, half * 512:(half + 1) * 512],
                            xkT[:, j * 128:(j + 1) * 128],
                            fk_sb[:, j * D + half * 512: j * D + (half + 1) * 512],
                            start=(j == 0), stop=(j == NCH - 1))
        pr.s.activation(r_sb[:], psF[:], AF.Sigmoid)
        pr.s.activation(kr[:], psK3[:], AF.Relu)
        for j in range(NCH):
            pr.t.transpose(psT3[:, j * 128:(j + 1) * 128],
                           kr[:, j * 128:(j + 1) * 128], ident[:])
        pr.s.activation(kaT[:], psT3[:], AF.Square)
        for half in range(2):
            for j in range(NCH):
                pr.t.matmul(psF[:, half * 512:(half + 1) * 512],
                            kaT[:, j * 128:(j + 1) * 128],
                            fv_sb[:, j * D + half * 512: j * D + (half + 1) * 512],
                            start=(j == 0), stop=(j == NCH - 1))
        pr.v.tensor_tensor(r_sb[:], r_sb[:], psF[:], ALU.mult)
        pr.v.tensor_tensor(x2sb[p][:], r_sb[:], x1sb[p][:], ALU.add)
        pr.y.dma_start(x2_h.ap()[t * TL:(t + 1) * TL, :], x2sb[p][:])


def emit_phase2(nc, pr, env, st):
    kres, vres, ident, onesf, jt, epsc = (env[k] for k in
        ["kres", "vres", "ident", "onesf", "jt", "epsc"])
    x_h, win, ctx_h = env["x_h"], env["win"], env["ctx_h"]
    rcph = env["rcph"]

    def sbt(name, shape, dtype):
        return st.enter_context(nc.sbuf_tensor(name, shape, dtype))

    def pst(name, shape, dtype=F32):
        return st.enter_context(nc.psum_tensor(name, shape, dtype))

    wq_sb = sbt("p2_wq", [128, NCH * D], BF16)
    bq_c = sbt("p2_bqc", [128, NCH], F32)
    xi0 = sbt("p2_x0", [128, D], F32)
    x_in = [xi0, xi0]
    stat = sbt("p2_stat", [128, 12], F32)
    mv = [sbt(f"p2_mv{i}", [128, 2], F32) for i in range(2)]
    col = [sbt(f"p2_col{i}", [128, 2], F32) for i in range(2)]
    ln_f = [sbt(f"p2_ln{i}", [128, D], F32) for i in range(2)]
    lnT = sbt("p2_lnT", [128, NCH * 512], BF16)
    qt = sbt("p2_qt", [128, NCH * 512], BF16)
    pt = [sbt(f"p2_pt{i}", [128, 2048], BF16) for i in range(2)]
    ctxe = [sbt(f"p2_ce{i}", [65, 512], F32) for i in range(2)]
    ctxs = sbt("p2_cs", [65, 512], F32)
    rcp = [sbt(f"p2_rcp{i}", [1, 512], F32) for i in range(2)]
    rcpb = [sbt(f"p2_rb{i}", [64, 512], F32) for i in range(2)]
    cnT = sbt("p2_cnT", [128, NCH * 512], BF16)
    psS = pst("p2_psS", [128, 2048])
    psC = [pst(f"p2_psC{i}", [65, 512]) for i in range(4)]

    G = NT // GT

    src = win["wq"].ap().rearrange("(j p) d -> p j d", p=128)
    pr.g.dma_start(wq_sb[:].rearrange("p (j d) -> p j d", d=D), src)
    pr.g.dma_start(bq_c[:], win["bq_c"].ap())

    for g in range(G):
        for u in range(GT):
            t = g * GT + u
            p = t % 2
            xb = x_in[p]
            pr.y.dma_start(xb[:], x_h.ap()[t * TL:(t + 1) * TL, :])
            pr.v.bn_stats(stat[:, 0:6], xb[:, 0:512])
            pr.v.bn_stats(stat[:, 6:12], xb[:, 512:1024])
            pr.v.bn_aggr(mv[p][:], stat[:])
            pr.s.activation(col[p][:, 0:1], mv[p][:, 1:2], AF.Sqrt, bias=epsc[:, 0:1])
            pr.v.reciprocal(col[p][:, 1:2], col[p][:, 0:1])
            pr.v.tensor_scalar(ln_f[p][:], xb[:], mv[p][:, 0:1], col[p][:, 1:2],
                               ALU.subtract, ALU.mult)
            for hf in range(2):
                for j in range(4):
                    jj = hf * 4 + j
                    pr.t.transpose(psS[:, j * 128:(j + 1) * 128],
                                   ln_f[p][:, jj * 128:(jj + 1) * 128], ident[:])
                pr.v.tensor_copy(
                    lnT[:].rearrange("p (j q) -> p j q", q=512)
                    [:, hf * 4:(hf + 1) * 4, u * 128:(u + 1) * 128],
                    psS[:, 0:512].rearrange("p (j q) -> p j q", q=128))
        for i in range(NCH):
            psQ = psS[:, 512:1024] if i % 2 == 0 else psS[:, 1536:2048]
            for j in range(NCH):
                pr.t.matmul(psQ, wq_sb[:, j * D + i * 128: j * D + (i + 1) * 128],
                            lnT[:, j * 512:(j + 1) * 512],
                            start=(j == 0), stop=(j == NCH - 1))
            pr.s.activation(qt[:, i * 512:(i + 1) * 512], psQ,
                            AF.Identity, bias=bq_c[:, i:i + 1])
        NB = NT // 2

        def s_mms(jj, b):
            for half in range(2):
                j = 2 * b + half
                pr.t.matmul(psS[:, half * 512:(half + 1) * 512],
                            kres[0:64, j * D + jj * 128: j * D + (jj + 1) * 128],
                            qt[0:64, jj * 512:(jj + 1) * 512],
                            tile_position=(0, 0))
                pr.t.matmul(psS[:, 1024 + half * 512:1536 + half * 512],
                            kres[64:128, j * D + jj * 128: j * D + (jj + 1) * 128],
                            qt[64:128, jj * 512:(jj + 1) * 512],
                            tile_position=(64, 0))

        def ctx_mms(jj, b):
            for ho in range(2):
                h = 2 * jj + ho
                for half in range(2):
                    j = 2 * b + half
                    pcol = ho * 1024 + half * 512
                    pr.t.matmul(psC[2 * ho][:],
                                vres[0:64, j * 1040 + h * 65: j * 1040 + (h + 1) * 65],
                                pt[b % 2][0:64, pcol:pcol + 512],
                                start=(j == 0), stop=(j == NT - 1),
                                tile_position=(0, 0))
                    pr.t.matmul(psC[2 * ho + 1][:],
                                vres[64:128, j * 1040 + h * 65: j * 1040 + (h + 1) * 65],
                                pt[b % 2][64:128, pcol:pcol + 512],
                                start=(j == 0), stop=(j == NT - 1),
                                tile_position=(64, 0))

        def ep_pre(jj):
            for ho in range(2):
                pr.v.tensor_copy(ctxs[:], psC[2 * ho + 1][:])
                pr.v.tensor_tensor(ctxe[ho][:], psC[2 * ho][:], ctxs[:], ALU.add)
                pr.v.reciprocal(rcp[ho][:], ctxe[ho][64:65, :])
                hap = rcph[ho].ap()[0:1, 0:512]
                pr.g.dma_start(hap, rcp[ho][:])
                bsrc = bass.broadcast_tensor_aps(hap, rcpb[ho][:])[0]
                pr.g.dma_start(rcpb[ho][:], bsrc)

        def ep_post(jj):
            for ho in range(2):
                pbase = 64 * ho
                pr.v.tensor_tensor(
                    cnT[pbase:pbase + 64, jj * 512:(jj + 1) * 512],
                    ctxe[ho][0:64, :], rcpb[ho][:], ALU.mult)

        for jj in range(NCH):
            for b in range(NB):
                s_mms(jj, b)
                pr.s.activation(pt[b % 2][:], psS[:], AF.Exp, scale=0.125)
                if b == 3 and jj > 0:
                    ep_post(jj - 1)
                if b >= 1:
                    ctx_mms(jj, b - 1)
            ctx_mms(jj, NB - 1)
            ep_pre(jj)
        ep_post(NCH - 1)
        for u in range(GT):
            t = g * GT + u
            pr.y.dma_start(
                ctx_h.ap()[t],
                cnT[:].rearrange("p (j q) -> p j q", q=512)[:, :, u * 128:(u + 1) * 128])


def emit_start(nc, pr, env):
    ident, onesf, jt, epsc = env["ident"], env["onesf"], env["jt"], env["epsc"]

    def s_clear(eng):
        for h_ in pr.all_sems():
            eng.sem_clear(h_)
    pr.ops["sync"].append((None, s_clear, "RAW"))

    def g_pre(eng):
        eng.memset(env["preg"][:], 0)
    pr.ops["gpsimd"].append((None, g_pre, "RAW"))

    def v_pre(eng):
        for i in range(16):
            eng.memset(env["prev"][:, i * 4:(i + 1) * 4], 0.0)
    pr.ops["vector"].append((None, v_pre, "RAW"))

    def c_pre(eng):
        for i in range(16):
            eng.memzero(env["pres"][:, i * 4:(i + 1) * 4])
    pr.ops["scalar"].append((None, c_pre, "RAW"))

    pr.g.iota(jt[:, 0:128], [[1, 128]], channel_multiplier=0,
              allow_small_or_imprecise_dtypes=True)
    pr.g.iota(jt[:, 128:256], [[0, 128]], channel_multiplier=1,
              allow_small_or_imprecise_dtypes=True)
    pr.v.tensor_tensor(ident[:], jt[:, 0:128], jt[:, 128:256], ALU.is_equal)
    pr.v.memset(onesf[:], 1.0)
    pr.v.memset(epsc[:], float(EPS))
    pr.v.memset(jt[:], 0.0)


def emit_phase1(nc, pr, env, st):
    kres, vres, ident, onesf, cm, jt, epsc = (env[k] for k in
        ["kres", "vres", "ident", "onesf", "cm", "jt", "epsc"])
    x_h, win, bi_h = env["x_h"], env["win"], env["bi_h"]

    def sbt(name, shape, dtype):
        return st.enter_context(nc.sbuf_tensor(name, shape, dtype))

    def pst(name, shape, dtype=F32):
        return st.enter_context(nc.psum_tensor(name, shape, dtype))

    wk_sb = sbt("p1_wk", [128, NCH * D], BF16)
    wv_sb = sbt("p1_wv", [128, NCH * D], BF16)
    bk_c = sbt("p1_bkc", [128, NCH], F32)
    bvb = sbt("p1_bvb", [128, D], F32)
    x_in = [sbt(f"p1_x{i}", [128, D], F32) for i in range(2)]
    stat = sbt("p1_stat", [128, 12], F32)
    mv = [sbt(f"p1_mv{i}", [128, 2], F32) for i in range(2)]
    col = [sbt(f"p1_col{i}", [128, 2], F32) for i in range(2)]
    ln_f = [sbt(f"p1_ln{i}", [128, D], F32) for i in range(2)]
    lnT = sbt("p1_lnT", [128, NCH * 512], BF16)
    bi_sb = sbt("p1_bi", [1, 2048], F32)
    row1 = sbt("p1_row", [1, D], F32)
    mx8 = sbt("p1_mx8", [1, 8], F32)
    psT = pst("p1_psT", [128, 1024])
    psK = [pst(f"p1_psK{i}", [128, 512]) for i in range(2)]
    psV = [pst(f"p1_psV{i}", [128, 512]) for i in range(2)]
    psB = pst("p1_psB", [128, 1024])

    G = NT // GT

    for w_sb, nm in ((wk_sb, "wk"), (wv_sb, "wv")):
        src = win[nm].ap().rearrange("(j p) d -> p j d", p=128)
        pr.g.dma_start(w_sb[:].rearrange("p (j d) -> p j d", d=D), src)
    pr.g.dma_start(bk_c[:], win["bk_c"].ap())
    pr.g.dma_start(bi_sb[:], bi_h.ap())
    pr.g.dma_start(row1[:], win["bv_r"].ap())

    pr.v.max(mx8[:], bi_sb[:])
    pr.v.tensor_scalar(bi_sb[:], bi_sb[:], mx8[0:1, 2:3], None, ALU.is_lt)
    for t in range(16):
        pr.t.transpose(psB[:, t:t + 1], bi_sb[0:1, t * 128:(t + 1) * 128],
                       ident[0:1, 0:1])
    pr.v.tensor_copy(cm[:], psB[:, 0:16])
    pr.t.matmul(psB[:, 0:512], onesf[:], row1[0:1, 0:512])
    pr.t.matmul(psB[:, 512:1024], onesf[:], row1[0:1, 512:1024])
    pr.v.tensor_copy(bvb[:], psB[:])
    pr.v.memset(vres[:].rearrange("p (b o) -> p b o", o=65)[:, :, 64:65], 1.0)

    for g in range(G):
        for u in range(GT):
            t = g * GT + u
            p = t % 2
            xb = x_in[p]
            pr.y.dma_start(xb[:], x_h.ap()[t * TL:(t + 1) * TL, :])
            pr.v.bn_stats(stat[:, 0:6], xb[:, 0:512])
            pr.v.bn_stats(stat[:, 6:12], xb[:, 512:1024])
            pr.v.bn_aggr(mv[p][:], stat[:])
            pr.s.activation(col[p][:, 0:1], mv[p][:, 1:2], AF.Sqrt, bias=epsc[:, 0:1])
            pr.v.reciprocal(col[p][:, 1:2], col[p][:, 0:1])
            pr.v.tensor_scalar(ln_f[p][:], xb[:], mv[p][:, 0:1], col[p][:, 1:2],
                               ALU.subtract, ALU.mult)
            for j in range(NCH):
                pr.t.transpose(psT[:, j * 128:(j + 1) * 128],
                               ln_f[p][:, j * 128:(j + 1) * 128], ident[:])
            pr.v.tensor_copy(
                lnT[:].rearrange("p (j q) -> p j q", q=512)[:, :, u * 128:(u + 1) * 128],
                psT[:].rearrange("p (j q) -> p j q", q=128))
        for i in range(NCH):
            pk = psK[i % 2]
            for j in range(NCH):
                pr.t.matmul(pk[:], wk_sb[:, j * D + i * 128: j * D + (i + 1) * 128],
                            lnT[:, j * 512:(j + 1) * 512],
                            start=(j == 0), stop=(j == NCH - 1))
            pr.s.activation(
                kres[:].rearrange("p (c j q) -> p c j q", j=NCH, q=128)
                [:, g * GT:(g + 1) * GT, i, :],
                pk[:].rearrange("p (u q) -> p u q", q=128),
                AF.Identity, bias=bk_c[:, i:i + 1])
        for u in range(GT):
            t = g * GT + u
            for half in range(2):
                pv = psV[half]
                for j in range(NCH):
                    pr.t.matmul(
                        pv[:],
                        lnT[:, j * 512 + u * 128: j * 512 + (u + 1) * 128],
                        wv_sb[:, j * D + half * 512: j * D + (half + 1) * 512],
                        start=(j == 0), stop=(j == NCH - 1))
            vblk = vres[:, t * 1040:(t + 1) * 1040]
            for half in range(2):
                pr.v.scalar_tensor_tensor(
                    vblk.rearrange("p (h o) -> p h o", o=65)[:, half * 8:(half + 1) * 8, 0:64],
                    psV[half][:].rearrange("p (h o) -> p h o", o=64),
                    1.0,
                    bvb[:, half * 512:(half + 1) * 512].rearrange("p (h o) -> p h o", o=64),
                    ALU.mult, ALU.add)
            if t < 16:
                pr.v.tensor_scalar(vblk, vblk, cm[:, t:t + 1], None, ALU.mult)



def stage_inputs(inputs):
    x = np.ascontiguousarray(np.asarray(inputs["L0_attn_a_x"], np.float32))
    tp = lambda a: np.ascontiguousarray(np.asarray(a, np.float32).T)
    chunked = lambda a: np.ascontiguousarray(np.asarray(a, np.float32).reshape(NCH, 128).T)
    row = lambda a: np.ascontiguousarray(np.asarray(a, np.float32)[None, :])

    common = {
        "wq_t": tp(inputs["L0_attn_a_Wq"]), "wk_t": tp(inputs["L0_attn_a_Wk"]),
        "wv_t": tp(inputs["L0_attn_a_Wv"]), "wo_t": tp(inputs["L0_attn_a_Wo"]),
        "bqw_t": tp(inputs["L0_attn_b_q_w"]), "bkw_t": tp(inputs["L0_attn_b_k_w"]),
        "bvw_t": tp(inputs["L0_attn_b_v_w"]), "bow_t": tp(inputs["L0_attn_b_o_w"]),
        "fr_t": tp(inputs["L0_ffn_W_r"]), "fk_t": tp(inputs["L0_ffn_W_k"]),
        "fv_t": tp(inputs["L0_ffn_W_v"]),
        "bq_c": chunked(inputs["L0_attn_a_bq"]), "bk_c": chunked(inputs["L0_attn_a_bk"]),
        "qb_c": chunked(inputs["L0_attn_b_q_b"]), "kb_c": chunked(inputs["L0_attn_b_k_b"]),
        "mur_c": chunked(inputs["L0_ffn_mu_r"]), "muk_c": chunked(inputs["L0_ffn_mu_k"]),
        "bv_r": row(inputs["L0_attn_a_bv"]), "bo_r": row(inputs["L0_attn_a_bo"]),
        "gam_r": row(inputs["L0_attn_a_gamma"]), "bet_r": row(inputs["L0_attn_a_beta"]),
        "vb_r": row(inputs["L0_attn_b_v_b"]), "ob_r": row(inputs["L0_attn_b_o_b"]),
        "bi": np.ascontiguousarray(np.asarray(inputs["L0_attn_a_band_importance"], np.float32)),
    }
    in_maps = []
    for c in range(8):
        m = dict(common)
        m["x0"] = np.ascontiguousarray(x[c])
        in_maps.append(m)
    return in_maps


def kernel(**inputs):
    in_maps = stage_inputs(inputs)
    nc = build_program(debug=0)
    res = run_bass_kernel_spmd(nc, in_maps, core_ids=list(range(8)))
    return np.stack([np.asarray(res.results[c]["out"], np.float32) for c in range(8)])


# version 15
# speedup vs baseline: 1.0821x; 1.0101x over previous
import sys
sys.path.insert(0, "/opt/trn_rl_repo")
import numpy as np
import concourse.bass as bass
import concourse.mybir as mybir
from concourse.bass_utils import run_bass_kernel_spmd



ENGS = ["tensor", "vector", "scalar", "gpsimd", "sync"]


def _names(args):
    out = []
    for a in args:
        if a is None or isinstance(a, (int, float)):
            continue
        t = getattr(a, "tensor", None)
        if t is not None:
            nm = getattr(t, "name", None)
            if nm is not None:
                if type(t).__name__ == "DRamTensorHandle":
                    nm = f"{nm}@{getattr(a, 'offset', 0)}"
                out.append(nm)
    return out


class _Facade:
    def __init__(self, prog, eng):
        self._p = prog
        self._e = eng

    def _st(self, reads, writes, inc, fn, dma=False):
        self._p.stage(self._e, fn, _names(reads), _names(writes), inc=inc, dma=dma)

    def dma_start(self, out, in_, **kw):
        self._st([in_], [out], True, lambda e: e.dma_start(out, in_, **kw), dma=True)

    def matmul(self, out, lhsT, rhs, start=True, stop=True, **kw):
        self._st([lhsT, rhs], [out], True,
                 lambda e: e.matmul(out, lhsT, rhs, start=start, stop=stop, **kw))

    def transpose(self, out, in_, ident, **kw):
        self._st([in_, ident], [out], True, lambda e: e.transpose(out, in_, ident, **kw))

    def activation(self, out, in_, func, bias=None, scale=None, accum_out=None):
        kw = {}
        if bias is not None:
            kw["bias"] = bias
        if scale is not None:
            kw["scale"] = scale
        if accum_out is not None:
            kw["accum_out"] = accum_out
        self._st([in_, bias, scale], [out, accum_out], True,
                 lambda e: e.activation(out, in_, func, **kw))

    def copy(self, out, in_):
        self._st([in_], [out], True, lambda e: e.copy(out, in_))

    def memzero(self, ap):
        self._st([], [ap], True, lambda e: e.memzero(ap))

    def tensor_copy(self, out, in_):
        self._st([in_], [out], True, lambda e: e.tensor_copy(out, in_))

    def memset(self, ap, c):
        self._st([], [ap], True, lambda e: e.memset(ap, c))

    def iota(self, ap, pattern, **kw):
        self._st([], [ap], True, lambda e: e.iota(ap, pattern, **kw))

    def tensor_tensor(self, out, in0, in1, op):
        self._st([in0, in1], [out], True, lambda e: e.tensor_tensor(out, in0, in1, op))

    def tensor_scalar(self, out, in0, s1, s2, op0, op1=None, accum_out=None):
        kw = dict(accum_out=accum_out) if accum_out is not None else {}
        if op1 is None:
            fn = lambda e: e.tensor_scalar(out, in0, s1, s2, op0, **kw)
        else:
            fn = lambda e: e.tensor_scalar(out, in0, s1, s2, op0, op1, **kw)
        self._st([in0, s1, s2], [out, accum_out], True, fn)

    def scalar_tensor_tensor(self, out, in0, scalar, in1, op0, op1):
        self._st([in0, scalar, in1], [out], True,
                 lambda e: e.scalar_tensor_tensor(out, in0, scalar, in1, op0, op1))

    def bn_stats(self, out, in_):
        self._st([in_], [out], True, lambda e: e.bn_stats(out, in_))

    def bn_aggr(self, out, in_):
        self._st([in_], [out], True, lambda e: e.bn_aggr(out, in_))

    def reciprocal(self, out, in_):
        self._st([in_], [out], True, lambda e: e.reciprocal(out, in_))

    def max(self, out, in_):
        self._st([in_], [out], True, lambda e: e.max(out, in_))

    def reduce_max(self, out, in_, axis, **kw):
        self._st([in_], [out], True, lambda e: e.reduce_max(out, in_, axis, **kw))

    def reduce_sum(self, out, in_, axis, **kw):
        self._st([in_], [out], True, lambda e: e.reduce_sum(out, in_, axis, **kw))

    def select(self, out, mask, a, b):
        self._st([mask, a, b], [out], True, lambda e: e.select(out, mask, a, b))

    def raw(self, fn, reads=(), writes=(), inc=True):
        self._st(list(reads), list(writes), inc, fn)


class Prog:
    def __init__(self, nc):
        self.nc = nc
        self.cnt_sem = {e: nc.alloc_semaphore(name=f"cnt_{e}") for e in ENGS}
        self.pD = nc.alloc_semaphore(name="pDr")
        self.cnt = {e: 0 for e in ENGS}
        self.NDQ = 8
        self.dma_sem = {e: [nc.alloc_semaphore(name=f"dq_{e}_{k}") for k in range(self.NDQ)]
                        for e in ("sync", "gpsimd", "scalar")}
        self.dma_cnt = {e: [0] * self.NDQ for e in ("sync", "gpsimd", "scalar")}
        self.dma_n = {e: 0 for e in ("sync", "gpsimd", "scalar")}
        self.pe_fifo = False
        self.last_w = {}
        self.readers = {}
        self.cur = {e: {} for e in ENGS}
        self.pD_n = 0
        self.ops = {e: [] for e in ENGS}
        for e in ENGS:
            setattr(self, {"tensor": "t", "vector": "v", "scalar": "s",
                           "gpsimd": "g", "sync": "y"}[e], _Facade(self, e))

    def all_sems(self):
        out = list(self.cnt_sem.values()) + [self.pD]
        for sems in self.dma_sem.values():
            out.extend(sems)
        return out

    def _sem_of(self, key):
        kind, ident = key
        if kind == "e":
            return self.cnt_sem[ident]
        e, k = ident
        return self.dma_sem[e][k]

    def _need(self, eng, waits, key, idx):
        kind, ident = key
        if kind == "e" and ident == eng:
            if idx > self.cnt[eng]:
                return
            if eng == "tensor" and self.pe_fifo:
                return
        cur = self.cur[eng].get(key, 0)
        if idx <= cur:
            return
        waits.append((key, idx))
        self.cur[eng][key] = idx

    def stage(self, eng, fn, rn, wn, inc=True, dma=False):
        waits = []
        for nm in rn:
            w = self.last_w.get(nm)
            if w is not None:
                self._need(eng, waits, (w[0], w[1]), w[2])
        for nm in wn:
            w = self.last_w.get(nm)
            if w is not None:
                self._need(eng, waits, (w[0], w[1]), w[2])
            for key, idx in self.readers.get(nm, {}).items():
                self._need(eng, waits, key, idx)
        if dma:
            k = self.dma_n[eng] % self.NDQ
            self.dma_n[eng] += 1
            if self.dma_cnt[eng][k]:
                self._need(eng, waits, ("dm", (eng, k)), self.dma_cnt[eng][k])
            self.dma_cnt[eng][k] += 16
            marker = ("dm", (eng, k), self.dma_cnt[eng][k])
            inc_spec = ("dm", (eng, k), 16)
        elif inc:
            self.cnt[eng] += 1
            marker = ("e", eng, self.cnt[eng])
            inc_spec = ("e", eng, 1)
        else:
            marker = ("e", eng, self.cnt[eng] + 1)
            inc_spec = None
        for nm in wn:
            self.last_w[nm] = marker
            self.readers[nm] = {}
        for nm in rn:
            self.readers.setdefault(nm, {})[(marker[0], marker[1])] = marker[2]
        self.ops[eng].append((waits, fn, inc_spec))

    def raw_stage(self, eng, fn):
        self.ops[eng].append(([], fn, None))

    def rendezvous(self):
        self.pD_n += len(ENGS)
        target = self.pD_n
        for e in ENGS:
            cnt = self.cnt[e]
            dts = []
            if e in self.dma_cnt:
                dts = [(self.dma_sem[e][k], c) for k, c in enumerate(self.dma_cnt[e]) if c]

            def f(eng, e=e, cnt=cnt, dts=dts, target=target):
                if cnt:
                    eng.wait_ge(self.cnt_sem[e], cnt)
                for s, c in dts:
                    eng.wait_ge(s, c)
                eng.sem_inc(self.pD, 1)
                eng.wait_ge(self.pD, target)
            self.ops[e].append((None, f, "RAW"))
        self.last_w.clear()
        self.readers.clear()
        for e in ENGS:
            self.cur[e] = {}

    def flush(self, block):
        for e in ENGS:
            ops = self.ops[e]
            if not ops:
                continue

            def section(eng, ops=ops, e=e):
                for waits, fn, inc_spec in ops:
                    if waits is None:
                        fn(eng)
                        continue
                    for key, val in (waits or ()):
                        eng.wait_ge(self._sem_of(key), val)
                    instr = fn(eng)
                    if inc_spec == "RAW":
                        continue
                    if inc_spec is not None:
                        k, ident, amt = inc_spec
                        sem = self._sem_of((k, ident))
                        instr.then_inc(sem, amt)
            getattr(block, e)(section)
        self.ops = {e: [] for e in ENGS}




F32 = mybir.dt.float32
BF16 = mybir.dt.bfloat16
AF = mybir.ActivationFunctionType
ALU = mybir.AluOpType

NT = 32
TL = 128
D = 1024
NCH = 8
H = 16
HD = 64
GT = 4
EPS = 1e-5

WEIGHT_NAMES = ["wq", "wk", "wv", "wo", "bqw", "bkw", "bvw", "bow", "fr", "fk", "fv"]


def build_program(debug=0):
    nc = bass.Bass("TRN2", target_bir_lowering=False)

    x_h = nc.dram_tensor("x0", [NT * TL, D], F32, kind="ExternalInput")
    win = {}
    for w in WEIGHT_NAMES:
        win[w] = nc.dram_tensor(w + "_t", [D, D], F32, kind="ExternalInput")
    for v in ["bq_c", "bk_c", "qb_c", "kb_c", "mur_c", "muk_c"]:
        win[v] = nc.dram_tensor(v, [128, NCH], F32, kind="ExternalInput")
    for v in ["bv_r", "bo_r", "gam_r", "bet_r", "vb_r", "ob_r"]:
        win[v] = nc.dram_tensor(v, [1, D], F32, kind="ExternalInput")
    bi_h = nc.dram_tensor("bi", [1, 2048], F32, kind="ExternalInput")

    out_h = nc.dram_tensor("out", [NT * TL, D], F32, kind="ExternalOutput")
    rcph = [nc.dram_tensor(f"rcph{i}", [1, 1024], F32, kind="Internal")
            for i in range(2)]
    dbg = "ExternalOutput" if debug else "Internal"
    ctx_h = nc.dram_tensor("ctxh", [NT, 128, NCH * 128], BF16, kind=dbg)
    x2_h = nc.dram_tensor("x2h", [NT * TL, D], F32, kind=dbg)
    x3_h = nc.dram_tensor("x3h", [NT * TL, D], F32, kind=dbg)
    kdbg_h = vdbg_h = None
    if debug:
        kdbg_h = nc.dram_tensor("kdbg", [NT, 128, D], BF16, kind="ExternalOutput")
        vdbg_h = nc.dram_tensor("vdbg", [NT, 128, 1040], BF16, kind="ExternalOutput")

    pr = Prog(nc)
    from contextlib import ExitStack
    with ExitStack() as top:
        def sb(name, shape, dtype):
            return top.enter_context(nc.sbuf_tensor(name, shape, dtype))

        ident = sb("ident", [128, 128], F32)
        onesf = sb("onesf", [1, 128], F32)
        cm = sb("cm", [128, 16], F32)
        epsc = sb("epsc", [128, 1], F32)
        jt = sb("jt", [128, 512], F32)
        preg = sb("preg", [128, 64], F32)
        prev = sb("prev", [128, 64], F32)
        pres = sb("pres", [128, 64], F32)
        env0 = dict(preg=preg, prev=prev, pres=pres)

        env = dict(x_h=x_h, win=win, bi_h=bi_h, out_h=out_h, ctx_h=ctx_h,
                   rcph=rcph,
                   x2_h=x2_h, x3_h=x3_h, kdbg_h=kdbg_h, vdbg_h=vdbg_h,
                   ident=ident, onesf=onesf, cm=cm, epsc=epsc, jt=jt, **env0)

        with nc.Block() as block:
            emit_start(nc, pr, env)

            with ExitStack() as p12:
                kres = p12.enter_context(nc.sbuf_tensor("kres", [128, NT * D], BF16))
                vres = p12.enter_context(nc.sbuf_tensor("vres", [128, NT * 1040], BF16))
                env["kres"], env["vres"] = kres, vres

                with ExitStack() as p1s:
                    pr.pe_fifo = True
                    emit_phase1(nc, pr, env, p1s)
                    pr.pe_fifo = False
                pr.rendezvous()

                if debug == 2:
                    for t in range(NT):
                        pr.y.dma_start(kdbg_h.ap()[t], kres[:, t * D:(t + 1) * D])
                        pr.y.dma_start(vdbg_h.ap()[t], vres[:, t * 1040:(t + 1) * 1040])
                    pr.rendezvous()
                    pr.flush(block)
                    return nc
                with ExitStack() as p2s:
                    pr.pe_fifo = True
                    emit_phase2(nc, pr, env, p2s)
                    pr.pe_fifo = False
                pr.rendezvous()
                if debug == 3:
                    pr.flush(block)
                    return nc
            with ExitStack() as p3s:
                pr.pe_fifo = True
                emit_phase3a(nc, pr, env, p3s)
                pr.pe_fifo = False
            pr.rendezvous()
            if debug == 4:
                pr.flush(block)
                return nc
            with ExitStack() as p3bs:
                pr.pe_fifo = True
                emit_phase3b(nc, pr, env, p3bs)
                pr.pe_fifo = False
            pr.rendezvous()
            if debug == 5:
                pr.flush(block)
                return nc
            with ExitStack() as p3cs:
                pr.pe_fifo = True
                emit_phase3c(nc, pr, env, p3cs)
                pr.pe_fifo = False
            pr.rendezvous()
            pr.flush(block)
    return nc


def ln_chain(pr, stat, mv, col, epsc, xb, out_f):
    pr.v.bn_stats(stat[:, 0:6], xb[:, 0:512])
    pr.v.bn_stats(stat[:, 6:12], xb[:, 512:1024])
    pr.v.bn_aggr(mv[:], stat[:])
    pr.s.activation(col[:, 0:1], mv[:, 1:2], AF.Sqrt, bias=epsc[:, 0:1])
    pr.v.reciprocal(col[:, 1:2], col[:, 0:1])
    pr.v.tensor_scalar(out_f[:], xb[:], mv[:, 0:1], col[:, 1:2],
                       ALU.subtract, ALU.mult)


def emit_phase3b(nc, pr, env, st):
    ident, onesf, jt, epsc = (env[k] for k in ["ident", "onesf", "jt", "epsc"])
    win, x2_h, x3_h = env["win"], env["x2_h"], env["x3_h"]
    rcph = env["rcph"]

    def sbt(name, shape, dtype):
        return st.enter_context(nc.sbuf_tensor(name, shape, dtype))

    def pst(name, shape, dtype=F32):
        return st.enter_context(nc.psum_tensor(name, shape, dtype))

    wq_sb = sbt("b_wq", [128, NCH * D], BF16)
    wk_sb = sbt("b_wk", [128, NCH * D], BF16)
    wv_sb = sbt("b_wv", [128, NCH * D], BF16)
    wo_sb = sbt("b_wo", [128, NCH * D], BF16)
    qb_c = sbt("b_qbc", [128, NCH], F32)
    kb_c = sbt("b_kbc", [128, NCH], F32)
    rows = sbt("b_rows", [1, 2 * D], F32)
    vbb = sbt("b_vbb", [128, D], F32)
    obb = sbt("b_obb", [128, D], F32)
    x2r = [sbt(f"b_x2_{i}", [128, D], F32) for i in range(3)]
    stat = sbt("b_stat", [128, 12], F32)
    mv = [sbt(f"b_mv{i}", [128, 2], F32) for i in range(2)]
    col = [sbt(f"b_col{i}", [128, 2], F32) for i in range(2)]
    ln_f = [sbt(f"b_ln{i}", [128, D], F32) for i in range(2)]
    lnT = [sbt(f"b_lnT{i}", [128, NCH * 128], BF16) for i in range(2)]
    qbT = [sbt(f"b_qbT{i}", [128, NCH * 128], BF16) for i in range(2)]
    kbT = [sbt(f"b_kbT{i}", [128, NCH * 128], BF16) for i in range(3)]
    kbZ = sbt("b_kbZ", [128, NCH * 128], BF16)
    vbr = [sbt(f"b_vb{i}", [128, H * 65], BF16) for i in range(3)]
    vbZ = sbt("b_vbZ", [128, H * 65], BF16)
    masks = [sbt(f"b_msk{i}", [128, NCH * 128], BF16) for i in range(3)]
    ptb = [sbt(f"b_pt{i}", [128, NCH * 128], BF16) for i in range(3)]
    ctxe = sbt("b_ce", [65, NCH * 128], F32)
    ctxs = sbt("b_cs", [65, NCH * 128], F32)
    rcpb3 = sbt("b_rb", [64, NCH * 128], F32)
    rcp = sbt("b_rcp", [1, NCH * 128], F32)
    cnT = sbt("b_cnT", [128, NCH * 128], BF16)
    x3sb = [sbt(f"b_x3_{i}", [128, D], F32) for i in range(2)]
    mt1 = sbt("b_mt1", [128, 128], F32)
    mt2 = sbt("b_mt2", [128, 128], F32)
    mt3 = sbt("b_mt3", [128, 128], F32)
    PA = pst("b_PA", [128, 1024])
    PB = pst("b_PB", [128, 1024])
    PC = pst("b_PC", [128, 1024])
    PD = pst("b_PD", [128, 1024])

    for w_sb, nm in ((wq_sb, "bqw"), (wk_sb, "bkw"), (wv_sb, "bvw"), (wo_sb, "bow")):
        src_ = win[nm].ap().rearrange("(j p) d -> p j d", p=128)
        pr.g.dma_start(w_sb[:].rearrange("p (j d) -> p j d", d=D), src_)
    pr.g.dma_start(qb_c[:], win["qb_c"].ap())
    pr.g.dma_start(kb_c[:], win["kb_c"].ap())
    pr.g.dma_start(rows[0:1, 0:D], win["vb_r"].ap())
    pr.g.dma_start(rows[0:1, D:2 * D], win["ob_r"].ap())
    for bt, off in ((vbb, 0), (obb, D)):
        for half in range(2):
            pr.t.matmul(PA[:, half * 512:(half + 1) * 512], onesf[:],
                        rows[0:1, off + half * 512: off + (half + 1) * 512])
        pr.v.tensor_copy(bt[:], PA[:])
    pr.g.iota(mt1[:], [[1, 128]], channel_multiplier=0,
              allow_small_or_imprecise_dtypes=True)
    pr.g.iota(mt2[:], [[0, 128]], channel_multiplier=1,
              allow_small_or_imprecise_dtypes=True)
    pr.v.tensor_tensor(mt2[:], mt2[:], mt1[:], ALU.subtract)
    for s in range(3):
        pr.v.tensor_scalar(mt1[:], mt2[:], float((s - 1) * 128), None, ALU.add)
        pr.v.tensor_scalar(mt3[:], mt1[:], 32.0, None, ALU.is_le)
        pr.v.tensor_scalar(mt1[:], mt1[:], -32.0, None, ALU.is_ge)
        pr.v.tensor_tensor(mt1[:], mt1[:], mt3[:], ALU.mult)
        for jj in range(NCH):
            pr.v.tensor_copy(masks[s][:, jj * 128:(jj + 1) * 128], mt1[:])
    pr.v.memset(kbZ[:], 0.0)
    pr.v.memset(vbZ[:], 0.0)
    pr.v.memset(vbZ[:].rearrange("p (h o) -> p h o", o=65)[:, :, 64:65], 1.0)
    for i in range(3):
        pr.v.memset(vbr[i][:].rearrange("p (h o) -> p h o", o=65)[:, :, 64:65], 1.0)

    for u in range(NT + 1):
        if u < NT:
            p = u % 2
            xb = x2r[u % 3]
            pr.y.dma_start(xb[:], x2_h.ap()[u * TL:(u + 1) * TL, :])
            ln_chain(pr, stat, mv[p], col[p], epsc, xb, ln_f[p])
            for j in range(NCH):
                pr.t.transpose(PD[:, j * 128:(j + 1) * 128],
                               ln_f[p][:, j * 128:(j + 1) * 128], ident[:])
            pr.v.tensor_copy(lnT[p][:], PD[:])
            for wsb, bias, dst, PX in ((wq_sb, qb_c, qbT[p], PA),
                                       (wk_sb, kb_c, kbT[u % 3], PD)):
                for i in range(NCH):
                    for j in range(NCH):
                        pr.t.matmul(PX[:, i * 128:(i + 1) * 128],
                                    wsb[:, j * D + i * 128: j * D + (i + 1) * 128],
                                    lnT[p][:, j * 128:(j + 1) * 128],
                                    start=(j == 0), stop=(j == NCH - 1))
                for i in range(NCH):
                    pr.s.activation(dst[:, i * 128:(i + 1) * 128],
                                    PX[:, i * 128:(i + 1) * 128],
                                    AF.Identity, bias=bias[:, i:i + 1])
            for half in range(2):
                for j in range(NCH):
                    pr.t.matmul(PB[:, half * 512:(half + 1) * 512],
                                lnT[p][:, j * 128:(j + 1) * 128],
                                wv_sb[:, j * D + half * 512: j * D + (half + 1) * 512],
                                start=(j == 0), stop=(j == NCH - 1))
            for half in range(2):
                pr.v.scalar_tensor_tensor(
                    vbr[u % 3][:].rearrange("p (h o) -> p h o", o=65)
                    [:, half * 8:(half + 1) * 8, 0:64],
                    PB[:, half * 512:(half + 1) * 512]
                    .rearrange("p (h o) -> p h o", o=64),
                    1.0,
                    vbb[:, half * 512:(half + 1) * 512]
                    .rearrange("p (h o) -> p h o", o=64),
                    ALU.mult, ALU.add)
        pr.pe_fifo = False
        if u == 0:
            continue
        w = u - 1
        pw = w % 2
        kslots = [kbZ if w == 0 else kbT[(w - 1) % 3], kbT[w % 3],
                  kbZ if w == NT - 1 else kbT[(w + 1) % 3]]
        vslots = [vbZ if w == 0 else vbr[(w - 1) % 3], vbr[w % 3],
                  vbZ if w == NT - 1 else vbr[(w + 1) % 3]]
        PS = [PA, PB, PC]
        for half in range(2):
            for s in range(3):
                for hp in range(4):
                    jj = 4 * half + hp
                    pr.t.matmul(PS[s][:, hp * 128:(hp + 1) * 128],
                                kslots[s][0:64, jj * 128:(jj + 1) * 128],
                                qbT[pw][0:64, jj * 128:(jj + 1) * 128],
                                tile_position=(0, 0))
                    pr.t.matmul(PS[s][:, 512 + hp * 128:640 + hp * 128],
                                kslots[s][64:128, jj * 128:(jj + 1) * 128],
                                qbT[pw][64:128, jj * 128:(jj + 1) * 128],
                                tile_position=(64, 0))
            for s in range(3):
                pr.s.activation(ptb[s][:], PS[s][:], AF.Exp, scale=0.125)
                pr.v.tensor_tensor(ptb[s][:], ptb[s][:], masks[s][:], ALU.mult)
            for hh in range(8):
                h = half * 8 + hh
                ncol = (hh % 2) * 512 + (hh // 2) * 128
                for s in range(3):
                    pr.t.matmul(PD[0:65, hh * 128:(hh + 1) * 128],
                                vslots[s][0:64, h * 65:(h + 1) * 65],
                                ptb[s][0:64, ncol:ncol + 128],
                                start=(s == 0), stop=(s == 2),
                                tile_position=(0, 0))
                    pr.t.matmul(PA[0:65, hh * 128:(hh + 1) * 128],
                                vslots[s][64:128, h * 65:(h + 1) * 65],
                                ptb[s][64:128, ncol:ncol + 128],
                                start=(s == 0), stop=(s == 2),
                                tile_position=(64, 0))
            pr.v.tensor_copy(ctxs[:], PA[0:65, :])
            pr.v.tensor_tensor(ctxe[:], PD[0:65, :], ctxs[:], ALU.add)
            pr.v.reciprocal(rcp[:], ctxe[64:65, :])
            hap = rcph[w % 2].ap()
            pr.g.dma_start(hap, rcp[:])
            bsrc = bass.broadcast_tensor_aps(hap, rcpb3[:])[0]
            pr.g.dma_start(rcpb3[:], bsrc)
            for par in range(2):
                pr.v.tensor_tensor(
                    cnT[par * 64:(par + 1) * 64, :]
                    .rearrange("p (m q) -> p m q", q=128)[:, 4 * half:4 * half + 4, :],
                    ctxe[0:64, :].rearrange("p (m q) -> p m q", q=256)
                    [:, :, par * 128:(par + 1) * 128],
                    rcpb3[:].rearrange("p (m q) -> p m q", q=256)
                    [:, :, par * 128:(par + 1) * 128],
                    ALU.mult)
        for half in range(2):
            for jj in range(NCH):
                pr.t.matmul(PB[:, half * 512:(half + 1) * 512],
                            cnT[:, jj * 128:(jj + 1) * 128],
                            wo_sb[:, jj * D + half * 512: jj * D + (half + 1) * 512],
                            start=(jj == 0), stop=(jj == NCH - 1))
        pr.v.scalar_tensor_tensor(x3sb[pw][:], PB[:], 1.0, obb[:], ALU.mult, ALU.add)
        pr.v.tensor_tensor(x3sb[pw][:], x3sb[pw][:], x2r[w % 3][:], ALU.add)
        pr.y.dma_start(x3_h.ap()[w * TL:(w + 1) * TL, :], x3sb[pw][:])


def emit_phase3c(nc, pr, env, st):
    ident, onesf, jt, epsc = (env[k] for k in ["ident", "onesf", "jt", "epsc"])
    win, x3_h, out_h = env["win"], env["x3_h"], env["out_h"]

    def sbt(name, shape, dtype):
        return st.enter_context(nc.sbuf_tensor(name, shape, dtype))

    def pst(name, shape, dtype=F32):
        return st.enter_context(nc.psum_tensor(name, shape, dtype))

    fr_sb = sbt("c_fr", [128, NCH * D], BF16)
    fk_sb = sbt("c_fk", [128, NCH * D], BF16)
    fv_sb = sbt("c_fv", [128, NCH * D], BF16)
    mur = sbt("c_mur", [128, NCH], F32)
    muk = sbt("c_muk", [128, NCH], F32)
    x_in = [sbt(f"c_x{i}", [128, D], F32) for i in range(2)]
    stat = sbt("c_stat", [128, 12], F32)
    mv = [sbt(f"c_mv{i}", [128, 2], F32) for i in range(2)]
    col = [sbt(f"c_col{i}", [128, 2], F32) for i in range(2)]
    ln_f = [sbt(f"c_ln{i}", [128, D], F32) for i in range(2)]
    lnT3 = [sbt(f"c_lnT{i}", [128, NCH * 130], BF16) for i in range(2)]
    xdf = sbt("c_xdf", [128, NCH * 128], BF16)
    xrT = sbt("c_xrT", [128, NCH * 128], BF16)
    xkT = sbt("c_xkT", [128, NCH * 128], BF16)
    r_sb = sbt("c_r", [128, D], F32)
    kr = sbt("c_kr", [128, D], F32)
    kaT = sbt("c_kaT", [128, NCH * 128], BF16)
    osb = [sbt(f"c_o{i}", [128, D], F32) for i in range(2)]
    psF = pst("c_psF", [128, 1024])
    psK3 = pst("c_psK", [128, 1024])
    psT3 = pst("c_psT", [128, 1024])

    for w_sb, nm in ((fr_sb, "fr"), (fk_sb, "fk"), (fv_sb, "fv")):
        src_ = win[nm].ap().rearrange("(j p) d -> p j d", p=128)
        pr.g.dma_start(w_sb[:].rearrange("p (j d) -> p j d", d=D), src_)
    pr.g.dma_start(mur[:], win["mur_c"].ap())
    pr.g.dma_start(muk[:], win["muk_c"].ap())

    for t in range(NT):
        p = t % 2
        xb = x_in[p]
        pr.y.dma_start(xb[:], x3_h.ap()[t * TL:(t + 1) * TL, :])
        ln_chain(pr, stat, mv[p], col[p], epsc, xb, ln_f[p])
        for j in range(NCH):
            pr.t.transpose(psT3[:, j * 128:(j + 1) * 128],
                           ln_f[p][:, j * 128:(j + 1) * 128], ident[:])
        lt = lnT3[p]
        ltprev = lnT3[1 - p]
        pr.v.tensor_copy(
            lt[:].rearrange("p (j q) -> p j q", q=130)[:, :, 1:129],
            psT3[:].rearrange("p (j q) -> p j q", q=128))
        if t == 0:
            pr.v.memset(lt[:].rearrange("p (j q) -> p j q", q=130)[:, :, 0:1], 0.0)
        else:
            pr.v.tensor_copy(
                lt[:].rearrange("p (j q) -> p j q", q=130)[:, :, 0:1],
                ltprev[:].rearrange("p (j q) -> p j q", q=130)[:, :, 128:129])
        ltv = lt[:].rearrange("p (j q) -> p j q", q=130)
        pr.v.tensor_tensor(xdf[:].rearrange("p (j q) -> p j q", q=128),
                           ltv[:, :, 1:129], ltv[:, :, 0:128], ALU.subtract)
        for j in range(NCH):
            pr.v.scalar_tensor_tensor(
                xrT[:, j * 128:(j + 1) * 128], xdf[:, j * 128:(j + 1) * 128],
                mur[:, j:j + 1], lt[:, j * 130: j * 130 + 128], ALU.mult, ALU.add)
            pr.v.scalar_tensor_tensor(
                xkT[:, j * 128:(j + 1) * 128], xdf[:, j * 128:(j + 1) * 128],
                muk[:, j:j + 1], lt[:, j * 130: j * 130 + 128], ALU.mult, ALU.add)
        for half in range(2):
            for j in range(NCH):
                pr.t.matmul(psF[:, half * 512:(half + 1) * 512],
                            xrT[:, j * 128:(j + 1) * 128],
                            fr_sb[:, j * D + half * 512: j * D + (half + 1) * 512],
                            start=(j == 0), stop=(j == NCH - 1))
            for j in range(NCH):
                pr.t.matmul(psK3[:, half * 512:(half + 1) * 512],
                            xkT[:, j * 128:(j + 1) * 128],
                            fk_sb[:, j * D + half * 512: j * D + (half + 1) * 512],
                            start=(j == 0), stop=(j == NCH - 1))
        pr.s.activation(r_sb[:], psF[:], AF.Sigmoid)
        pr.s.activation(kr[:], psK3[:], AF.Relu)
        for j in range(NCH):
            pr.t.transpose(psT3[:, j * 128:(j + 1) * 128],
                           kr[:, j * 128:(j + 1) * 128], ident[:])
        pr.s.activation(kaT[:], psT3[:], AF.Square)
        for half in range(2):
            for j in range(NCH):
                pr.t.matmul(psF[:, half * 512:(half + 1) * 512],
                            kaT[:, j * 128:(j + 1) * 128],
                            fv_sb[:, j * D + half * 512: j * D + (half + 1) * 512],
                            start=(j == 0), stop=(j == NCH - 1))
        pr.v.tensor_tensor(r_sb[:], r_sb[:], psF[:], ALU.mult)
        pr.v.tensor_tensor(osb[p][:], r_sb[:], xb[:], ALU.add)
        pr.y.dma_start(env["out_h"].ap()[t * TL:(t + 1) * TL, :], osb[p][:])


def emit_phase3a(nc, pr, env, st):
    ident, onesf, jt, epsc = (env[k] for k in ["ident", "onesf", "jt", "epsc"])
    x_h, win, ctx_h, x2_h = env["x_h"], env["win"], env["ctx_h"], env["x2_h"]

    def sbt(name, shape, dtype):
        return st.enter_context(nc.sbuf_tensor(name, shape, dtype))

    def pst(name, shape, dtype=F32):
        return st.enter_context(nc.psum_tensor(name, shape, dtype))

    wo_sb = sbt("p3_wo", [128, NCH * D], BF16)
    fr_sb = sbt("p3_fr", [128, NCH * D], BF16)
    fk_sb = sbt("p3_fk", [128, NCH * D], BF16)
    fv_sb = sbt("p3_fv", [128, NCH * D], BF16)
    mur = sbt("p3_mur", [128, NCH], F32)
    muk = sbt("p3_muk", [128, NCH], F32)
    rows = sbt("p3_rows", [1, 3 * D], F32)
    bob = sbt("p3_bob", [128, D], F32)
    gab = sbt("p3_gab", [128, D], F32)
    beb = sbt("p3_beb", [128, D], F32)
    x_in = [sbt(f"p3_x{i}", [128, D], F32) for i in range(2)]
    ct_in = [sbt(f"p3_ct{i}", [128, NCH * 128], BF16) for i in range(2)]
    stat = sbt("p3_stat", [128, 12], F32)
    mv = [sbt(f"p3_mv{i}", [128, 2], F32) for i in range(2)]
    col = [sbt(f"p3_col{i}", [128, 2], F32) for i in range(2)]
    ln_f = [sbt(f"p3_ln{i}", [128, D], F32) for i in range(2)]
    ysb = sbt("p3_y", [128, D], F32)
    x1sb = [sbt(f"p3_x1_{i}", [128, D], F32) for i in range(2)]
    lnT3 = [sbt(f"p3_lnT{i}", [128, NCH * 130], BF16) for i in range(2)]
    xdf = sbt("p3_xdf", [128, NCH * 128], BF16)
    xrT = sbt("p3_xrT", [128, NCH * 128], BF16)
    xkT = sbt("p3_xkT", [128, NCH * 128], BF16)
    r_sb = sbt("p3_r", [128, D], F32)
    kr = sbt("p3_kr", [128, D], F32)
    kaT = sbt("p3_kaT", [128, NCH * 128], BF16)
    x2sb = [sbt(f"p3_x2_{i}", [128, D], F32) for i in range(2)]
    psO = pst("p3_psO", [128, 1024])
    psF = pst("p3_psF", [128, 1024])
    psK3 = pst("p3_psK", [128, 1024])
    psT3 = pst("p3_psT", [128, 1024])

    for w_sb, nm in ((wo_sb, "wo"), (fr_sb, "fr"), (fk_sb, "fk"), (fv_sb, "fv")):
        src_ = win[nm].ap().rearrange("(j p) d -> p j d", p=128)
        pr.g.dma_start(w_sb[:].rearrange("p (j d) -> p j d", d=D), src_)
    pr.g.dma_start(mur[:], win["mur_c"].ap())
    pr.g.dma_start(muk[:], win["muk_c"].ap())
    pr.g.dma_start(rows[0:1, 0:D], win["bo_r"].ap())
    pr.g.dma_start(rows[0:1, D:2 * D], win["gam_r"].ap())
    pr.g.dma_start(rows[0:1, 2 * D:3 * D], win["bet_r"].ap())
    for bt, off in ((bob, 0), (gab, D), (beb, 2 * D)):
        for half in range(2):
            pr.t.matmul(psO[:, half * 512:(half + 1) * 512], onesf[:],
                        rows[0:1, off + half * 512: off + (half + 1) * 512])
        pr.v.tensor_copy(bt[:], psO[:])

    def ln_chain(xb, p, out_f):
        pr.v.bn_stats(stat[:, 0:6], xb[:, 0:512])
        pr.v.bn_stats(stat[:, 6:12], xb[:, 512:1024])
        pr.v.bn_aggr(mv[p][:], stat[:])
        pr.s.activation(col[p][:, 0:1], mv[p][:, 1:2], AF.Sqrt, bias=epsc[:, 0:1])
        pr.v.reciprocal(col[p][:, 1:2], col[p][:, 0:1])
        pr.v.tensor_scalar(out_f[:], xb[:], mv[p][:, 0:1], col[p][:, 1:2],
                           ALU.subtract, ALU.mult)

    for t in range(NT):
        p = t % 2
        xb = x_in[p]
        ctb = ct_in[p]
        pr.y.dma_start(xb[:], x_h.ap()[t * TL:(t + 1) * TL, :])
        pr.y.dma_start(ctb[:], ctx_h.ap()[t])
        ln_chain(xb, p, ln_f[p])
        for half in range(2):
            for jj in range(NCH):
                pr.t.matmul(psO[:, half * 512:(half + 1) * 512],
                            ctb[:, jj * 128:(jj + 1) * 128],
                            wo_sb[:, jj * D + half * 512: jj * D + (half + 1) * 512],
                            start=(jj == 0), stop=(jj == NCH - 1))
        pr.v.scalar_tensor_tensor(ysb[:], psO[:], 1.0, bob[:], ALU.mult, ALU.add)
        pr.v.tensor_tensor(ysb[:], ysb[:], ln_f[p][:], ALU.add)
        ln_chain(ysb, p, ln_f[p])
        pr.v.tensor_tensor(ln_f[p][:], ln_f[p][:], gab[:], ALU.mult)
        pr.v.tensor_tensor(ln_f[p][:], ln_f[p][:], beb[:], ALU.add)
        pr.v.tensor_tensor(x1sb[p][:], ln_f[p][:], xb[:], ALU.add)
        ln_chain(x1sb[p], p, ln_f[p])
        for j in range(NCH):
            pr.t.transpose(psT3[:, j * 128:(j + 1) * 128],
                           ln_f[p][:, j * 128:(j + 1) * 128], ident[:])
        lt = lnT3[p]
        ltprev = lnT3[1 - p]
        pr.v.tensor_copy(
            lt[:].rearrange("p (j q) -> p j q", q=130)[:, :, 1:129],
            psT3[:].rearrange("p (j q) -> p j q", q=128))
        if t == 0:
            pr.v.memset(lt[:].rearrange("p (j q) -> p j q", q=130)[:, :, 0:1], 0.0)
        else:
            pr.v.tensor_copy(
                lt[:].rearrange("p (j q) -> p j q", q=130)[:, :, 0:1],
                ltprev[:].rearrange("p (j q) -> p j q", q=130)[:, :, 128:129])
        ltv = lt[:].rearrange("p (j q) -> p j q", q=130)
        pr.v.tensor_tensor(xdf[:].rearrange("p (j q) -> p j q", q=128),
                           ltv[:, :, 1:129], ltv[:, :, 0:128], ALU.subtract)
        for j in range(NCH):
            pr.v.scalar_tensor_tensor(
                xrT[:, j * 128:(j + 1) * 128], xdf[:, j * 128:(j + 1) * 128],
                mur[:, j:j + 1], lt[:, j * 130: j * 130 + 128], ALU.mult, ALU.add)
            pr.v.scalar_tensor_tensor(
                xkT[:, j * 128:(j + 1) * 128], xdf[:, j * 128:(j + 1) * 128],
                muk[:, j:j + 1], lt[:, j * 130: j * 130 + 128], ALU.mult, ALU.add)
        for half in range(2):
            for j in range(NCH):
                pr.t.matmul(psF[:, half * 512:(half + 1) * 512],
                            xrT[:, j * 128:(j + 1) * 128],
                            fr_sb[:, j * D + half * 512: j * D + (half + 1) * 512],
                            start=(j == 0), stop=(j == NCH - 1))
            for j in range(NCH):
                pr.t.matmul(psK3[:, half * 512:(half + 1) * 512],
                            xkT[:, j * 128:(j + 1) * 128],
                            fk_sb[:, j * D + half * 512: j * D + (half + 1) * 512],
                            start=(j == 0), stop=(j == NCH - 1))
        pr.s.activation(r_sb[:], psF[:], AF.Sigmoid)
        pr.s.activation(kr[:], psK3[:], AF.Relu)
        for j in range(NCH):
            pr.t.transpose(psT3[:, j * 128:(j + 1) * 128],
                           kr[:, j * 128:(j + 1) * 128], ident[:])
        pr.s.activation(kaT[:], psT3[:], AF.Square)
        for half in range(2):
            for j in range(NCH):
                pr.t.matmul(psF[:, half * 512:(half + 1) * 512],
                            kaT[:, j * 128:(j + 1) * 128],
                            fv_sb[:, j * D + half * 512: j * D + (half + 1) * 512],
                            start=(j == 0), stop=(j == NCH - 1))
        pr.v.tensor_tensor(r_sb[:], r_sb[:], psF[:], ALU.mult)
        pr.v.tensor_tensor(x2sb[p][:], r_sb[:], x1sb[p][:], ALU.add)
        pr.y.dma_start(x2_h.ap()[t * TL:(t + 1) * TL, :], x2sb[p][:])


def emit_phase2(nc, pr, env, st):
    kres, vres, ident, onesf, jt, epsc = (env[k] for k in
        ["kres", "vres", "ident", "onesf", "jt", "epsc"])
    x_h, win, ctx_h = env["x_h"], env["win"], env["ctx_h"]
    rcph = env["rcph"]

    def sbt(name, shape, dtype):
        return st.enter_context(nc.sbuf_tensor(name, shape, dtype))

    def pst(name, shape, dtype=F32):
        return st.enter_context(nc.psum_tensor(name, shape, dtype))

    wq_sb = sbt("p2_wq", [128, NCH * D], BF16)
    bq_c = sbt("p2_bqc", [128, NCH], F32)
    xi0 = sbt("p2_x0", [128, D], F32)
    x_in = [xi0, xi0]
    stat = sbt("p2_stat", [128, 12], F32)
    mv = [sbt(f"p2_mv{i}", [128, 2], F32) for i in range(2)]
    col = [sbt(f"p2_col{i}", [128, 2], F32) for i in range(2)]
    ln_f = [sbt(f"p2_ln{i}", [128, D], F32) for i in range(2)]
    lnT = sbt("p2_lnT", [128, NCH * 512], BF16)
    qt = sbt("p2_qt", [128, NCH * 512], BF16)
    pt = [sbt(f"p2_pt{i}", [128, 2048], BF16) for i in range(2)]
    ctxe = [sbt(f"p2_ce{i}", [65, 512], F32) for i in range(2)]
    ctxs = sbt("p2_cs", [65, 512], F32)
    rcp = [sbt(f"p2_rcp{i}", [1, 512], F32) for i in range(2)]
    rcpb = [sbt(f"p2_rb{i}", [64, 512], F32) for i in range(2)]
    cnT = sbt("p2_cnT", [128, NCH * 512], BF16)
    psS = pst("p2_psS", [128, 2048])
    psC = [pst(f"p2_psC{i}", [65, 512]) for i in range(4)]

    G = NT // GT

    src = win["wq"].ap().rearrange("(j p) d -> p j d", p=128)
    pr.g.dma_start(wq_sb[:].rearrange("p (j d) -> p j d", d=D), src)
    pr.g.dma_start(bq_c[:], win["bq_c"].ap())

    for g in range(G):
        for u in range(GT):
            t = g * GT + u
            p = t % 2
            xb = x_in[p]
            pr.y.dma_start(xb[:], x_h.ap()[t * TL:(t + 1) * TL, :])
            pr.v.bn_stats(stat[:, 0:6], xb[:, 0:512])
            pr.v.bn_stats(stat[:, 6:12], xb[:, 512:1024])
            pr.v.bn_aggr(mv[p][:], stat[:])
            pr.s.activation(col[p][:, 0:1], mv[p][:, 1:2], AF.Sqrt, bias=epsc[:, 0:1])
            pr.v.reciprocal(col[p][:, 1:2], col[p][:, 0:1])
            pr.v.tensor_scalar(ln_f[p][:], xb[:], mv[p][:, 0:1], col[p][:, 1:2],
                               ALU.subtract, ALU.mult)
            for hf in range(2):
                for j in range(4):
                    jj = hf * 4 + j
                    pr.t.transpose(psS[:, j * 128:(j + 1) * 128],
                                   ln_f[p][:, jj * 128:(jj + 1) * 128], ident[:])
                pr.v.tensor_copy(
                    lnT[:].rearrange("p (j q) -> p j q", q=512)
                    [:, hf * 4:(hf + 1) * 4, u * 128:(u + 1) * 128],
                    psS[:, 0:512].rearrange("p (j q) -> p j q", q=128))
        for i in range(NCH):
            psQ = psS[:, 512:1024] if i % 2 == 0 else psS[:, 1536:2048]
            for j in range(NCH):
                pr.t.matmul(psQ, wq_sb[:, j * D + i * 128: j * D + (i + 1) * 128],
                            lnT[:, j * 512:(j + 1) * 512],
                            start=(j == 0), stop=(j == NCH - 1))
            pr.s.activation(qt[:, i * 512:(i + 1) * 512], psQ,
                            AF.Identity, bias=bq_c[:, i:i + 1])
        NB = NT // 2

        def s_mms(jj, b):
            for half in range(2):
                j = 2 * b + half
                pr.t.matmul(psS[:, half * 512:(half + 1) * 512],
                            kres[0:64, j * D + jj * 128: j * D + (jj + 1) * 128],
                            qt[0:64, jj * 512:(jj + 1) * 512],
                            tile_position=(0, 0))
                pr.t.matmul(psS[:, 1024 + half * 512:1536 + half * 512],
                            kres[64:128, j * D + jj * 128: j * D + (jj + 1) * 128],
                            qt[64:128, jj * 512:(jj + 1) * 512],
                            tile_position=(64, 0))

        def ctx_mms(jj, b):
            for ho in range(2):
                h = 2 * jj + ho
                for half in range(2):
                    j = 2 * b + half
                    pcol = ho * 1024 + half * 512
                    pr.t.matmul(psC[2 * ho][:],
                                vres[0:64, j * 1040 + h * 65: j * 1040 + (h + 1) * 65],
                                pt[b % 2][0:64, pcol:pcol + 512],
                                start=(j == 0), stop=(j == NT - 1),
                                tile_position=(0, 0))
                    pr.t.matmul(psC[2 * ho + 1][:],
                                vres[64:128, j * 1040 + h * 65: j * 1040 + (h + 1) * 65],
                                pt[b % 2][64:128, pcol:pcol + 512],
                                start=(j == 0), stop=(j == NT - 1),
                                tile_position=(64, 0))

        def ep_pre(jj):
            for ho in range(2):
                pr.v.tensor_copy(ctxs[:], psC[2 * ho + 1][:])
                pr.v.tensor_tensor(ctxe[ho][:], psC[2 * ho][:], ctxs[:], ALU.add)
                pr.v.reciprocal(rcp[ho][:], ctxe[ho][64:65, :])
                hap = rcph[ho].ap()[0:1, 0:512]
                pr.g.dma_start(hap, rcp[ho][:])
                bsrc = bass.broadcast_tensor_aps(hap, rcpb[ho][:])[0]
                pr.g.dma_start(rcpb[ho][:], bsrc)

        def ep_post(jj):
            for ho in range(2):
                pbase = 64 * ho
                pr.v.tensor_tensor(
                    cnT[pbase:pbase + 64, jj * 512:(jj + 1) * 512],
                    ctxe[ho][0:64, :], rcpb[ho][:], ALU.mult)

        for jj in range(NCH):
            for b in range(NB):
                s_mms(jj, b)
                pr.s.activation(pt[b % 2][:], psS[:], AF.Exp, scale=0.125)
                if b == 3 and jj > 0:
                    ep_post(jj - 1)
                if b >= 1:
                    ctx_mms(jj, b - 1)
            ctx_mms(jj, NB - 1)
            ep_pre(jj)
        ep_post(NCH - 1)
        for u in range(GT):
            t = g * GT + u
            pr.y.dma_start(
                ctx_h.ap()[t],
                cnT[:].rearrange("p (j q) -> p j q", q=512)[:, :, u * 128:(u + 1) * 128])


def emit_start(nc, pr, env):
    ident, onesf, jt, epsc = env["ident"], env["onesf"], env["jt"], env["epsc"]

    def s_clear(eng):
        for h_ in pr.all_sems():
            eng.sem_clear(h_)
    pr.ops["sync"].append((None, s_clear, "RAW"))

    def g_pre(eng):
        eng.memset(env["preg"][:], 0)
    pr.ops["gpsimd"].append((None, g_pre, "RAW"))

    def v_pre(eng):
        for i in range(16):
            eng.memset(env["prev"][:, i * 4:(i + 1) * 4], 0.0)
    pr.ops["vector"].append((None, v_pre, "RAW"))

    def c_pre(eng):
        for i in range(16):
            eng.memzero(env["pres"][:, i * 4:(i + 1) * 4])
    pr.ops["scalar"].append((None, c_pre, "RAW"))

    pr.g.iota(jt[:, 0:128], [[1, 128]], channel_multiplier=0,
              allow_small_or_imprecise_dtypes=True)
    pr.g.iota(jt[:, 128:256], [[0, 128]], channel_multiplier=1,
              allow_small_or_imprecise_dtypes=True)
    pr.v.tensor_tensor(ident[:], jt[:, 0:128], jt[:, 128:256], ALU.is_equal)
    pr.v.memset(onesf[:], 1.0)
    pr.v.memset(epsc[:], float(EPS))
    pr.v.memset(jt[:], 0.0)


def emit_phase1(nc, pr, env, st):
    kres, vres, ident, onesf, cm, jt, epsc = (env[k] for k in
        ["kres", "vres", "ident", "onesf", "cm", "jt", "epsc"])
    x_h, win, bi_h = env["x_h"], env["win"], env["bi_h"]

    def sbt(name, shape, dtype):
        return st.enter_context(nc.sbuf_tensor(name, shape, dtype))

    def pst(name, shape, dtype=F32):
        return st.enter_context(nc.psum_tensor(name, shape, dtype))

    wk_sb = sbt("p1_wk", [128, NCH * D], BF16)
    wv_sb = sbt("p1_wv", [128, NCH * D], BF16)
    bk_c = sbt("p1_bkc", [128, NCH], F32)
    bvb = sbt("p1_bvb", [128, D], F32)
    x_in = [sbt(f"p1_x{i}", [128, D], F32) for i in range(2)]
    stat = sbt("p1_stat", [128, 12], F32)
    mv = [sbt(f"p1_mv{i}", [128, 2], F32) for i in range(2)]
    col = [sbt(f"p1_col{i}", [128, 2], F32) for i in range(2)]
    ln_f = [sbt(f"p1_ln{i}", [128, D], F32) for i in range(2)]
    lnT = sbt("p1_lnT", [128, NCH * 512], BF16)
    bi_sb = sbt("p1_bi", [1, 2048], F32)
    row1 = sbt("p1_row", [1, D], F32)
    mx8 = sbt("p1_mx8", [1, 8], F32)
    psT = pst("p1_psT", [128, 1024])
    psK = [pst(f"p1_psK{i}", [128, 512]) for i in range(2)]
    psV = [pst(f"p1_psV{i}", [128, 512]) for i in range(2)]
    psB = pst("p1_psB", [128, 1024])

    G = NT // GT

    for w_sb, nm in ((wk_sb, "wk"), (wv_sb, "wv")):
        src = win[nm].ap().rearrange("(j p) d -> p j d", p=128)
        pr.g.dma_start(w_sb[:].rearrange("p (j d) -> p j d", d=D), src)
    pr.g.dma_start(bk_c[:], win["bk_c"].ap())
    pr.g.dma_start(bi_sb[:], bi_h.ap())
    pr.g.dma_start(row1[:], win["bv_r"].ap())

    pr.v.max(mx8[:], bi_sb[:])
    pr.v.tensor_scalar(bi_sb[:], bi_sb[:], mx8[0:1, 2:3], None, ALU.is_lt)
    for t in range(16):
        pr.t.transpose(psB[:, t:t + 1], bi_sb[0:1, t * 128:(t + 1) * 128],
                       ident[0:1, 0:1])
    pr.v.tensor_copy(cm[:], psB[:, 0:16])
    pr.t.matmul(psB[:, 0:512], onesf[:], row1[0:1, 0:512])
    pr.t.matmul(psB[:, 512:1024], onesf[:], row1[0:1, 512:1024])
    pr.v.tensor_copy(bvb[:], psB[:])
    pr.v.memset(vres[:].rearrange("p (b o) -> p b o", o=65)[:, :, 64:65], 1.0)

    for g in range(G):
        for u in range(GT):
            t = g * GT + u
            p = t % 2
            xb = x_in[p]
            pr.y.dma_start(xb[:], x_h.ap()[t * TL:(t + 1) * TL, :])
            pr.v.bn_stats(stat[:, 0:6], xb[:, 0:512])
            pr.v.bn_stats(stat[:, 6:12], xb[:, 512:1024])
            pr.v.bn_aggr(mv[p][:], stat[:])
            pr.s.activation(col[p][:, 0:1], mv[p][:, 1:2], AF.Sqrt, bias=epsc[:, 0:1])
            pr.v.reciprocal(col[p][:, 1:2], col[p][:, 0:1])
            pr.v.tensor_scalar(ln_f[p][:], xb[:], mv[p][:, 0:1], col[p][:, 1:2],
                               ALU.subtract, ALU.mult)
            for j in range(NCH):
                pr.t.transpose(psT[:, j * 128:(j + 1) * 128],
                               ln_f[p][:, j * 128:(j + 1) * 128], ident[:])
            pr.v.tensor_copy(
                lnT[:].rearrange("p (j q) -> p j q", q=512)[:, :, u * 128:(u + 1) * 128],
                psT[:].rearrange("p (j q) -> p j q", q=128))
        for i in range(NCH):
            pk = psK[i % 2]
            for j in range(NCH):
                pr.t.matmul(pk[:], wk_sb[:, j * D + i * 128: j * D + (i + 1) * 128],
                            lnT[:, j * 512:(j + 1) * 512],
                            start=(j == 0), stop=(j == NCH - 1))
            pr.s.activation(
                kres[:].rearrange("p (c j q) -> p c j q", j=NCH, q=128)
                [:, g * GT:(g + 1) * GT, i, :],
                pk[:].rearrange("p (u q) -> p u q", q=128),
                AF.Identity, bias=bk_c[:, i:i + 1])
        for u in range(GT):
            t = g * GT + u
            for half in range(2):
                pv = psV[half]
                for j in range(NCH):
                    pr.t.matmul(
                        pv[:],
                        lnT[:, j * 512 + u * 128: j * 512 + (u + 1) * 128],
                        wv_sb[:, j * D + half * 512: j * D + (half + 1) * 512],
                        start=(j == 0), stop=(j == NCH - 1))
            vblk = vres[:, t * 1040:(t + 1) * 1040]
            for half in range(2):
                pr.v.scalar_tensor_tensor(
                    vblk.rearrange("p (h o) -> p h o", o=65)[:, half * 8:(half + 1) * 8, 0:64],
                    psV[half][:].rearrange("p (h o) -> p h o", o=64),
                    1.0,
                    bvb[:, half * 512:(half + 1) * 512].rearrange("p (h o) -> p h o", o=64),
                    ALU.mult, ALU.add)
            if t < 16:
                pr.v.tensor_scalar(vblk, vblk, cm[:, t:t + 1], None, ALU.mult)



def stage_inputs(inputs):
    x = np.ascontiguousarray(np.asarray(inputs["L0_attn_a_x"], np.float32))
    tp = lambda a: np.ascontiguousarray(np.asarray(a, np.float32).T)
    chunked = lambda a: np.ascontiguousarray(np.asarray(a, np.float32).reshape(NCH, 128).T)
    row = lambda a: np.ascontiguousarray(np.asarray(a, np.float32)[None, :])

    common = {
        "wq_t": tp(inputs["L0_attn_a_Wq"]), "wk_t": tp(inputs["L0_attn_a_Wk"]),
        "wv_t": tp(inputs["L0_attn_a_Wv"]), "wo_t": tp(inputs["L0_attn_a_Wo"]),
        "bqw_t": tp(inputs["L0_attn_b_q_w"]), "bkw_t": tp(inputs["L0_attn_b_k_w"]),
        "bvw_t": tp(inputs["L0_attn_b_v_w"]), "bow_t": tp(inputs["L0_attn_b_o_w"]),
        "fr_t": tp(inputs["L0_ffn_W_r"]), "fk_t": tp(inputs["L0_ffn_W_k"]),
        "fv_t": tp(inputs["L0_ffn_W_v"]),
        "bq_c": chunked(inputs["L0_attn_a_bq"]), "bk_c": chunked(inputs["L0_attn_a_bk"]),
        "qb_c": chunked(inputs["L0_attn_b_q_b"]), "kb_c": chunked(inputs["L0_attn_b_k_b"]),
        "mur_c": chunked(inputs["L0_ffn_mu_r"]), "muk_c": chunked(inputs["L0_ffn_mu_k"]),
        "bv_r": row(inputs["L0_attn_a_bv"]), "bo_r": row(inputs["L0_attn_a_bo"]),
        "gam_r": row(inputs["L0_attn_a_gamma"]), "bet_r": row(inputs["L0_attn_a_beta"]),
        "vb_r": row(inputs["L0_attn_b_v_b"]), "ob_r": row(inputs["L0_attn_b_o_b"]),
        "bi": np.ascontiguousarray(np.asarray(inputs["L0_attn_a_band_importance"], np.float32)),
    }
    in_maps = []
    for c in range(8):
        m = dict(common)
        m["x0"] = np.ascontiguousarray(x[c])
        in_maps.append(m)
    return in_maps


def kernel(**inputs):
    in_maps = stage_inputs(inputs)
    nc = build_program(debug=0)
    res = run_bass_kernel_spmd(nc, in_maps, core_ids=list(range(8)))
    return np.stack([np.asarray(res.results[c]["out"], np.float32) for c in range(8)])


# version 16
# speedup vs baseline: 1.0964x; 1.0133x over previous
import sys
sys.path.insert(0, "/opt/trn_rl_repo")
import numpy as np
import concourse.bass as bass
import concourse.mybir as mybir
from concourse.bass_utils import run_bass_kernel_spmd



ENGS = ["tensor", "vector", "scalar", "gpsimd", "sync"]


def _names(args):
    out = []
    for a in args:
        if a is None or isinstance(a, (int, float)):
            continue
        t = getattr(a, "tensor", None)
        if t is not None:
            nm = getattr(t, "name", None)
            if nm is not None:
                if type(t).__name__ == "DRamTensorHandle":
                    nm = f"{nm}@{getattr(a, 'offset', 0)}"
                out.append(nm)
    return out


class _Facade:
    def __init__(self, prog, eng):
        self._p = prog
        self._e = eng

    def _st(self, reads, writes, inc, fn, dma=False):
        self._p.stage(self._e, fn, _names(reads), _names(writes), inc=inc, dma=dma)

    def dma_start(self, out, in_, **kw):
        self._st([in_], [out], True, lambda e: e.dma_start(out, in_, **kw), dma=True)

    def matmul(self, out, lhsT, rhs, start=True, stop=True, **kw):
        self._st([lhsT, rhs], [out], True,
                 lambda e: e.matmul(out, lhsT, rhs, start=start, stop=stop, **kw))

    def transpose(self, out, in_, ident, **kw):
        self._st([in_, ident], [out], True, lambda e: e.transpose(out, in_, ident, **kw))

    def activation(self, out, in_, func, bias=None, scale=None, accum_out=None):
        kw = {}
        if bias is not None:
            kw["bias"] = bias
        if scale is not None:
            kw["scale"] = scale
        if accum_out is not None:
            kw["accum_out"] = accum_out
        self._st([in_, bias, scale], [out, accum_out], True,
                 lambda e: e.activation(out, in_, func, **kw))

    def copy(self, out, in_):
        self._st([in_], [out], True, lambda e: e.copy(out, in_))

    def memzero(self, ap):
        self._st([], [ap], True, lambda e: e.memzero(ap))

    def tensor_copy(self, out, in_):
        self._st([in_], [out], True, lambda e: e.tensor_copy(out, in_))

    def memset(self, ap, c):
        self._st([], [ap], True, lambda e: e.memset(ap, c))

    def iota(self, ap, pattern, **kw):
        self._st([], [ap], True, lambda e: e.iota(ap, pattern, **kw))

    def tensor_tensor(self, out, in0, in1, op):
        self._st([in0, in1], [out], True, lambda e: e.tensor_tensor(out, in0, in1, op))

    def tensor_scalar(self, out, in0, s1, s2, op0, op1=None, accum_out=None):
        kw = dict(accum_out=accum_out) if accum_out is not None else {}
        if op1 is None:
            fn = lambda e: e.tensor_scalar(out, in0, s1, s2, op0, **kw)
        else:
            fn = lambda e: e.tensor_scalar(out, in0, s1, s2, op0, op1, **kw)
        self._st([in0, s1, s2], [out, accum_out], True, fn)

    def scalar_tensor_tensor(self, out, in0, scalar, in1, op0, op1):
        self._st([in0, scalar, in1], [out], True,
                 lambda e: e.scalar_tensor_tensor(out, in0, scalar, in1, op0, op1))

    def bn_stats(self, out, in_):
        self._st([in_], [out], True, lambda e: e.bn_stats(out, in_))

    def bn_aggr(self, out, in_):
        self._st([in_], [out], True, lambda e: e.bn_aggr(out, in_))

    def reciprocal(self, out, in_):
        self._st([in_], [out], True, lambda e: e.reciprocal(out, in_))

    def max(self, out, in_):
        self._st([in_], [out], True, lambda e: e.max(out, in_))

    def reduce_max(self, out, in_, axis, **kw):
        self._st([in_], [out], True, lambda e: e.reduce_max(out, in_, axis, **kw))

    def reduce_sum(self, out, in_, axis, **kw):
        self._st([in_], [out], True, lambda e: e.reduce_sum(out, in_, axis, **kw))

    def select(self, out, mask, a, b):
        self._st([mask, a, b], [out], True, lambda e: e.select(out, mask, a, b))

    def raw(self, fn, reads=(), writes=(), inc=True):
        self._st(list(reads), list(writes), inc, fn)


class Prog:
    def __init__(self, nc):
        self.nc = nc
        self.cnt_sem = {e: nc.alloc_semaphore(name=f"cnt_{e}") for e in ENGS}
        self.pD = nc.alloc_semaphore(name="pDr")
        self.cnt = {e: 0 for e in ENGS}
        self.NDQ = 8
        self.dma_sem = {e: [nc.alloc_semaphore(name=f"dq_{e}_{k}") for k in range(self.NDQ)]
                        for e in ("sync", "gpsimd", "scalar")}
        self.dma_cnt = {e: [0] * self.NDQ for e in ("sync", "gpsimd", "scalar")}
        self.dma_n = {e: 0 for e in ("sync", "gpsimd", "scalar")}
        self.pe_fifo = False
        self.last_w = {}
        self.readers = {}
        self.cur = {e: {} for e in ENGS}
        self.pD_n = 0
        self.ops = {e: [] for e in ENGS}
        for e in ENGS:
            setattr(self, {"tensor": "t", "vector": "v", "scalar": "s",
                           "gpsimd": "g", "sync": "y"}[e], _Facade(self, e))

    def all_sems(self):
        out = list(self.cnt_sem.values()) + [self.pD]
        for sems in self.dma_sem.values():
            out.extend(sems)
        return out

    def _sem_of(self, key):
        kind, ident = key
        if kind == "e":
            return self.cnt_sem[ident]
        e, k = ident
        return self.dma_sem[e][k]

    def _need(self, eng, waits, key, idx):
        kind, ident = key
        if kind == "e" and ident == eng:
            if idx > self.cnt[eng]:
                return
            if eng == "tensor" and self.pe_fifo:
                return
        cur = self.cur[eng].get(key, 0)
        if idx <= cur:
            return
        waits.append((key, idx))
        self.cur[eng][key] = idx

    def stage(self, eng, fn, rn, wn, inc=True, dma=False):
        waits = []
        for nm in rn:
            w = self.last_w.get(nm)
            if w is not None:
                self._need(eng, waits, (w[0], w[1]), w[2])
        for nm in wn:
            w = self.last_w.get(nm)
            if w is not None:
                self._need(eng, waits, (w[0], w[1]), w[2])
            for key, idx in self.readers.get(nm, {}).items():
                self._need(eng, waits, key, idx)
        if dma:
            k = self.dma_n[eng] % self.NDQ
            self.dma_n[eng] += 1
            if self.dma_cnt[eng][k]:
                self._need(eng, waits, ("dm", (eng, k)), self.dma_cnt[eng][k])
            self.dma_cnt[eng][k] += 16
            marker = ("dm", (eng, k), self.dma_cnt[eng][k])
            inc_spec = ("dm", (eng, k), 16)
        elif inc:
            self.cnt[eng] += 1
            marker = ("e", eng, self.cnt[eng])
            inc_spec = ("e", eng, 1)
        else:
            marker = ("e", eng, self.cnt[eng] + 1)
            inc_spec = None
        for nm in wn:
            self.last_w[nm] = marker
            self.readers[nm] = {}
        for nm in rn:
            self.readers.setdefault(nm, {})[(marker[0], marker[1])] = marker[2]
        self.ops[eng].append((waits, fn, inc_spec))

    def raw_stage(self, eng, fn):
        self.ops[eng].append(([], fn, None))

    def rendezvous(self):
        self.pD_n += len(ENGS)
        target = self.pD_n
        for e in ENGS:
            cnt = self.cnt[e]
            dts = []
            if e in self.dma_cnt:
                dts = [(self.dma_sem[e][k], c) for k, c in enumerate(self.dma_cnt[e]) if c]

            def f(eng, e=e, cnt=cnt, dts=dts, target=target):
                if cnt:
                    eng.wait_ge(self.cnt_sem[e], cnt)
                for s, c in dts:
                    eng.wait_ge(s, c)
                eng.sem_inc(self.pD, 1)
                eng.wait_ge(self.pD, target)
            self.ops[e].append((None, f, "RAW"))
        self.last_w.clear()
        self.readers.clear()
        for e in ENGS:
            self.cur[e] = {}

    def flush(self, block):
        for e in ENGS:
            ops = self.ops[e]
            if not ops:
                continue

            def section(eng, ops=ops, e=e):
                for waits, fn, inc_spec in ops:
                    if waits is None:
                        fn(eng)
                        continue
                    for key, val in (waits or ()):
                        eng.wait_ge(self._sem_of(key), val)
                    instr = fn(eng)
                    if inc_spec == "RAW":
                        continue
                    if inc_spec is not None:
                        k, ident, amt = inc_spec
                        sem = self._sem_of((k, ident))
                        instr.then_inc(sem, amt)
            getattr(block, e)(section)
        self.ops = {e: [] for e in ENGS}




F32 = mybir.dt.float32
BF16 = mybir.dt.bfloat16
AF = mybir.ActivationFunctionType
ALU = mybir.AluOpType

NT = 32
TL = 128
D = 1024
NCH = 8
H = 16
HD = 64
GT = 4
EPS = 1e-5

WEIGHT_NAMES = ["wq", "wk", "wv", "wo", "bqw", "bkw", "bvw", "bow", "fr", "fk", "fv"]


def build_program(debug=0):
    nc = bass.Bass("TRN2", target_bir_lowering=False)

    x_h = nc.dram_tensor("x0", [NT * TL, D], F32, kind="ExternalInput")
    win = {}
    for w in WEIGHT_NAMES:
        win[w] = nc.dram_tensor(w + "_t", [D, D], F32, kind="ExternalInput")
    for v in ["bq_c", "bk_c", "qb_c", "kb_c", "mur_c", "muk_c"]:
        win[v] = nc.dram_tensor(v, [128, NCH], F32, kind="ExternalInput")
    for v in ["bv_r", "bo_r", "gam_r", "bet_r", "vb_r", "ob_r"]:
        win[v] = nc.dram_tensor(v, [1, D], F32, kind="ExternalInput")
    bi_h = nc.dram_tensor("bi", [1, 2048], F32, kind="ExternalInput")

    out_h = nc.dram_tensor("out", [NT * TL, D], F32, kind="ExternalOutput")
    rcph = [nc.dram_tensor(f"rcph{i}", [1, 1024], F32, kind="Internal")
            for i in range(2)]
    dbg = "ExternalOutput" if debug else "Internal"
    ctx_h = nc.dram_tensor("ctxh", [NT, 128, NCH * 128], BF16, kind=dbg)
    x2_h = nc.dram_tensor("x2h", [NT * TL, D], F32, kind=dbg)
    x3_h = nc.dram_tensor("x3h", [NT * TL, D], F32, kind=dbg)
    kdbg_h = vdbg_h = None
    if debug:
        kdbg_h = nc.dram_tensor("kdbg", [NT, 128, D], BF16, kind="ExternalOutput")
        vdbg_h = nc.dram_tensor("vdbg", [NT, 128, 1040], BF16, kind="ExternalOutput")

    pr = Prog(nc)
    from contextlib import ExitStack
    with ExitStack() as top:
        def sb(name, shape, dtype):
            return top.enter_context(nc.sbuf_tensor(name, shape, dtype))

        ident = sb("ident", [128, 128], F32)
        onesf = sb("onesf", [1, 128], F32)
        cm = sb("cm", [128, 16], F32)
        epsc = sb("epsc", [128, 1], F32)
        jt = sb("jt", [128, 512], F32)
        preg = sb("preg", [128, 64], F32)
        prev = sb("prev", [128, 64], F32)
        pres = sb("pres", [128, 64], F32)
        env0 = dict(preg=preg, prev=prev, pres=pres)

        env = dict(x_h=x_h, win=win, bi_h=bi_h, out_h=out_h, ctx_h=ctx_h,
                   rcph=rcph,
                   x2_h=x2_h, x3_h=x3_h, kdbg_h=kdbg_h, vdbg_h=vdbg_h,
                   ident=ident, onesf=onesf, cm=cm, epsc=epsc, jt=jt, **env0)

        with nc.Block() as block:
            emit_start(nc, pr, env)

            with ExitStack() as p12:
                kres = p12.enter_context(nc.sbuf_tensor("kres", [128, NT * D], BF16))
                vres = p12.enter_context(nc.sbuf_tensor("vres", [128, NT * 1040], BF16))
                env["kres"], env["vres"] = kres, vres

                with ExitStack() as p1s:
                    pr.pe_fifo = True
                    emit_phase1(nc, pr, env, p1s)
                    pr.pe_fifo = False
                pr.rendezvous()

                if debug == 2:
                    for t in range(NT):
                        pr.y.dma_start(kdbg_h.ap()[t], kres[:, t * D:(t + 1) * D])
                        pr.y.dma_start(vdbg_h.ap()[t], vres[:, t * 1040:(t + 1) * 1040])
                    pr.rendezvous()
                    pr.flush(block)
                    return nc
                with ExitStack() as p2s:
                    pr.pe_fifo = True
                    emit_phase2(nc, pr, env, p2s)
                    pr.pe_fifo = False
                pr.rendezvous()
                if debug == 3:
                    pr.flush(block)
                    return nc
            with ExitStack() as p3s:
                pr.pe_fifo = True
                emit_phase3a(nc, pr, env, p3s)
                pr.pe_fifo = False
            pr.rendezvous()
            if debug == 4:
                pr.flush(block)
                return nc
            with ExitStack() as p3bs:
                pr.pe_fifo = True
                emit_phase3b(nc, pr, env, p3bs)
                pr.pe_fifo = False
            pr.rendezvous()
            if debug == 5:
                pr.flush(block)
                return nc
            with ExitStack() as p3cs:
                pr.pe_fifo = True
                emit_phase3c(nc, pr, env, p3cs)
                pr.pe_fifo = False
            pr.rendezvous()
            pr.flush(block)
    return nc


def ln_chain(pr, stat, mv, col, epsc, xb, out_f):
    pr.v.bn_stats(stat[:, 0:6], xb[:, 0:512])
    pr.v.bn_stats(stat[:, 6:12], xb[:, 512:1024])
    pr.v.bn_aggr(mv[:], stat[:])
    pr.s.activation(col[:, 0:1], mv[:, 1:2], AF.Sqrt, bias=epsc[:, 0:1])
    pr.v.reciprocal(col[:, 1:2], col[:, 0:1])
    pr.v.tensor_scalar(out_f[:], xb[:], mv[:, 0:1], col[:, 1:2],
                       ALU.subtract, ALU.mult)


def emit_phase3b(nc, pr, env, st):
    ident, onesf, jt, epsc = (env[k] for k in ["ident", "onesf", "jt", "epsc"])
    win, x2_h, x3_h = env["win"], env["x2_h"], env["x3_h"]
    rcph = env["rcph"]

    def sbt(name, shape, dtype):
        return st.enter_context(nc.sbuf_tensor(name, shape, dtype))

    def pst(name, shape, dtype=F32):
        return st.enter_context(nc.psum_tensor(name, shape, dtype))

    wq_sb = sbt("b_wq", [128, NCH * D], BF16)
    wk_sb = sbt("b_wk", [128, NCH * D], BF16)
    wv_sb = sbt("b_wv", [128, NCH * D], BF16)
    wo_sb = sbt("b_wo", [128, NCH * D], BF16)
    qb_c = sbt("b_qbc", [128, NCH], F32)
    kb_c = sbt("b_kbc", [128, NCH], F32)
    rows = sbt("b_rows", [1, 2 * D], F32)
    vbb = sbt("b_vbb", [128, D], F32)
    obb = sbt("b_obb", [128, D], F32)
    x2r = [sbt(f"b_x2_{i}", [128, D], F32) for i in range(3)]
    stat = sbt("b_stat", [128, 12], F32)
    mv = [sbt(f"b_mv{i}", [128, 2], F32) for i in range(2)]
    col = [sbt(f"b_col{i}", [128, 2], F32) for i in range(2)]
    ln_f = [sbt(f"b_ln{i}", [128, D], F32) for i in range(2)]
    lnT = [sbt(f"b_lnT{i}", [128, NCH * 128], BF16) for i in range(2)]
    qbT = [sbt(f"b_qbT{i}", [128, NCH * 128], BF16) for i in range(2)]
    kbT = [sbt(f"b_kbT{i}", [128, NCH * 128], BF16) for i in range(3)]
    kbZ = sbt("b_kbZ", [128, NCH * 128], BF16)
    vbr = [sbt(f"b_vb{i}", [128, H * 65], BF16) for i in range(3)]
    vbZ = sbt("b_vbZ", [128, H * 65], BF16)
    masks = [sbt(f"b_msk{i}", [128, NCH * 128], BF16) for i in range(3)]
    ptb = [sbt(f"b_pt{i}", [128, NCH * 128], BF16) for i in range(3)]
    ctxe = sbt("b_ce", [65, NCH * 128], F32)
    ctxs = sbt("b_cs", [65, NCH * 128], F32)
    rcpb3 = sbt("b_rb", [64, NCH * 128], F32)
    rcp = sbt("b_rcp", [1, NCH * 128], F32)
    cnT = [sbt(f"b_cnT{i}", [128, NCH * 128], BF16) for i in range(2)]
    x3sb = [sbt(f"b_x3_{i}", [128, D], F32) for i in range(2)]
    mt1 = sbt("b_mt1", [128, 128], F32)
    mt2 = sbt("b_mt2", [128, 128], F32)
    mt3 = sbt("b_mt3", [128, 128], F32)
    PA = pst("b_PA", [128, 1024])
    PB = pst("b_PB", [128, 1024])
    PC = pst("b_PC", [128, 1024])
    PD = pst("b_PD", [128, 1024])

    for w_sb, nm in ((wq_sb, "bqw"), (wk_sb, "bkw"), (wv_sb, "bvw"), (wo_sb, "bow")):
        src_ = win[nm].ap().rearrange("(j p) d -> p j d", p=128)
        pr.g.dma_start(w_sb[:].rearrange("p (j d) -> p j d", d=D), src_)
    pr.g.dma_start(qb_c[:], win["qb_c"].ap())
    pr.g.dma_start(kb_c[:], win["kb_c"].ap())
    pr.g.dma_start(rows[0:1, 0:D], win["vb_r"].ap())
    pr.g.dma_start(rows[0:1, D:2 * D], win["ob_r"].ap())
    for bt, off in ((vbb, 0), (obb, D)):
        for half in range(2):
            pr.t.matmul(PA[:, half * 512:(half + 1) * 512], onesf[:],
                        rows[0:1, off + half * 512: off + (half + 1) * 512])
        pr.v.tensor_copy(bt[:], PA[:])
    pr.g.iota(mt1[:], [[1, 128]], channel_multiplier=0,
              allow_small_or_imprecise_dtypes=True)
    pr.g.iota(mt2[:], [[0, 128]], channel_multiplier=1,
              allow_small_or_imprecise_dtypes=True)
    pr.v.tensor_tensor(mt2[:], mt2[:], mt1[:], ALU.subtract)
    for s in range(3):
        pr.v.tensor_scalar(mt1[:], mt2[:], float((s - 1) * 128), None, ALU.add)
        pr.v.tensor_scalar(mt3[:], mt1[:], 32.0, None, ALU.is_le)
        pr.v.tensor_scalar(mt1[:], mt1[:], -32.0, None, ALU.is_ge)
        pr.v.tensor_tensor(mt1[:], mt1[:], mt3[:], ALU.mult)
        for jj in range(NCH):
            pr.v.tensor_copy(masks[s][:, jj * 128:(jj + 1) * 128], mt1[:])
    pr.v.memset(kbZ[:], 0.0)
    pr.v.memset(vbZ[:], 0.0)
    pr.v.memset(vbZ[:].rearrange("p (h o) -> p h o", o=65)[:, :, 64:65], 1.0)
    for i in range(3):
        pr.v.memset(vbr[i][:].rearrange("p (h o) -> p h o", o=65)[:, :, 64:65], 1.0)

    for u in range(NT + 2):
        if u >= NT + 1:
            for half in range(2):
                for jj in range(NCH):
                    pr.t.matmul(PB[:, half * 512:(half + 1) * 512],
                                cnT[(NT - 1) % 2][:, jj * 128:(jj + 1) * 128],
                                wo_sb[:, jj * D + half * 512:
                                      jj * D + (half + 1) * 512],
                                start=(jj == 0), stop=(jj == NCH - 1))
            pr.v.scalar_tensor_tensor(x3sb[(NT - 1) % 2][:], PB[:], 1.0, obb[:],
                                      ALU.mult, ALU.add)
            pr.v.tensor_tensor(x3sb[(NT - 1) % 2][:], x3sb[(NT - 1) % 2][:],
                               x2r[(NT - 1) % 3][:], ALU.add)
            pr.y.dma_start(x3_h.ap()[(NT - 1) * TL:NT * TL, :],
                           x3sb[(NT - 1) % 2][:])
            continue
        if u < NT:
            p = u % 2
            xb = x2r[u % 3]
            pr.y.dma_start(xb[:], x2_h.ap()[u * TL:(u + 1) * TL, :])
            ln_chain(pr, stat, mv[p], col[p], epsc, xb, ln_f[p])
            for j in range(NCH):
                pr.t.transpose(PD[:, j * 128:(j + 1) * 128],
                               ln_f[p][:, j * 128:(j + 1) * 128], ident[:])
            pr.v.tensor_copy(lnT[p][:], PD[:])
            for wsb, bias, dst, PX in ((wq_sb, qb_c, qbT[p], PA),
                                       (wk_sb, kb_c, kbT[u % 3], PD)):
                for i in range(NCH):
                    for j in range(NCH):
                        pr.t.matmul(PX[:, i * 128:(i + 1) * 128],
                                    wsb[:, j * D + i * 128: j * D + (i + 1) * 128],
                                    lnT[p][:, j * 128:(j + 1) * 128],
                                    start=(j == 0), stop=(j == NCH - 1))
                for i in range(NCH):
                    pr.s.activation(dst[:, i * 128:(i + 1) * 128],
                                    PX[:, i * 128:(i + 1) * 128],
                                    AF.Identity, bias=bias[:, i:i + 1])
            for half in range(2):
                for j in range(NCH):
                    pr.t.matmul(PB[:, half * 512:(half + 1) * 512],
                                lnT[p][:, j * 128:(j + 1) * 128],
                                wv_sb[:, j * D + half * 512: j * D + (half + 1) * 512],
                                start=(j == 0), stop=(j == NCH - 1))
            for half in range(2):
                pr.v.scalar_tensor_tensor(
                    vbr[u % 3][:].rearrange("p (h o) -> p h o", o=65)
                    [:, half * 8:(half + 1) * 8, 0:64],
                    PB[:, half * 512:(half + 1) * 512]
                    .rearrange("p (h o) -> p h o", o=64),
                    1.0,
                    vbb[:, half * 512:(half + 1) * 512]
                    .rearrange("p (h o) -> p h o", o=64),
                    ALU.mult, ALU.add)
        pr.pe_fifo = False
        if u == 0:
            continue
        w = u - 1
        pw = w % 2

        def oproj(ww):
            for half in range(2):
                for jj in range(NCH):
                    pr.t.matmul(PB[:, half * 512:(half + 1) * 512],
                                cnT[ww % 2][:, jj * 128:(jj + 1) * 128],
                                wo_sb[:, jj * D + half * 512:
                                      jj * D + (half + 1) * 512],
                                start=(jj == 0), stop=(jj == NCH - 1))
            pr.v.scalar_tensor_tensor(x3sb[ww % 2][:], PB[:], 1.0, obb[:],
                                      ALU.mult, ALU.add)
            pr.v.tensor_tensor(x3sb[ww % 2][:], x3sb[ww % 2][:],
                               x2r[ww % 3][:], ALU.add)
            pr.y.dma_start(x3_h.ap()[ww * TL:(ww + 1) * TL, :], x3sb[ww % 2][:])

        if w >= 1:
            oproj(w - 1)
        kslots = [kbZ if w == 0 else kbT[(w - 1) % 3], kbT[w % 3],
                  kbZ if w == NT - 1 else kbT[(w + 1) % 3]]
        vslots = [vbZ if w == 0 else vbr[(w - 1) % 3], vbr[w % 3],
                  vbZ if w == NT - 1 else vbr[(w + 1) % 3]]
        PS = [PA, PB, PC]
        for half in range(2):
            for s in range(3):
                for hp in range(4):
                    jj = 4 * half + hp
                    pr.t.matmul(PS[s][:, hp * 128:(hp + 1) * 128],
                                kslots[s][0:64, jj * 128:(jj + 1) * 128],
                                qbT[pw][0:64, jj * 128:(jj + 1) * 128],
                                tile_position=(0, 0))
                    pr.t.matmul(PS[s][:, 512 + hp * 128:640 + hp * 128],
                                kslots[s][64:128, jj * 128:(jj + 1) * 128],
                                qbT[pw][64:128, jj * 128:(jj + 1) * 128],
                                tile_position=(64, 0))
            for s in range(3):
                pr.s.activation(ptb[s][:], PS[s][:], AF.Exp, scale=0.125)
                pr.v.tensor_tensor(ptb[s][:], ptb[s][:], masks[s][:], ALU.mult)
            for hh in range(8):
                h = half * 8 + hh
                ncol = (hh % 2) * 512 + (hh // 2) * 128
                for s in range(3):
                    pr.t.matmul(PD[0:65, hh * 128:(hh + 1) * 128],
                                vslots[s][0:64, h * 65:(h + 1) * 65],
                                ptb[s][0:64, ncol:ncol + 128],
                                start=(s == 0), stop=(s == 2),
                                tile_position=(0, 0))
                    pr.t.matmul(PA[0:65, hh * 128:(hh + 1) * 128],
                                vslots[s][64:128, h * 65:(h + 1) * 65],
                                ptb[s][64:128, ncol:ncol + 128],
                                start=(s == 0), stop=(s == 2),
                                tile_position=(64, 0))
            pr.v.tensor_copy(ctxs[:], PA[0:65, :])
            pr.v.tensor_tensor(ctxe[:], PD[0:65, :], ctxs[:], ALU.add)
            pr.v.reciprocal(rcp[:], ctxe[64:65, :])
            hap = rcph[w % 2].ap()
            pr.g.dma_start(hap, rcp[:])
            bsrc = bass.broadcast_tensor_aps(hap, rcpb3[:])[0]
            pr.g.dma_start(rcpb3[:], bsrc)
            for par in range(2):
                pr.v.tensor_tensor(
                    cnT[w % 2][par * 64:(par + 1) * 64, :]
                    .rearrange("p (m q) -> p m q", q=128)[:, 4 * half:4 * half + 4, :],
                    ctxe[0:64, :].rearrange("p (m q) -> p m q", q=256)
                    [:, :, par * 128:(par + 1) * 128],
                    rcpb3[:].rearrange("p (m q) -> p m q", q=256)
                    [:, :, par * 128:(par + 1) * 128],
                    ALU.mult)



def emit_phase3c(nc, pr, env, st):
    ident, onesf, jt, epsc = (env[k] for k in ["ident", "onesf", "jt", "epsc"])
    win, x3_h, out_h = env["win"], env["x3_h"], env["out_h"]

    def sbt(name, shape, dtype):
        return st.enter_context(nc.sbuf_tensor(name, shape, dtype))

    def pst(name, shape, dtype=F32):
        return st.enter_context(nc.psum_tensor(name, shape, dtype))

    fr_sb = sbt("c_fr", [128, NCH * D], BF16)
    fk_sb = sbt("c_fk", [128, NCH * D], BF16)
    fv_sb = sbt("c_fv", [128, NCH * D], BF16)
    mur = sbt("c_mur", [128, NCH], F32)
    muk = sbt("c_muk", [128, NCH], F32)
    x_in = [sbt(f"c_x{i}", [128, D], F32) for i in range(2)]
    stat = sbt("c_stat", [128, 12], F32)
    mv = [sbt(f"c_mv{i}", [128, 2], F32) for i in range(2)]
    col = [sbt(f"c_col{i}", [128, 2], F32) for i in range(2)]
    ln_f = [sbt(f"c_ln{i}", [128, D], F32) for i in range(2)]
    lnT3 = [sbt(f"c_lnT{i}", [128, NCH * 130], BF16) for i in range(2)]
    xdf = sbt("c_xdf", [128, NCH * 128], BF16)
    xrT = sbt("c_xrT", [128, NCH * 128], BF16)
    xkT = sbt("c_xkT", [128, NCH * 128], BF16)
    r_sb = sbt("c_r", [128, D], F32)
    kr = sbt("c_kr", [128, D], F32)
    kaT = sbt("c_kaT", [128, NCH * 128], BF16)
    osb = [sbt(f"c_o{i}", [128, D], F32) for i in range(2)]
    psF = pst("c_psF", [128, 1024])
    psK3 = pst("c_psK", [128, 1024])
    psT3 = pst("c_psT", [128, 1024])

    for w_sb, nm in ((fr_sb, "fr"), (fk_sb, "fk"), (fv_sb, "fv")):
        src_ = win[nm].ap().rearrange("(j p) d -> p j d", p=128)
        pr.g.dma_start(w_sb[:].rearrange("p (j d) -> p j d", d=D), src_)
    pr.g.dma_start(mur[:], win["mur_c"].ap())
    pr.g.dma_start(muk[:], win["muk_c"].ap())

    for t in range(NT):
        p = t % 2
        xb = x_in[p]
        pr.y.dma_start(xb[:], x3_h.ap()[t * TL:(t + 1) * TL, :])
        ln_chain(pr, stat, mv[p], col[p], epsc, xb, ln_f[p])
        for j in range(NCH):
            pr.t.transpose(psT3[:, j * 128:(j + 1) * 128],
                           ln_f[p][:, j * 128:(j + 1) * 128], ident[:])
        lt = lnT3[p]
        ltprev = lnT3[1 - p]
        pr.v.tensor_copy(
            lt[:].rearrange("p (j q) -> p j q", q=130)[:, :, 1:129],
            psT3[:].rearrange("p (j q) -> p j q", q=128))
        if t == 0:
            pr.v.memset(lt[:].rearrange("p (j q) -> p j q", q=130)[:, :, 0:1], 0.0)
        else:
            pr.v.tensor_copy(
                lt[:].rearrange("p (j q) -> p j q", q=130)[:, :, 0:1],
                ltprev[:].rearrange("p (j q) -> p j q", q=130)[:, :, 128:129])
        ltv = lt[:].rearrange("p (j q) -> p j q", q=130)
        pr.v.tensor_tensor(xdf[:].rearrange("p (j q) -> p j q", q=128),
                           ltv[:, :, 1:129], ltv[:, :, 0:128], ALU.subtract)
        for j in range(NCH):
            pr.v.scalar_tensor_tensor(
                xrT[:, j * 128:(j + 1) * 128], xdf[:, j * 128:(j + 1) * 128],
                mur[:, j:j + 1], lt[:, j * 130: j * 130 + 128], ALU.mult, ALU.add)
            pr.v.scalar_tensor_tensor(
                xkT[:, j * 128:(j + 1) * 128], xdf[:, j * 128:(j + 1) * 128],
                muk[:, j:j + 1], lt[:, j * 130: j * 130 + 128], ALU.mult, ALU.add)
        for half in range(2):
            for j in range(NCH):
                pr.t.matmul(psF[:, half * 512:(half + 1) * 512],
                            xrT[:, j * 128:(j + 1) * 128],
                            fr_sb[:, j * D + half * 512: j * D + (half + 1) * 512],
                            start=(j == 0), stop=(j == NCH - 1))
            for j in range(NCH):
                pr.t.matmul(psK3[:, half * 512:(half + 1) * 512],
                            xkT[:, j * 128:(j + 1) * 128],
                            fk_sb[:, j * D + half * 512: j * D + (half + 1) * 512],
                            start=(j == 0), stop=(j == NCH - 1))
        pr.s.activation(r_sb[:], psF[:], AF.Sigmoid)
        pr.s.activation(kr[:], psK3[:], AF.Relu)
        for j in range(NCH):
            pr.t.transpose(psT3[:, j * 128:(j + 1) * 128],
                           kr[:, j * 128:(j + 1) * 128], ident[:])
        pr.s.activation(kaT[:], psT3[:], AF.Square)
        for half in range(2):
            for j in range(NCH):
                pr.t.matmul(psF[:, half * 512:(half + 1) * 512],
                            kaT[:, j * 128:(j + 1) * 128],
                            fv_sb[:, j * D + half * 512: j * D + (half + 1) * 512],
                            start=(j == 0), stop=(j == NCH - 1))
        pr.v.tensor_tensor(r_sb[:], r_sb[:], psF[:], ALU.mult)
        pr.v.tensor_tensor(osb[p][:], r_sb[:], xb[:], ALU.add)
        pr.y.dma_start(env["out_h"].ap()[t * TL:(t + 1) * TL, :], osb[p][:])


def emit_phase3a(nc, pr, env, st):
    ident, onesf, jt, epsc = (env[k] for k in ["ident", "onesf", "jt", "epsc"])
    x_h, win, ctx_h, x2_h = env["x_h"], env["win"], env["ctx_h"], env["x2_h"]

    def sbt(name, shape, dtype):
        return st.enter_context(nc.sbuf_tensor(name, shape, dtype))

    def pst(name, shape, dtype=F32):
        return st.enter_context(nc.psum_tensor(name, shape, dtype))

    wo_sb = sbt("p3_wo", [128, NCH * D], BF16)
    fr_sb = sbt("p3_fr", [128, NCH * D], BF16)
    fk_sb = sbt("p3_fk", [128, NCH * D], BF16)
    fv_sb = sbt("p3_fv", [128, NCH * D], BF16)
    mur = sbt("p3_mur", [128, NCH], F32)
    muk = sbt("p3_muk", [128, NCH], F32)
    rows = sbt("p3_rows", [1, 3 * D], F32)
    bob = sbt("p3_bob", [128, D], F32)
    gab = sbt("p3_gab", [128, D], F32)
    beb = sbt("p3_beb", [128, D], F32)
    x_in = [sbt(f"p3_x{i}", [128, D], F32) for i in range(2)]
    ct_in = [sbt(f"p3_ct{i}", [128, NCH * 128], BF16) for i in range(2)]
    stat = sbt("p3_stat", [128, 12], F32)
    mv = [sbt(f"p3_mv{i}", [128, 2], F32) for i in range(2)]
    col = [sbt(f"p3_col{i}", [128, 2], F32) for i in range(2)]
    ln_f = [sbt(f"p3_ln{i}", [128, D], F32) for i in range(2)]
    ysb = sbt("p3_y", [128, D], F32)
    x1sb = [sbt(f"p3_x1_{i}", [128, D], F32) for i in range(2)]
    lnT3 = [sbt(f"p3_lnT{i}", [128, NCH * 130], BF16) for i in range(2)]
    xdf = sbt("p3_xdf", [128, NCH * 128], BF16)
    xrT = sbt("p3_xrT", [128, NCH * 128], BF16)
    xkT = sbt("p3_xkT", [128, NCH * 128], BF16)
    r_sb = sbt("p3_r", [128, D], F32)
    kr = sbt("p3_kr", [128, D], F32)
    kaT = sbt("p3_kaT", [128, NCH * 128], BF16)
    x2sb = [sbt(f"p3_x2_{i}", [128, D], F32) for i in range(2)]
    psO = pst("p3_psO", [128, 1024])
    psF = pst("p3_psF", [128, 1024])
    psK3 = pst("p3_psK", [128, 1024])
    psT3 = pst("p3_psT", [128, 1024])

    for w_sb, nm in ((wo_sb, "wo"), (fr_sb, "fr"), (fk_sb, "fk"), (fv_sb, "fv")):
        src_ = win[nm].ap().rearrange("(j p) d -> p j d", p=128)
        pr.g.dma_start(w_sb[:].rearrange("p (j d) -> p j d", d=D), src_)
    pr.g.dma_start(mur[:], win["mur_c"].ap())
    pr.g.dma_start(muk[:], win["muk_c"].ap())
    pr.g.dma_start(rows[0:1, 0:D], win["bo_r"].ap())
    pr.g.dma_start(rows[0:1, D:2 * D], win["gam_r"].ap())
    pr.g.dma_start(rows[0:1, 2 * D:3 * D], win["bet_r"].ap())
    for bt, off in ((bob, 0), (gab, D), (beb, 2 * D)):
        for half in range(2):
            pr.t.matmul(psO[:, half * 512:(half + 1) * 512], onesf[:],
                        rows[0:1, off + half * 512: off + (half + 1) * 512])
        pr.v.tensor_copy(bt[:], psO[:])

    def ln_chain(xb, p, out_f):
        pr.v.bn_stats(stat[:, 0:6], xb[:, 0:512])
        pr.v.bn_stats(stat[:, 6:12], xb[:, 512:1024])
        pr.v.bn_aggr(mv[p][:], stat[:])
        pr.s.activation(col[p][:, 0:1], mv[p][:, 1:2], AF.Sqrt, bias=epsc[:, 0:1])
        pr.v.reciprocal(col[p][:, 1:2], col[p][:, 0:1])
        pr.v.tensor_scalar(out_f[:], xb[:], mv[p][:, 0:1], col[p][:, 1:2],
                           ALU.subtract, ALU.mult)

    for t in range(NT):
        p = t % 2
        xb = x_in[p]
        ctb = ct_in[p]
        pr.y.dma_start(xb[:], x_h.ap()[t * TL:(t + 1) * TL, :])
        pr.y.dma_start(ctb[:], ctx_h.ap()[t])
        ln_chain(xb, p, ln_f[p])
        for half in range(2):
            for jj in range(NCH):
                pr.t.matmul(psO[:, half * 512:(half + 1) * 512],
                            ctb[:, jj * 128:(jj + 1) * 128],
                            wo_sb[:, jj * D + half * 512: jj * D + (half + 1) * 512],
                            start=(jj == 0), stop=(jj == NCH - 1))
        pr.v.scalar_tensor_tensor(ysb[:], psO[:], 1.0, bob[:], ALU.mult, ALU.add)
        pr.v.tensor_tensor(ysb[:], ysb[:], ln_f[p][:], ALU.add)
        ln_chain(ysb, p, ln_f[p])
        pr.v.tensor_tensor(ln_f[p][:], ln_f[p][:], gab[:], ALU.mult)
        pr.v.tensor_tensor(ln_f[p][:], ln_f[p][:], beb[:], ALU.add)
        pr.v.tensor_tensor(x1sb[p][:], ln_f[p][:], xb[:], ALU.add)
        ln_chain(x1sb[p], p, ln_f[p])
        for j in range(NCH):
            pr.t.transpose(psT3[:, j * 128:(j + 1) * 128],
                           ln_f[p][:, j * 128:(j + 1) * 128], ident[:])
        lt = lnT3[p]
        ltprev = lnT3[1 - p]
        pr.v.tensor_copy(
            lt[:].rearrange("p (j q) -> p j q", q=130)[:, :, 1:129],
            psT3[:].rearrange("p (j q) -> p j q", q=128))
        if t == 0:
            pr.v.memset(lt[:].rearrange("p (j q) -> p j q", q=130)[:, :, 0:1], 0.0)
        else:
            pr.v.tensor_copy(
                lt[:].rearrange("p (j q) -> p j q", q=130)[:, :, 0:1],
                ltprev[:].rearrange("p (j q) -> p j q", q=130)[:, :, 128:129])
        ltv = lt[:].rearrange("p (j q) -> p j q", q=130)
        pr.v.tensor_tensor(xdf[:].rearrange("p (j q) -> p j q", q=128),
                           ltv[:, :, 1:129], ltv[:, :, 0:128], ALU.subtract)
        for j in range(NCH):
            pr.v.scalar_tensor_tensor(
                xrT[:, j * 128:(j + 1) * 128], xdf[:, j * 128:(j + 1) * 128],
                mur[:, j:j + 1], lt[:, j * 130: j * 130 + 128], ALU.mult, ALU.add)
            pr.v.scalar_tensor_tensor(
                xkT[:, j * 128:(j + 1) * 128], xdf[:, j * 128:(j + 1) * 128],
                muk[:, j:j + 1], lt[:, j * 130: j * 130 + 128], ALU.mult, ALU.add)
        for half in range(2):
            for j in range(NCH):
                pr.t.matmul(psF[:, half * 512:(half + 1) * 512],
                            xrT[:, j * 128:(j + 1) * 128],
                            fr_sb[:, j * D + half * 512: j * D + (half + 1) * 512],
                            start=(j == 0), stop=(j == NCH - 1))
            for j in range(NCH):
                pr.t.matmul(psK3[:, half * 512:(half + 1) * 512],
                            xkT[:, j * 128:(j + 1) * 128],
                            fk_sb[:, j * D + half * 512: j * D + (half + 1) * 512],
                            start=(j == 0), stop=(j == NCH - 1))
        pr.s.activation(r_sb[:], psF[:], AF.Sigmoid)
        pr.s.activation(kr[:], psK3[:], AF.Relu)
        for j in range(NCH):
            pr.t.transpose(psT3[:, j * 128:(j + 1) * 128],
                           kr[:, j * 128:(j + 1) * 128], ident[:])
        pr.s.activation(kaT[:], psT3[:], AF.Square)
        for half in range(2):
            for j in range(NCH):
                pr.t.matmul(psF[:, half * 512:(half + 1) * 512],
                            kaT[:, j * 128:(j + 1) * 128],
                            fv_sb[:, j * D + half * 512: j * D + (half + 1) * 512],
                            start=(j == 0), stop=(j == NCH - 1))
        pr.v.tensor_tensor(r_sb[:], r_sb[:], psF[:], ALU.mult)
        pr.v.tensor_tensor(x2sb[p][:], r_sb[:], x1sb[p][:], ALU.add)
        pr.y.dma_start(x2_h.ap()[t * TL:(t + 1) * TL, :], x2sb[p][:])


def emit_phase2(nc, pr, env, st):
    kres, vres, ident, onesf, jt, epsc = (env[k] for k in
        ["kres", "vres", "ident", "onesf", "jt", "epsc"])
    x_h, win, ctx_h = env["x_h"], env["win"], env["ctx_h"]
    rcph = env["rcph"]

    def sbt(name, shape, dtype):
        return st.enter_context(nc.sbuf_tensor(name, shape, dtype))

    def pst(name, shape, dtype=F32):
        return st.enter_context(nc.psum_tensor(name, shape, dtype))

    wq_sb = sbt("p2_wq", [128, NCH * D], BF16)
    bq_c = sbt("p2_bqc", [128, NCH], F32)
    xi0 = sbt("p2_x0", [128, D], F32)
    x_in = [xi0, xi0]
    stat = sbt("p2_stat", [128, 12], F32)
    mv = [sbt(f"p2_mv{i}", [128, 2], F32) for i in range(2)]
    col = [sbt(f"p2_col{i}", [128, 2], F32) for i in range(2)]
    ln_f = [sbt(f"p2_ln{i}", [128, D], F32) for i in range(2)]
    lnT = sbt("p2_lnT", [128, NCH * 512], BF16)
    qt = sbt("p2_qt", [128, NCH * 512], BF16)
    pt = [sbt(f"p2_pt{i}", [128, 2048], BF16) for i in range(2)]
    ctxe = [sbt(f"p2_ce{i}", [65, 512], F32) for i in range(2)]
    ctxs = sbt("p2_cs", [65, 512], F32)
    rcp = [sbt(f"p2_rcp{i}", [1, 512], F32) for i in range(2)]
    rcpb = [sbt(f"p2_rb{i}", [64, 512], F32) for i in range(2)]
    cnT = sbt("p2_cnT", [128, NCH * 512], BF16)
    psS = pst("p2_psS", [128, 2048])
    psC = [pst(f"p2_psC{i}", [65, 512]) for i in range(4)]

    G = NT // GT

    src = win["wq"].ap().rearrange("(j p) d -> p j d", p=128)
    pr.g.dma_start(wq_sb[:].rearrange("p (j d) -> p j d", d=D), src)
    pr.g.dma_start(bq_c[:], win["bq_c"].ap())

    for g in range(G):
        for u in range(GT):
            t = g * GT + u
            p = t % 2
            xb = x_in[p]
            pr.y.dma_start(xb[:], x_h.ap()[t * TL:(t + 1) * TL, :])
            pr.v.bn_stats(stat[:, 0:6], xb[:, 0:512])
            pr.v.bn_stats(stat[:, 6:12], xb[:, 512:1024])
            pr.v.bn_aggr(mv[p][:], stat[:])
            pr.s.activation(col[p][:, 0:1], mv[p][:, 1:2], AF.Sqrt, bias=epsc[:, 0:1])
            pr.v.reciprocal(col[p][:, 1:2], col[p][:, 0:1])
            pr.v.tensor_scalar(ln_f[p][:], xb[:], mv[p][:, 0:1], col[p][:, 1:2],
                               ALU.subtract, ALU.mult)
            for hf in range(2):
                for j in range(4):
                    jj = hf * 4 + j
                    pr.t.transpose(psS[:, j * 128:(j + 1) * 128],
                                   ln_f[p][:, jj * 128:(jj + 1) * 128], ident[:])
                pr.v.tensor_copy(
                    lnT[:].rearrange("p (j q) -> p j q", q=512)
                    [:, hf * 4:(hf + 1) * 4, u * 128:(u + 1) * 128],
                    psS[:, 0:512].rearrange("p (j q) -> p j q", q=128))
        for i in range(NCH):
            psQ = psS[:, 512:1024] if i % 2 == 0 else psS[:, 1536:2048]
            for j in range(NCH):
                pr.t.matmul(psQ, wq_sb[:, j * D + i * 128: j * D + (i + 1) * 128],
                            lnT[:, j * 512:(j + 1) * 512],
                            start=(j == 0), stop=(j == NCH - 1))
            pr.s.activation(qt[:, i * 512:(i + 1) * 512], psQ,
                            AF.Identity, bias=bq_c[:, i:i + 1])
        NB = NT // 2

        def s_mms(jj, b):
            for half in range(2):
                j = 2 * b + half
                pr.t.matmul(psS[:, half * 512:(half + 1) * 512],
                            kres[0:64, j * D + jj * 128: j * D + (jj + 1) * 128],
                            qt[0:64, jj * 512:(jj + 1) * 512],
                            tile_position=(0, 0))
                pr.t.matmul(psS[:, 1024 + half * 512:1536 + half * 512],
                            kres[64:128, j * D + jj * 128: j * D + (jj + 1) * 128],
                            qt[64:128, jj * 512:(jj + 1) * 512],
                            tile_position=(64, 0))

        def ctx_mms(jj, b):
            for ho in range(2):
                h = 2 * jj + ho
                for half in range(2):
                    j = 2 * b + half
                    pcol = ho * 1024 + half * 512
                    pr.t.matmul(psC[2 * ho][:],
                                vres[0:64, j * 1040 + h * 65: j * 1040 + (h + 1) * 65],
                                pt[b % 2][0:64, pcol:pcol + 512],
                                start=(j == 0), stop=(j == NT - 1),
                                tile_position=(0, 0))
                    pr.t.matmul(psC[2 * ho + 1][:],
                                vres[64:128, j * 1040 + h * 65: j * 1040 + (h + 1) * 65],
                                pt[b % 2][64:128, pcol:pcol + 512],
                                start=(j == 0), stop=(j == NT - 1),
                                tile_position=(64, 0))

        def ep_pre(jj):
            for ho in range(2):
                pr.v.tensor_copy(ctxs[:], psC[2 * ho + 1][:])
                pr.v.tensor_tensor(ctxe[ho][:], psC[2 * ho][:], ctxs[:], ALU.add)
                pr.v.reciprocal(rcp[ho][:], ctxe[ho][64:65, :])
                hap = rcph[ho].ap()[0:1, 0:512]
                pr.g.dma_start(hap, rcp[ho][:])
                bsrc = bass.broadcast_tensor_aps(hap, rcpb[ho][:])[0]
                pr.g.dma_start(rcpb[ho][:], bsrc)

        def ep_post(jj):
            for ho in range(2):
                pbase = 64 * ho
                pr.v.tensor_tensor(
                    cnT[pbase:pbase + 64, jj * 512:(jj + 1) * 512],
                    ctxe[ho][0:64, :], rcpb[ho][:], ALU.mult)

        for jj in range(NCH):
            for b in range(NB):
                s_mms(jj, b)
                pr.s.activation(pt[b % 2][:], psS[:], AF.Exp, scale=0.125)
                if b == 3 and jj > 0:
                    ep_post(jj - 1)
                if b >= 1:
                    ctx_mms(jj, b - 1)
            ctx_mms(jj, NB - 1)
            ep_pre(jj)
        ep_post(NCH - 1)
        for u in range(GT):
            t = g * GT + u
            pr.y.dma_start(
                ctx_h.ap()[t],
                cnT[:].rearrange("p (j q) -> p j q", q=512)[:, :, u * 128:(u + 1) * 128])


def emit_start(nc, pr, env):
    ident, onesf, jt, epsc = env["ident"], env["onesf"], env["jt"], env["epsc"]

    def s_clear(eng):
        for h_ in pr.all_sems():
            eng.sem_clear(h_)
    pr.ops["sync"].append((None, s_clear, "RAW"))

    def g_pre(eng):
        eng.memset(env["preg"][:], 0)
    pr.ops["gpsimd"].append((None, g_pre, "RAW"))

    def v_pre(eng):
        for i in range(16):
            eng.memset(env["prev"][:, i * 4:(i + 1) * 4], 0.0)
    pr.ops["vector"].append((None, v_pre, "RAW"))

    def c_pre(eng):
        for i in range(16):
            eng.memzero(env["pres"][:, i * 4:(i + 1) * 4])
    pr.ops["scalar"].append((None, c_pre, "RAW"))

    pr.g.iota(jt[:, 0:128], [[1, 128]], channel_multiplier=0,
              allow_small_or_imprecise_dtypes=True)
    pr.g.iota(jt[:, 128:256], [[0, 128]], channel_multiplier=1,
              allow_small_or_imprecise_dtypes=True)
    pr.v.tensor_tensor(ident[:], jt[:, 0:128], jt[:, 128:256], ALU.is_equal)
    pr.v.memset(onesf[:], 1.0)
    pr.v.memset(epsc[:], float(EPS))
    pr.v.memset(jt[:], 0.0)


def emit_phase1(nc, pr, env, st):
    kres, vres, ident, onesf, cm, jt, epsc = (env[k] for k in
        ["kres", "vres", "ident", "onesf", "cm", "jt", "epsc"])
    x_h, win, bi_h = env["x_h"], env["win"], env["bi_h"]

    def sbt(name, shape, dtype):
        return st.enter_context(nc.sbuf_tensor(name, shape, dtype))

    def pst(name, shape, dtype=F32):
        return st.enter_context(nc.psum_tensor(name, shape, dtype))

    wk_sb = sbt("p1_wk", [128, NCH * D], BF16)
    wv_sb = sbt("p1_wv", [128, NCH * D], BF16)
    bk_c = sbt("p1_bkc", [128, NCH], F32)
    bvb = sbt("p1_bvb", [128, D], F32)
    x_in = [sbt(f"p1_x{i}", [128, D], F32) for i in range(2)]
    stat = sbt("p1_stat", [128, 12], F32)
    mv = [sbt(f"p1_mv{i}", [128, 2], F32) for i in range(2)]
    col = [sbt(f"p1_col{i}", [128, 2], F32) for i in range(2)]
    ln_f = [sbt(f"p1_ln{i}", [128, D], F32) for i in range(2)]
    lnT = sbt("p1_lnT", [128, NCH * 512], BF16)
    bi_sb = sbt("p1_bi", [1, 2048], F32)
    row1 = sbt("p1_row", [1, D], F32)
    mx8 = sbt("p1_mx8", [1, 8], F32)
    psT = pst("p1_psT", [128, 1024])
    psK = [pst(f"p1_psK{i}", [128, 512]) for i in range(2)]
    psV = [pst(f"p1_psV{i}", [128, 512]) for i in range(2)]
    psB = pst("p1_psB", [128, 1024])

    G = NT // GT

    for w_sb, nm in ((wk_sb, "wk"), (wv_sb, "wv")):
        src = win[nm].ap().rearrange("(j p) d -> p j d", p=128)
        pr.g.dma_start(w_sb[:].rearrange("p (j d) -> p j d", d=D), src)
    pr.g.dma_start(bk_c[:], win["bk_c"].ap())
    pr.g.dma_start(bi_sb[:], bi_h.ap())
    pr.g.dma_start(row1[:], win["bv_r"].ap())

    pr.v.max(mx8[:], bi_sb[:])
    pr.v.tensor_scalar(bi_sb[:], bi_sb[:], mx8[0:1, 2:3], None, ALU.is_lt)
    for t in range(16):
        pr.t.transpose(psB[:, t:t + 1], bi_sb[0:1, t * 128:(t + 1) * 128],
                       ident[0:1, 0:1])
    pr.v.tensor_copy(cm[:], psB[:, 0:16])
    pr.t.matmul(psB[:, 0:512], onesf[:], row1[0:1, 0:512])
    pr.t.matmul(psB[:, 512:1024], onesf[:], row1[0:1, 512:1024])
    pr.v.tensor_copy(bvb[:], psB[:])
    pr.v.memset(vres[:].rearrange("p (b o) -> p b o", o=65)[:, :, 64:65], 1.0)

    for g in range(G):
        for u in range(GT):
            t = g * GT + u
            p = t % 2
            xb = x_in[p]
            pr.y.dma_start(xb[:], x_h.ap()[t * TL:(t + 1) * TL, :])
            pr.v.bn_stats(stat[:, 0:6], xb[:, 0:512])
            pr.v.bn_stats(stat[:, 6:12], xb[:, 512:1024])
            pr.v.bn_aggr(mv[p][:], stat[:])
            pr.s.activation(col[p][:, 0:1], mv[p][:, 1:2], AF.Sqrt, bias=epsc[:, 0:1])
            pr.v.reciprocal(col[p][:, 1:2], col[p][:, 0:1])
            pr.v.tensor_scalar(ln_f[p][:], xb[:], mv[p][:, 0:1], col[p][:, 1:2],
                               ALU.subtract, ALU.mult)
            for j in range(NCH):
                pr.t.transpose(psT[:, j * 128:(j + 1) * 128],
                               ln_f[p][:, j * 128:(j + 1) * 128], ident[:])
            pr.v.tensor_copy(
                lnT[:].rearrange("p (j q) -> p j q", q=512)[:, :, u * 128:(u + 1) * 128],
                psT[:].rearrange("p (j q) -> p j q", q=128))
        for i in range(NCH):
            pk = psK[i % 2]
            for j in range(NCH):
                pr.t.matmul(pk[:], wk_sb[:, j * D + i * 128: j * D + (i + 1) * 128],
                            lnT[:, j * 512:(j + 1) * 512],
                            start=(j == 0), stop=(j == NCH - 1))
            pr.s.activation(
                kres[:].rearrange("p (c j q) -> p c j q", j=NCH, q=128)
                [:, g * GT:(g + 1) * GT, i, :],
                pk[:].rearrange("p (u q) -> p u q", q=128),
                AF.Identity, bias=bk_c[:, i:i + 1])
        for u in range(GT):
            t = g * GT + u
            for half in range(2):
                pv = psV[half]
                for j in range(NCH):
                    pr.t.matmul(
                        pv[:],
                        lnT[:, j * 512 + u * 128: j * 512 + (u + 1) * 128],
                        wv_sb[:, j * D + half * 512: j * D + (half + 1) * 512],
                        start=(j == 0), stop=(j == NCH - 1))
            vblk = vres[:, t * 1040:(t + 1) * 1040]
            for half in range(2):
                pr.v.scalar_tensor_tensor(
                    vblk.rearrange("p (h o) -> p h o", o=65)[:, half * 8:(half + 1) * 8, 0:64],
                    psV[half][:].rearrange("p (h o) -> p h o", o=64),
                    1.0,
                    bvb[:, half * 512:(half + 1) * 512].rearrange("p (h o) -> p h o", o=64),
                    ALU.mult, ALU.add)
            if t < 16:
                pr.v.tensor_scalar(vblk, vblk, cm[:, t:t + 1], None, ALU.mult)



def stage_inputs(inputs):
    x = np.ascontiguousarray(np.asarray(inputs["L0_attn_a_x"], np.float32))
    tp = lambda a: np.ascontiguousarray(np.asarray(a, np.float32).T)
    chunked = lambda a: np.ascontiguousarray(np.asarray(a, np.float32).reshape(NCH, 128).T)
    row = lambda a: np.ascontiguousarray(np.asarray(a, np.float32)[None, :])

    common = {
        "wq_t": tp(inputs["L0_attn_a_Wq"]), "wk_t": tp(inputs["L0_attn_a_Wk"]),
        "wv_t": tp(inputs["L0_attn_a_Wv"]), "wo_t": tp(inputs["L0_attn_a_Wo"]),
        "bqw_t": tp(inputs["L0_attn_b_q_w"]), "bkw_t": tp(inputs["L0_attn_b_k_w"]),
        "bvw_t": tp(inputs["L0_attn_b_v_w"]), "bow_t": tp(inputs["L0_attn_b_o_w"]),
        "fr_t": tp(inputs["L0_ffn_W_r"]), "fk_t": tp(inputs["L0_ffn_W_k"]),
        "fv_t": tp(inputs["L0_ffn_W_v"]),
        "bq_c": chunked(inputs["L0_attn_a_bq"]), "bk_c": chunked(inputs["L0_attn_a_bk"]),
        "qb_c": chunked(inputs["L0_attn_b_q_b"]), "kb_c": chunked(inputs["L0_attn_b_k_b"]),
        "mur_c": chunked(inputs["L0_ffn_mu_r"]), "muk_c": chunked(inputs["L0_ffn_mu_k"]),
        "bv_r": row(inputs["L0_attn_a_bv"]), "bo_r": row(inputs["L0_attn_a_bo"]),
        "gam_r": row(inputs["L0_attn_a_gamma"]), "bet_r": row(inputs["L0_attn_a_beta"]),
        "vb_r": row(inputs["L0_attn_b_v_b"]), "ob_r": row(inputs["L0_attn_b_o_b"]),
        "bi": np.ascontiguousarray(np.asarray(inputs["L0_attn_a_band_importance"], np.float32)),
    }
    in_maps = []
    for c in range(8):
        m = dict(common)
        m["x0"] = np.ascontiguousarray(x[c])
        in_maps.append(m)
    return in_maps


def kernel(**inputs):
    in_maps = stage_inputs(inputs)
    nc = build_program(debug=0)
    res = run_bass_kernel_spmd(nc, in_maps, core_ids=list(range(8)))
    return np.stack([np.asarray(res.results[c]["out"], np.float32) for c in range(8)])


# version 17 (final)
# speedup vs baseline: 1.0968x; 1.0003x over previous
import sys
sys.path.insert(0, "/opt/trn_rl_repo")
import numpy as np
import concourse.bass as bass
import concourse.mybir as mybir
from concourse.bass_utils import run_bass_kernel_spmd



ENGS = ["tensor", "vector", "scalar", "gpsimd", "sync"]


def _names(args):
    out = []
    for a in args:
        if a is None or isinstance(a, (int, float)):
            continue
        t = getattr(a, "tensor", None)
        if t is not None:
            nm = getattr(t, "name", None)
            if nm is not None:
                if type(t).__name__ == "DRamTensorHandle":
                    nm = f"{nm}@{getattr(a, 'offset', 0)}"
                out.append(nm)
    return out


class _Facade:
    def __init__(self, prog, eng):
        self._p = prog
        self._e = eng

    def _st(self, reads, writes, inc, fn, dma=False):
        self._p.stage(self._e, fn, _names(reads), _names(writes), inc=inc, dma=dma)

    def dma_start(self, out, in_, **kw):
        self._st([in_], [out], True, lambda e: e.dma_start(out, in_, **kw), dma=True)

    def matmul(self, out, lhsT, rhs, start=True, stop=True, **kw):
        self._st([lhsT, rhs], [out], True,
                 lambda e: e.matmul(out, lhsT, rhs, start=start, stop=stop, **kw))

    def transpose(self, out, in_, ident, **kw):
        self._st([in_, ident], [out], True, lambda e: e.transpose(out, in_, ident, **kw))

    def activation(self, out, in_, func, bias=None, scale=None, accum_out=None):
        kw = {}
        if bias is not None:
            kw["bias"] = bias
        if scale is not None:
            kw["scale"] = scale
        if accum_out is not None:
            kw["accum_out"] = accum_out
        self._st([in_, bias, scale], [out, accum_out], True,
                 lambda e: e.activation(out, in_, func, **kw))

    def copy(self, out, in_):
        self._st([in_], [out], True, lambda e: e.copy(out, in_))

    def memzero(self, ap):
        self._st([], [ap], True, lambda e: e.memzero(ap))

    def tensor_copy(self, out, in_):
        self._st([in_], [out], True, lambda e: e.tensor_copy(out, in_))

    def memset(self, ap, c):
        self._st([], [ap], True, lambda e: e.memset(ap, c))

    def iota(self, ap, pattern, **kw):
        self._st([], [ap], True, lambda e: e.iota(ap, pattern, **kw))

    def tensor_tensor(self, out, in0, in1, op):
        self._st([in0, in1], [out], True, lambda e: e.tensor_tensor(out, in0, in1, op))

    def tensor_scalar(self, out, in0, s1, s2, op0, op1=None, accum_out=None):
        kw = dict(accum_out=accum_out) if accum_out is not None else {}
        if op1 is None:
            fn = lambda e: e.tensor_scalar(out, in0, s1, s2, op0, **kw)
        else:
            fn = lambda e: e.tensor_scalar(out, in0, s1, s2, op0, op1, **kw)
        self._st([in0, s1, s2], [out, accum_out], True, fn)

    def scalar_tensor_tensor(self, out, in0, scalar, in1, op0, op1):
        self._st([in0, scalar, in1], [out], True,
                 lambda e: e.scalar_tensor_tensor(out, in0, scalar, in1, op0, op1))

    def bn_stats(self, out, in_):
        self._st([in_], [out], True, lambda e: e.bn_stats(out, in_))

    def bn_aggr(self, out, in_):
        self._st([in_], [out], True, lambda e: e.bn_aggr(out, in_))

    def reciprocal(self, out, in_):
        self._st([in_], [out], True, lambda e: e.reciprocal(out, in_))

    def max(self, out, in_):
        self._st([in_], [out], True, lambda e: e.max(out, in_))

    def reduce_max(self, out, in_, axis, **kw):
        self._st([in_], [out], True, lambda e: e.reduce_max(out, in_, axis, **kw))

    def reduce_sum(self, out, in_, axis, **kw):
        self._st([in_], [out], True, lambda e: e.reduce_sum(out, in_, axis, **kw))

    def select(self, out, mask, a, b):
        self._st([mask, a, b], [out], True, lambda e: e.select(out, mask, a, b))

    def raw(self, fn, reads=(), writes=(), inc=True):
        self._st(list(reads), list(writes), inc, fn)


class Prog:
    def __init__(self, nc):
        self.nc = nc
        self.cnt_sem = {e: nc.alloc_semaphore(name=f"cnt_{e}") for e in ENGS}
        self.pD = nc.alloc_semaphore(name="pDr")
        self.cnt = {e: 0 for e in ENGS}
        self.NDQ = 8
        self.dma_sem = {e: [nc.alloc_semaphore(name=f"dq_{e}_{k}") for k in range(self.NDQ)]
                        for e in ("sync", "gpsimd", "scalar")}
        self.dma_cnt = {e: [0] * self.NDQ for e in ("sync", "gpsimd", "scalar")}
        self.dma_n = {e: 0 for e in ("sync", "gpsimd", "scalar")}
        self.pe_fifo = False
        self.last_w = {}
        self.readers = {}
        self.cur = {e: {} for e in ENGS}
        self.pD_n = 0
        self.ops = {e: [] for e in ENGS}
        for e in ENGS:
            setattr(self, {"tensor": "t", "vector": "v", "scalar": "s",
                           "gpsimd": "g", "sync": "y"}[e], _Facade(self, e))

    def all_sems(self):
        out = list(self.cnt_sem.values()) + [self.pD]
        for sems in self.dma_sem.values():
            out.extend(sems)
        return out

    def _sem_of(self, key):
        kind, ident = key
        if kind == "e":
            return self.cnt_sem[ident]
        e, k = ident
        return self.dma_sem[e][k]

    def _need(self, eng, waits, key, idx):
        kind, ident = key
        if kind == "e" and ident == eng:
            if idx > self.cnt[eng]:
                return
            if eng == "tensor" and self.pe_fifo:
                return
        cur = self.cur[eng].get(key, 0)
        if idx <= cur:
            return
        waits.append((key, idx))
        self.cur[eng][key] = idx

    def stage(self, eng, fn, rn, wn, inc=True, dma=False):
        waits = []
        for nm in rn:
            w = self.last_w.get(nm)
            if w is not None:
                self._need(eng, waits, (w[0], w[1]), w[2])
        for nm in wn:
            w = self.last_w.get(nm)
            if w is not None:
                self._need(eng, waits, (w[0], w[1]), w[2])
            for key, idx in self.readers.get(nm, {}).items():
                self._need(eng, waits, key, idx)
        if dma:
            k = self.dma_n[eng] % self.NDQ
            self.dma_n[eng] += 1
            if self.dma_cnt[eng][k]:
                self._need(eng, waits, ("dm", (eng, k)), self.dma_cnt[eng][k])
            self.dma_cnt[eng][k] += 16
            marker = ("dm", (eng, k), self.dma_cnt[eng][k])
            inc_spec = ("dm", (eng, k), 16)
        elif inc:
            self.cnt[eng] += 1
            marker = ("e", eng, self.cnt[eng])
            inc_spec = ("e", eng, 1)
        else:
            marker = ("e", eng, self.cnt[eng] + 1)
            inc_spec = None
        for nm in wn:
            self.last_w[nm] = marker
            self.readers[nm] = {}
        for nm in rn:
            self.readers.setdefault(nm, {})[(marker[0], marker[1])] = marker[2]
        self.ops[eng].append((waits, fn, inc_spec))

    def raw_stage(self, eng, fn):
        self.ops[eng].append(([], fn, None))

    def rendezvous(self):
        self.pD_n += len(ENGS)
        target = self.pD_n
        for e in ENGS:
            cnt = self.cnt[e]
            dts = []
            if e in self.dma_cnt:
                dts = [(self.dma_sem[e][k], c) for k, c in enumerate(self.dma_cnt[e]) if c]

            def f(eng, e=e, cnt=cnt, dts=dts, target=target):
                if cnt:
                    eng.wait_ge(self.cnt_sem[e], cnt)
                for s, c in dts:
                    eng.wait_ge(s, c)
                eng.sem_inc(self.pD, 1)
                eng.wait_ge(self.pD, target)
            self.ops[e].append((None, f, "RAW"))
        self.last_w.clear()
        self.readers.clear()
        for e in ENGS:
            self.cur[e] = {}

    def flush(self, block):
        for e in ENGS:
            ops = self.ops[e]
            if not ops:
                continue

            def section(eng, ops=ops, e=e):
                for waits, fn, inc_spec in ops:
                    if waits is None:
                        fn(eng)
                        continue
                    for key, val in (waits or ()):
                        eng.wait_ge(self._sem_of(key), val)
                    instr = fn(eng)
                    if inc_spec == "RAW":
                        continue
                    if inc_spec is not None:
                        k, ident, amt = inc_spec
                        sem = self._sem_of((k, ident))
                        instr.then_inc(sem, amt)
            getattr(block, e)(section)
        self.ops = {e: [] for e in ENGS}




F32 = mybir.dt.float32
BF16 = mybir.dt.bfloat16
AF = mybir.ActivationFunctionType
ALU = mybir.AluOpType

NT = 32
TL = 128
D = 1024
NCH = 8
H = 16
HD = 64
GT = 4
EPS = 1e-5

WEIGHT_NAMES = ["wq", "wk", "wv", "wo", "bqw", "bkw", "bvw", "bow", "fr", "fk", "fv"]


def build_program(debug=0):
    nc = bass.Bass("TRN2", target_bir_lowering=False)

    x_h = nc.dram_tensor("x0", [NT * TL, D], F32, kind="ExternalInput")
    win = {}
    for w in WEIGHT_NAMES:
        win[w] = nc.dram_tensor(w + "_t", [D, D], F32, kind="ExternalInput")
    for v in ["bq_c", "bk_c", "qb_c", "kb_c", "mur_c", "muk_c"]:
        win[v] = nc.dram_tensor(v, [128, NCH], F32, kind="ExternalInput")
    for v in ["bv_r", "bo_r", "gam_r", "bet_r", "vb_r", "ob_r"]:
        win[v] = nc.dram_tensor(v, [1, D], F32, kind="ExternalInput")
    bi_h = nc.dram_tensor("bi", [1, 2048], F32, kind="ExternalInput")

    out_h = nc.dram_tensor("out", [NT * TL, D], F32, kind="ExternalOutput")
    rcph = [nc.dram_tensor(f"rcph{i}", [1, 1024], F32, kind="Internal")
            for i in range(2)]
    dbg = "ExternalOutput" if debug else "Internal"
    ctx_h = nc.dram_tensor("ctxh", [NT, 128, NCH * 128], BF16, kind=dbg)
    x2_h = nc.dram_tensor("x2h", [NT * TL, D], F32, kind=dbg)
    x3_h = nc.dram_tensor("x3h", [NT * TL, D], F32, kind=dbg)
    kdbg_h = vdbg_h = None
    if debug:
        kdbg_h = nc.dram_tensor("kdbg", [NT, 128, D], BF16, kind="ExternalOutput")
        vdbg_h = nc.dram_tensor("vdbg", [NT, 128, 1040], BF16, kind="ExternalOutput")

    pr = Prog(nc)
    from contextlib import ExitStack
    with ExitStack() as top:
        def sb(name, shape, dtype):
            return top.enter_context(nc.sbuf_tensor(name, shape, dtype))

        ident = sb("ident", [128, 128], F32)
        onesf = sb("onesf", [1, 128], F32)
        cm = sb("cm", [128, 16], F32)
        epsc = sb("epsc", [128, 1], F32)
        jt = sb("jt", [128, 512], F32)
        preg = sb("preg", [128, 64], F32)
        prev = sb("prev", [128, 64], F32)
        pres = sb("pres", [128, 64], F32)
        env0 = dict(preg=preg, prev=prev, pres=pres)

        env = dict(x_h=x_h, win=win, bi_h=bi_h, out_h=out_h, ctx_h=ctx_h,
                   rcph=rcph,
                   x2_h=x2_h, x3_h=x3_h, kdbg_h=kdbg_h, vdbg_h=vdbg_h,
                   ident=ident, onesf=onesf, cm=cm, epsc=epsc, jt=jt, **env0)

        with nc.Block() as block:
            emit_start(nc, pr, env)

            with ExitStack() as p12:
                kres = p12.enter_context(nc.sbuf_tensor("kres", [128, NT * D], BF16))
                vres = p12.enter_context(nc.sbuf_tensor("vres", [128, NT * 1040], BF16))
                env["kres"], env["vres"] = kres, vres

                with ExitStack() as p1s:
                    pr.pe_fifo = True
                    emit_phase1(nc, pr, env, p1s)
                    pr.pe_fifo = False
                pr.rendezvous()

                if debug == 2:
                    for t in range(NT):
                        pr.y.dma_start(kdbg_h.ap()[t], kres[:, t * D:(t + 1) * D])
                        pr.y.dma_start(vdbg_h.ap()[t], vres[:, t * 1040:(t + 1) * 1040])
                    pr.rendezvous()
                    pr.flush(block)
                    return nc
                with ExitStack() as p2s:
                    pr.pe_fifo = True
                    emit_phase2(nc, pr, env, p2s)
                    pr.pe_fifo = False
                pr.rendezvous()
                if debug == 3:
                    pr.flush(block)
                    return nc
            with ExitStack() as p3s:
                pr.pe_fifo = True
                emit_phase3a(nc, pr, env, p3s)
                pr.pe_fifo = False
            pr.rendezvous()
            if debug == 4:
                pr.flush(block)
                return nc
            with ExitStack() as p3bs:
                pr.pe_fifo = True
                emit_phase3b(nc, pr, env, p3bs)
                pr.pe_fifo = False
            pr.rendezvous()
            if debug == 5:
                pr.flush(block)
                return nc
            with ExitStack() as p3cs:
                pr.pe_fifo = True
                emit_phase3c(nc, pr, env, p3cs)
                pr.pe_fifo = False
            pr.rendezvous()
            pr.flush(block)
    return nc


def ln_chain(pr, stat, mv, col, epsc, xb, out_f):
    pr.v.bn_stats(stat[:, 0:6], xb[:, 0:512])
    pr.v.bn_stats(stat[:, 6:12], xb[:, 512:1024])
    pr.v.bn_aggr(mv[:], stat[:])
    pr.s.activation(col[:, 0:1], mv[:, 1:2], AF.Sqrt, bias=epsc[:, 0:1])
    pr.v.reciprocal(col[:, 1:2], col[:, 0:1])
    pr.v.tensor_scalar(out_f[:], xb[:], mv[:, 0:1], col[:, 1:2],
                       ALU.subtract, ALU.mult)


def emit_phase3b(nc, pr, env, st):
    ident, onesf, jt, epsc = (env[k] for k in ["ident", "onesf", "jt", "epsc"])
    win, x2_h, x3_h = env["win"], env["x2_h"], env["x3_h"]
    rcph = env["rcph"]

    def sbt(name, shape, dtype):
        return st.enter_context(nc.sbuf_tensor(name, shape, dtype))

    def pst(name, shape, dtype=F32):
        return st.enter_context(nc.psum_tensor(name, shape, dtype))

    wq_sb = sbt("b_wq", [128, NCH * D], BF16)
    wk_sb = sbt("b_wk", [128, NCH * D], BF16)
    wv_sb = sbt("b_wv", [128, NCH * D], BF16)
    wo_sb = sbt("b_wo", [128, NCH * D], BF16)
    qb_c = sbt("b_qbc", [128, NCH], F32)
    kb_c = sbt("b_kbc", [128, NCH], F32)
    rows = sbt("b_rows", [1, 2 * D], F32)
    vbb = sbt("b_vbb", [128, D], F32)
    obb = sbt("b_obb", [128, D], F32)
    x2r = [sbt(f"b_x2_{i}", [128, D], F32) for i in range(3)]
    stat = sbt("b_stat", [128, 12], F32)
    mv = [sbt(f"b_mv{i}", [128, 2], F32) for i in range(2)]
    col = [sbt(f"b_col{i}", [128, 2], F32) for i in range(2)]
    ln_f = [sbt(f"b_ln{i}", [128, D], F32) for i in range(2)]
    lnT = [sbt(f"b_lnT{i}", [128, NCH * 128], BF16) for i in range(2)]
    qbT = [sbt(f"b_qbT{i}", [128, NCH * 128], BF16) for i in range(2)]
    kbT = [sbt(f"b_kbT{i}", [128, NCH * 128], BF16) for i in range(3)]
    kbZ = sbt("b_kbZ", [128, NCH * 128], BF16)
    vbr = [sbt(f"b_vb{i}", [128, H * 65], BF16) for i in range(3)]
    vbZ = sbt("b_vbZ", [128, H * 65], BF16)
    masks = [sbt(f"b_msk{i}", [128, NCH * 128], BF16) for i in range(3)]
    ptb = [sbt(f"b_pt{i}", [128, NCH * 128], BF16) for i in range(3)]
    ctxe = sbt("b_ce", [65, NCH * 128], F32)
    ctxs = sbt("b_cs", [65, NCH * 128], F32)
    rcpb3 = sbt("b_rb", [64, NCH * 128], F32)
    rcp = sbt("b_rcp", [1, NCH * 128], F32)
    cnT = [sbt(f"b_cnT{i}", [128, NCH * 128], BF16) for i in range(2)]
    x3sb = [sbt(f"b_x3_{i}", [128, D], F32) for i in range(2)]
    mt1 = sbt("b_mt1", [128, 128], F32)
    mt2 = sbt("b_mt2", [128, 128], F32)
    mt3 = sbt("b_mt3", [128, 128], F32)
    PA = pst("b_PA", [128, 1024])
    PB = pst("b_PB", [128, 1024])
    PC = pst("b_PC", [128, 1024])
    PD = pst("b_PD", [128, 1024])

    for w_sb, nm in ((wq_sb, "bqw"), (wk_sb, "bkw"), (wv_sb, "bvw"), (wo_sb, "bow")):
        src_ = win[nm].ap().rearrange("(j p) d -> p j d", p=128)
        pr.g.dma_start(w_sb[:].rearrange("p (j d) -> p j d", d=D), src_)
    pr.g.dma_start(qb_c[:], win["qb_c"].ap())
    pr.g.dma_start(kb_c[:], win["kb_c"].ap())
    pr.g.dma_start(rows[0:1, 0:D], win["vb_r"].ap())
    pr.g.dma_start(rows[0:1, D:2 * D], win["ob_r"].ap())
    for bt, off in ((vbb, 0), (obb, D)):
        for half in range(2):
            pr.t.matmul(PA[:, half * 512:(half + 1) * 512], onesf[:],
                        rows[0:1, off + half * 512: off + (half + 1) * 512])
        pr.v.tensor_copy(bt[:], PA[:])
    pr.g.iota(mt1[:], [[1, 128]], channel_multiplier=0,
              allow_small_or_imprecise_dtypes=True)
    pr.g.iota(mt2[:], [[0, 128]], channel_multiplier=1,
              allow_small_or_imprecise_dtypes=True)
    pr.v.tensor_tensor(mt2[:], mt2[:], mt1[:], ALU.subtract)
    for s in range(3):
        pr.v.tensor_scalar(mt1[:], mt2[:], float((s - 1) * 128), None, ALU.add)
        pr.v.tensor_scalar(mt3[:], mt1[:], 32.0, None, ALU.is_le)
        pr.v.tensor_scalar(mt1[:], mt1[:], -32.0, None, ALU.is_ge)
        pr.v.tensor_tensor(mt1[:], mt1[:], mt3[:], ALU.mult)
        for jj in range(NCH):
            pr.v.tensor_copy(masks[s][:, jj * 128:(jj + 1) * 128], mt1[:])
    pr.v.memset(kbZ[:], 0.0)
    pr.v.memset(vbZ[:], 0.0)
    pr.v.memset(vbZ[:].rearrange("p (h o) -> p h o", o=65)[:, :, 64:65], 1.0)
    for i in range(3):
        pr.v.memset(vbr[i][:].rearrange("p (h o) -> p h o", o=65)[:, :, 64:65], 1.0)

    def ep_fin(ww, half):
        pr.v.reciprocal(rcp[:], ctxe[64:65, :])
        hap = rcph[ww % 2].ap()
        pr.g.dma_start(hap, rcp[:])
        bsrc = bass.broadcast_tensor_aps(hap, rcpb3[:])[0]
        pr.g.dma_start(rcpb3[:], bsrc)
        for par in range(2):
            pr.v.tensor_tensor(
                cnT[ww % 2][par * 64:(par + 1) * 64, :]
                .rearrange("p (m q) -> p m q", q=128)[:, 4 * half:4 * half + 4, :],
                ctxe[0:64, :].rearrange("p (m q) -> p m q", q=256)
                [:, :, par * 128:(par + 1) * 128],
                rcpb3[:].rearrange("p (m q) -> p m q", q=256)
                [:, :, par * 128:(par + 1) * 128],
                ALU.mult)

    for u in range(NT + 2):
        if u >= NT + 1:
            ep_fin(NT - 1, 1)
            for half in range(2):
                for jj in range(NCH):
                    pr.t.matmul(PB[:, half * 512:(half + 1) * 512],
                                cnT[(NT - 1) % 2][:, jj * 128:(jj + 1) * 128],
                                wo_sb[:, jj * D + half * 512:
                                      jj * D + (half + 1) * 512],
                                start=(jj == 0), stop=(jj == NCH - 1))
            pr.v.scalar_tensor_tensor(x3sb[(NT - 1) % 2][:], PB[:], 1.0, obb[:],
                                      ALU.mult, ALU.add)
            pr.v.tensor_tensor(x3sb[(NT - 1) % 2][:], x3sb[(NT - 1) % 2][:],
                               x2r[(NT - 1) % 3][:], ALU.add)
            pr.y.dma_start(x3_h.ap()[(NT - 1) * TL:NT * TL, :],
                           x3sb[(NT - 1) % 2][:])
            continue
        if u < NT:
            p = u % 2
            xb = x2r[u % 3]
            pr.y.dma_start(xb[:], x2_h.ap()[u * TL:(u + 1) * TL, :])
            ln_chain(pr, stat, mv[p], col[p], epsc, xb, ln_f[p])
            for j in range(NCH):
                pr.t.transpose(PD[:, j * 128:(j + 1) * 128],
                               ln_f[p][:, j * 128:(j + 1) * 128], ident[:])
            pr.v.tensor_copy(lnT[p][:], PD[:])
            for wsb, bias, dst, PX in ((wq_sb, qb_c, qbT[p], PA),
                                       (wk_sb, kb_c, kbT[u % 3], PD)):
                for i in range(NCH):
                    for j in range(NCH):
                        pr.t.matmul(PX[:, i * 128:(i + 1) * 128],
                                    wsb[:, j * D + i * 128: j * D + (i + 1) * 128],
                                    lnT[p][:, j * 128:(j + 1) * 128],
                                    start=(j == 0), stop=(j == NCH - 1))
                for i in range(NCH):
                    pr.s.activation(dst[:, i * 128:(i + 1) * 128],
                                    PX[:, i * 128:(i + 1) * 128],
                                    AF.Identity, bias=bias[:, i:i + 1])
            for half in range(2):
                for j in range(NCH):
                    pr.t.matmul(PB[:, half * 512:(half + 1) * 512],
                                lnT[p][:, j * 128:(j + 1) * 128],
                                wv_sb[:, j * D + half * 512: j * D + (half + 1) * 512],
                                start=(j == 0), stop=(j == NCH - 1))
            for half in range(2):
                pr.v.scalar_tensor_tensor(
                    vbr[u % 3][:].rearrange("p (h o) -> p h o", o=65)
                    [:, half * 8:(half + 1) * 8, 0:64],
                    PB[:, half * 512:(half + 1) * 512]
                    .rearrange("p (h o) -> p h o", o=64),
                    1.0,
                    vbb[:, half * 512:(half + 1) * 512]
                    .rearrange("p (h o) -> p h o", o=64),
                    ALU.mult, ALU.add)
        pr.pe_fifo = False
        if u == 0:
            continue
        w = u - 1
        pw = w % 2

        def oproj(ww):
            for half in range(2):
                for jj in range(NCH):
                    pr.t.matmul(PB[:, half * 512:(half + 1) * 512],
                                cnT[ww % 2][:, jj * 128:(jj + 1) * 128],
                                wo_sb[:, jj * D + half * 512:
                                      jj * D + (half + 1) * 512],
                                start=(jj == 0), stop=(jj == NCH - 1))
            pr.v.scalar_tensor_tensor(x3sb[ww % 2][:], PB[:], 1.0, obb[:],
                                      ALU.mult, ALU.add)
            pr.v.tensor_tensor(x3sb[ww % 2][:], x3sb[ww % 2][:],
                               x2r[ww % 3][:], ALU.add)
            pr.y.dma_start(x3_h.ap()[ww * TL:(ww + 1) * TL, :], x3sb[ww % 2][:])

        if w >= 1:
            ep_fin(w - 1, 1)
            oproj(w - 1)
        kslots = [kbZ if w == 0 else kbT[(w - 1) % 3], kbT[w % 3],
                  kbZ if w == NT - 1 else kbT[(w + 1) % 3]]
        vslots = [vbZ if w == 0 else vbr[(w - 1) % 3], vbr[w % 3],
                  vbZ if w == NT - 1 else vbr[(w + 1) % 3]]
        PS = [PA, PB, PC]
        for half in range(2):
            for s in range(3):
                for hp in range(4):
                    jj = 4 * half + hp
                    pr.t.matmul(PS[s][:, hp * 128:(hp + 1) * 128],
                                kslots[s][0:64, jj * 128:(jj + 1) * 128],
                                qbT[pw][0:64, jj * 128:(jj + 1) * 128],
                                tile_position=(0, 0))
                    pr.t.matmul(PS[s][:, 512 + hp * 128:640 + hp * 128],
                                kslots[s][64:128, jj * 128:(jj + 1) * 128],
                                qbT[pw][64:128, jj * 128:(jj + 1) * 128],
                                tile_position=(64, 0))
            for s in range(3):
                pr.s.activation(ptb[s][:], PS[s][:], AF.Exp, scale=0.125)
                pr.v.tensor_tensor(ptb[s][:], ptb[s][:], masks[s][:], ALU.mult)
            for hh in range(8):
                h = half * 8 + hh
                ncol = (hh % 2) * 512 + (hh // 2) * 128
                for s in range(3):
                    pr.t.matmul(PD[0:65, hh * 128:(hh + 1) * 128],
                                vslots[s][0:64, h * 65:(h + 1) * 65],
                                ptb[s][0:64, ncol:ncol + 128],
                                start=(s == 0), stop=(s == 2),
                                tile_position=(0, 0))
                    pr.t.matmul(PA[0:65, hh * 128:(hh + 1) * 128],
                                vslots[s][64:128, h * 65:(h + 1) * 65],
                                ptb[s][64:128, ncol:ncol + 128],
                                start=(s == 0), stop=(s == 2),
                                tile_position=(64, 0))
            pr.v.tensor_copy(ctxs[:], PA[0:65, :])
            pr.v.tensor_tensor(ctxe[:], PD[0:65, :], ctxs[:], ALU.add)
            if half == 0:
                ep_fin(w, 0)



def emit_phase3c(nc, pr, env, st):
    ident, onesf, jt, epsc = (env[k] for k in ["ident", "onesf", "jt", "epsc"])
    win, x3_h, out_h = env["win"], env["x3_h"], env["out_h"]

    def sbt(name, shape, dtype):
        return st.enter_context(nc.sbuf_tensor(name, shape, dtype))

    def pst(name, shape, dtype=F32):
        return st.enter_context(nc.psum_tensor(name, shape, dtype))

    fr_sb = sbt("c_fr", [128, NCH * D], BF16)
    fk_sb = sbt("c_fk", [128, NCH * D], BF16)
    fv_sb = sbt("c_fv", [128, NCH * D], BF16)
    mur = sbt("c_mur", [128, NCH], F32)
    muk = sbt("c_muk", [128, NCH], F32)
    x_in = [sbt(f"c_x{i}", [128, D], F32) for i in range(2)]
    stat = sbt("c_stat", [128, 12], F32)
    mv = [sbt(f"c_mv{i}", [128, 2], F32) for i in range(2)]
    col = [sbt(f"c_col{i}", [128, 2], F32) for i in range(2)]
    ln_f = [sbt(f"c_ln{i}", [128, D], F32) for i in range(2)]
    lnT3 = [sbt(f"c_lnT{i}", [128, NCH * 130], BF16) for i in range(2)]
    xdf = sbt("c_xdf", [128, NCH * 128], BF16)
    xrT = sbt("c_xrT", [128, NCH * 128], BF16)
    xkT = sbt("c_xkT", [128, NCH * 128], BF16)
    r_sb = sbt("c_r", [128, D], F32)
    kr = sbt("c_kr", [128, D], F32)
    kaT = sbt("c_kaT", [128, NCH * 128], BF16)
    osb = [sbt(f"c_o{i}", [128, D], F32) for i in range(2)]
    psF = pst("c_psF", [128, 1024])
    psK3 = pst("c_psK", [128, 1024])
    psT3 = pst("c_psT", [128, 1024])

    for w_sb, nm in ((fr_sb, "fr"), (fk_sb, "fk"), (fv_sb, "fv")):
        src_ = win[nm].ap().rearrange("(j p) d -> p j d", p=128)
        pr.g.dma_start(w_sb[:].rearrange("p (j d) -> p j d", d=D), src_)
    pr.g.dma_start(mur[:], win["mur_c"].ap())
    pr.g.dma_start(muk[:], win["muk_c"].ap())

    for t in range(NT):
        p = t % 2
        xb = x_in[p]
        pr.y.dma_start(xb[:], x3_h.ap()[t * TL:(t + 1) * TL, :])
        ln_chain(pr, stat, mv[p], col[p], epsc, xb, ln_f[p])
        for j in range(NCH):
            pr.t.transpose(psT3[:, j * 128:(j + 1) * 128],
                           ln_f[p][:, j * 128:(j + 1) * 128], ident[:])
        lt = lnT3[p]
        ltprev = lnT3[1 - p]
        pr.v.tensor_copy(
            lt[:].rearrange("p (j q) -> p j q", q=130)[:, :, 1:129],
            psT3[:].rearrange("p (j q) -> p j q", q=128))
        if t == 0:
            pr.v.memset(lt[:].rearrange("p (j q) -> p j q", q=130)[:, :, 0:1], 0.0)
        else:
            pr.v.tensor_copy(
                lt[:].rearrange("p (j q) -> p j q", q=130)[:, :, 0:1],
                ltprev[:].rearrange("p (j q) -> p j q", q=130)[:, :, 128:129])
        ltv = lt[:].rearrange("p (j q) -> p j q", q=130)
        pr.v.tensor_tensor(xdf[:].rearrange("p (j q) -> p j q", q=128),
                           ltv[:, :, 1:129], ltv[:, :, 0:128], ALU.subtract)
        for j in range(NCH):
            pr.v.scalar_tensor_tensor(
                xrT[:, j * 128:(j + 1) * 128], xdf[:, j * 128:(j + 1) * 128],
                mur[:, j:j + 1], lt[:, j * 130: j * 130 + 128], ALU.mult, ALU.add)
            pr.v.scalar_tensor_tensor(
                xkT[:, j * 128:(j + 1) * 128], xdf[:, j * 128:(j + 1) * 128],
                muk[:, j:j + 1], lt[:, j * 130: j * 130 + 128], ALU.mult, ALU.add)
        for half in range(2):
            for j in range(NCH):
                pr.t.matmul(psF[:, half * 512:(half + 1) * 512],
                            xrT[:, j * 128:(j + 1) * 128],
                            fr_sb[:, j * D + half * 512: j * D + (half + 1) * 512],
                            start=(j == 0), stop=(j == NCH - 1))
            for j in range(NCH):
                pr.t.matmul(psK3[:, half * 512:(half + 1) * 512],
                            xkT[:, j * 128:(j + 1) * 128],
                            fk_sb[:, j * D + half * 512: j * D + (half + 1) * 512],
                            start=(j == 0), stop=(j == NCH - 1))
        pr.s.activation(r_sb[:], psF[:], AF.Sigmoid)
        pr.s.activation(kr[:], psK3[:], AF.Relu)
        for j in range(NCH):
            pr.t.transpose(psT3[:, j * 128:(j + 1) * 128],
                           kr[:, j * 128:(j + 1) * 128], ident[:])
        pr.s.activation(kaT[:], psT3[:], AF.Square)
        for half in range(2):
            for j in range(NCH):
                pr.t.matmul(psF[:, half * 512:(half + 1) * 512],
                            kaT[:, j * 128:(j + 1) * 128],
                            fv_sb[:, j * D + half * 512: j * D + (half + 1) * 512],
                            start=(j == 0), stop=(j == NCH - 1))
        pr.v.tensor_tensor(r_sb[:], r_sb[:], psF[:], ALU.mult)
        pr.v.tensor_tensor(osb[p][:], r_sb[:], xb[:], ALU.add)
        pr.y.dma_start(env["out_h"].ap()[t * TL:(t + 1) * TL, :], osb[p][:])


def emit_phase3a(nc, pr, env, st):
    ident, onesf, jt, epsc = (env[k] for k in ["ident", "onesf", "jt", "epsc"])
    x_h, win, ctx_h, x2_h = env["x_h"], env["win"], env["ctx_h"], env["x2_h"]

    def sbt(name, shape, dtype):
        return st.enter_context(nc.sbuf_tensor(name, shape, dtype))

    def pst(name, shape, dtype=F32):
        return st.enter_context(nc.psum_tensor(name, shape, dtype))

    wo_sb = sbt("p3_wo", [128, NCH * D], BF16)
    fr_sb = sbt("p3_fr", [128, NCH * D], BF16)
    fk_sb = sbt("p3_fk", [128, NCH * D], BF16)
    fv_sb = sbt("p3_fv", [128, NCH * D], BF16)
    mur = sbt("p3_mur", [128, NCH], F32)
    muk = sbt("p3_muk", [128, NCH], F32)
    rows = sbt("p3_rows", [1, 3 * D], F32)
    bob = sbt("p3_bob", [128, D], F32)
    gab = sbt("p3_gab", [128, D], F32)
    beb = sbt("p3_beb", [128, D], F32)
    x_in = [sbt(f"p3_x{i}", [128, D], F32) for i in range(2)]
    ct_in = [sbt(f"p3_ct{i}", [128, NCH * 128], BF16) for i in range(2)]
    stat = sbt("p3_stat", [128, 12], F32)
    mv = [sbt(f"p3_mv{i}", [128, 2], F32) for i in range(2)]
    col = [sbt(f"p3_col{i}", [128, 2], F32) for i in range(2)]
    ln_f = [sbt(f"p3_ln{i}", [128, D], F32) for i in range(2)]
    ysb = sbt("p3_y", [128, D], F32)
    x1sb = [sbt(f"p3_x1_{i}", [128, D], F32) for i in range(2)]
    lnT3 = [sbt(f"p3_lnT{i}", [128, NCH * 130], BF16) for i in range(2)]
    xdf = sbt("p3_xdf", [128, NCH * 128], BF16)
    xrT = sbt("p3_xrT", [128, NCH * 128], BF16)
    xkT = sbt("p3_xkT", [128, NCH * 128], BF16)
    r_sb = sbt("p3_r", [128, D], F32)
    kr = sbt("p3_kr", [128, D], F32)
    kaT = sbt("p3_kaT", [128, NCH * 128], BF16)
    x2sb = [sbt(f"p3_x2_{i}", [128, D], F32) for i in range(2)]
    psO = pst("p3_psO", [128, 1024])
    psF = pst("p3_psF", [128, 1024])
    psK3 = pst("p3_psK", [128, 1024])
    psT3 = pst("p3_psT", [128, 1024])

    for w_sb, nm in ((wo_sb, "wo"), (fr_sb, "fr"), (fk_sb, "fk"), (fv_sb, "fv")):
        src_ = win[nm].ap().rearrange("(j p) d -> p j d", p=128)
        pr.g.dma_start(w_sb[:].rearrange("p (j d) -> p j d", d=D), src_)
    pr.g.dma_start(mur[:], win["mur_c"].ap())
    pr.g.dma_start(muk[:], win["muk_c"].ap())
    pr.g.dma_start(rows[0:1, 0:D], win["bo_r"].ap())
    pr.g.dma_start(rows[0:1, D:2 * D], win["gam_r"].ap())
    pr.g.dma_start(rows[0:1, 2 * D:3 * D], win["bet_r"].ap())
    for bt, off in ((bob, 0), (gab, D), (beb, 2 * D)):
        for half in range(2):
            pr.t.matmul(psO[:, half * 512:(half + 1) * 512], onesf[:],
                        rows[0:1, off + half * 512: off + (half + 1) * 512])
        pr.v.tensor_copy(bt[:], psO[:])

    def ln_chain(xb, p, out_f):
        pr.v.bn_stats(stat[:, 0:6], xb[:, 0:512])
        pr.v.bn_stats(stat[:, 6:12], xb[:, 512:1024])
        pr.v.bn_aggr(mv[p][:], stat[:])
        pr.s.activation(col[p][:, 0:1], mv[p][:, 1:2], AF.Sqrt, bias=epsc[:, 0:1])
        pr.v.reciprocal(col[p][:, 1:2], col[p][:, 0:1])
        pr.v.tensor_scalar(out_f[:], xb[:], mv[p][:, 0:1], col[p][:, 1:2],
                           ALU.subtract, ALU.mult)

    for t in range(NT):
        p = t % 2
        xb = x_in[p]
        ctb = ct_in[p]
        pr.y.dma_start(xb[:], x_h.ap()[t * TL:(t + 1) * TL, :])
        pr.y.dma_start(ctb[:], ctx_h.ap()[t])
        ln_chain(xb, p, ln_f[p])
        for half in range(2):
            for jj in range(NCH):
                pr.t.matmul(psO[:, half * 512:(half + 1) * 512],
                            ctb[:, jj * 128:(jj + 1) * 128],
                            wo_sb[:, jj * D + half * 512: jj * D + (half + 1) * 512],
                            start=(jj == 0), stop=(jj == NCH - 1))
        pr.v.scalar_tensor_tensor(ysb[:], psO[:], 1.0, bob[:], ALU.mult, ALU.add)
        pr.v.tensor_tensor(ysb[:], ysb[:], ln_f[p][:], ALU.add)
        ln_chain(ysb, p, ln_f[p])
        pr.v.tensor_tensor(ln_f[p][:], ln_f[p][:], gab[:], ALU.mult)
        pr.v.tensor_tensor(ln_f[p][:], ln_f[p][:], beb[:], ALU.add)
        pr.v.tensor_tensor(x1sb[p][:], ln_f[p][:], xb[:], ALU.add)
        ln_chain(x1sb[p], p, ln_f[p])
        for j in range(NCH):
            pr.t.transpose(psT3[:, j * 128:(j + 1) * 128],
                           ln_f[p][:, j * 128:(j + 1) * 128], ident[:])
        lt = lnT3[p]
        ltprev = lnT3[1 - p]
        pr.v.tensor_copy(
            lt[:].rearrange("p (j q) -> p j q", q=130)[:, :, 1:129],
            psT3[:].rearrange("p (j q) -> p j q", q=128))
        if t == 0:
            pr.v.memset(lt[:].rearrange("p (j q) -> p j q", q=130)[:, :, 0:1], 0.0)
        else:
            pr.v.tensor_copy(
                lt[:].rearrange("p (j q) -> p j q", q=130)[:, :, 0:1],
                ltprev[:].rearrange("p (j q) -> p j q", q=130)[:, :, 128:129])
        ltv = lt[:].rearrange("p (j q) -> p j q", q=130)
        pr.v.tensor_tensor(xdf[:].rearrange("p (j q) -> p j q", q=128),
                           ltv[:, :, 1:129], ltv[:, :, 0:128], ALU.subtract)
        for j in range(NCH):
            pr.v.scalar_tensor_tensor(
                xrT[:, j * 128:(j + 1) * 128], xdf[:, j * 128:(j + 1) * 128],
                mur[:, j:j + 1], lt[:, j * 130: j * 130 + 128], ALU.mult, ALU.add)
            pr.v.scalar_tensor_tensor(
                xkT[:, j * 128:(j + 1) * 128], xdf[:, j * 128:(j + 1) * 128],
                muk[:, j:j + 1], lt[:, j * 130: j * 130 + 128], ALU.mult, ALU.add)
        for half in range(2):
            for j in range(NCH):
                pr.t.matmul(psF[:, half * 512:(half + 1) * 512],
                            xrT[:, j * 128:(j + 1) * 128],
                            fr_sb[:, j * D + half * 512: j * D + (half + 1) * 512],
                            start=(j == 0), stop=(j == NCH - 1))
            for j in range(NCH):
                pr.t.matmul(psK3[:, half * 512:(half + 1) * 512],
                            xkT[:, j * 128:(j + 1) * 128],
                            fk_sb[:, j * D + half * 512: j * D + (half + 1) * 512],
                            start=(j == 0), stop=(j == NCH - 1))
        pr.s.activation(r_sb[:], psF[:], AF.Sigmoid)
        pr.s.activation(kr[:], psK3[:], AF.Relu)
        for j in range(NCH):
            pr.t.transpose(psT3[:, j * 128:(j + 1) * 128],
                           kr[:, j * 128:(j + 1) * 128], ident[:])
        pr.s.activation(kaT[:], psT3[:], AF.Square)
        for half in range(2):
            for j in range(NCH):
                pr.t.matmul(psF[:, half * 512:(half + 1) * 512],
                            kaT[:, j * 128:(j + 1) * 128],
                            fv_sb[:, j * D + half * 512: j * D + (half + 1) * 512],
                            start=(j == 0), stop=(j == NCH - 1))
        pr.v.tensor_tensor(r_sb[:], r_sb[:], psF[:], ALU.mult)
        pr.v.tensor_tensor(x2sb[p][:], r_sb[:], x1sb[p][:], ALU.add)
        pr.y.dma_start(x2_h.ap()[t * TL:(t + 1) * TL, :], x2sb[p][:])


def emit_phase2(nc, pr, env, st):
    kres, vres, ident, onesf, jt, epsc = (env[k] for k in
        ["kres", "vres", "ident", "onesf", "jt", "epsc"])
    x_h, win, ctx_h = env["x_h"], env["win"], env["ctx_h"]
    rcph = env["rcph"]

    def sbt(name, shape, dtype):
        return st.enter_context(nc.sbuf_tensor(name, shape, dtype))

    def pst(name, shape, dtype=F32):
        return st.enter_context(nc.psum_tensor(name, shape, dtype))

    wq_sb = sbt("p2_wq", [128, NCH * D], BF16)
    bq_c = sbt("p2_bqc", [128, NCH], F32)
    xi0 = sbt("p2_x0", [128, D], F32)
    x_in = [xi0, xi0]
    stat = sbt("p2_stat", [128, 12], F32)
    mv = [sbt(f"p2_mv{i}", [128, 2], F32) for i in range(2)]
    col = [sbt(f"p2_col{i}", [128, 2], F32) for i in range(2)]
    ln_f = [sbt(f"p2_ln{i}", [128, D], F32) for i in range(2)]
    lnT = sbt("p2_lnT", [128, NCH * 512], BF16)
    qt = sbt("p2_qt", [128, NCH * 512], BF16)
    pt = [sbt(f"p2_pt{i}", [128, 2048], BF16) for i in range(2)]
    ctxe = [sbt(f"p2_ce{i}", [65, 512], F32) for i in range(2)]
    ctxs = sbt("p2_cs", [65, 512], F32)
    rcp = [sbt(f"p2_rcp{i}", [1, 512], F32) for i in range(2)]
    rcpb = [sbt(f"p2_rb{i}", [64, 512], F32) for i in range(2)]
    cnT = sbt("p2_cnT", [128, NCH * 512], BF16)
    psS = pst("p2_psS", [128, 2048])
    psC = [pst(f"p2_psC{i}", [65, 512]) for i in range(4)]

    G = NT // GT

    src = win["wq"].ap().rearrange("(j p) d -> p j d", p=128)
    pr.g.dma_start(wq_sb[:].rearrange("p (j d) -> p j d", d=D), src)
    pr.g.dma_start(bq_c[:], win["bq_c"].ap())

    for g in range(G):
        for u in range(GT):
            t = g * GT + u
            p = t % 2
            xb = x_in[p]
            pr.y.dma_start(xb[:], x_h.ap()[t * TL:(t + 1) * TL, :])
            pr.v.bn_stats(stat[:, 0:6], xb[:, 0:512])
            pr.v.bn_stats(stat[:, 6:12], xb[:, 512:1024])
            pr.v.bn_aggr(mv[p][:], stat[:])
            pr.s.activation(col[p][:, 0:1], mv[p][:, 1:2], AF.Sqrt, bias=epsc[:, 0:1])
            pr.v.reciprocal(col[p][:, 1:2], col[p][:, 0:1])
            pr.v.tensor_scalar(ln_f[p][:], xb[:], mv[p][:, 0:1], col[p][:, 1:2],
                               ALU.subtract, ALU.mult)
            for hf in range(2):
                for j in range(4):
                    jj = hf * 4 + j
                    pr.t.transpose(psS[:, j * 128:(j + 1) * 128],
                                   ln_f[p][:, jj * 128:(jj + 1) * 128], ident[:])
                pr.v.tensor_copy(
                    lnT[:].rearrange("p (j q) -> p j q", q=512)
                    [:, hf * 4:(hf + 1) * 4, u * 128:(u + 1) * 128],
                    psS[:, 0:512].rearrange("p (j q) -> p j q", q=128))
        for i in range(NCH):
            psQ = psS[:, 512:1024] if i % 2 == 0 else psS[:, 1536:2048]
            for j in range(NCH):
                pr.t.matmul(psQ, wq_sb[:, j * D + i * 128: j * D + (i + 1) * 128],
                            lnT[:, j * 512:(j + 1) * 512],
                            start=(j == 0), stop=(j == NCH - 1))
            pr.s.activation(qt[:, i * 512:(i + 1) * 512], psQ,
                            AF.Identity, bias=bq_c[:, i:i + 1])
        NB = NT // 2

        def s_mms(jj, b):
            for half in range(2):
                j = 2 * b + half
                pr.t.matmul(psS[:, half * 512:(half + 1) * 512],
                            kres[0:64, j * D + jj * 128: j * D + (jj + 1) * 128],
                            qt[0:64, jj * 512:(jj + 1) * 512],
                            tile_position=(0, 0))
                pr.t.matmul(psS[:, 1024 + half * 512:1536 + half * 512],
                            kres[64:128, j * D + jj * 128: j * D + (jj + 1) * 128],
                            qt[64:128, jj * 512:(jj + 1) * 512],
                            tile_position=(64, 0))

        def ctx_mms(jj, b):
            for ho in range(2):
                h = 2 * jj + ho
                for half in range(2):
                    j = 2 * b + half
                    pcol = ho * 1024 + half * 512
                    pr.t.matmul(psC[2 * ho][:],
                                vres[0:64, j * 1040 + h * 65: j * 1040 + (h + 1) * 65],
                                pt[b % 2][0:64, pcol:pcol + 512],
                                start=(j == 0), stop=(j == NT - 1),
                                tile_position=(0, 0))
                    pr.t.matmul(psC[2 * ho + 1][:],
                                vres[64:128, j * 1040 + h * 65: j * 1040 + (h + 1) * 65],
                                pt[b % 2][64:128, pcol:pcol + 512],
                                start=(j == 0), stop=(j == NT - 1),
                                tile_position=(64, 0))

        def ep_pre(jj):
            for ho in range(2):
                pr.v.tensor_copy(ctxs[:], psC[2 * ho + 1][:])
                pr.v.tensor_tensor(ctxe[ho][:], psC[2 * ho][:], ctxs[:], ALU.add)
                pr.v.reciprocal(rcp[ho][:], ctxe[ho][64:65, :])
                hap = rcph[ho].ap()[0:1, 0:512]
                pr.g.dma_start(hap, rcp[ho][:])
                bsrc = bass.broadcast_tensor_aps(hap, rcpb[ho][:])[0]
                pr.g.dma_start(rcpb[ho][:], bsrc)

        def ep_post(jj):
            for ho in range(2):
                pbase = 64 * ho
                pr.v.tensor_tensor(
                    cnT[pbase:pbase + 64, jj * 512:(jj + 1) * 512],
                    ctxe[ho][0:64, :], rcpb[ho][:], ALU.mult)

        for jj in range(NCH):
            for b in range(NB):
                s_mms(jj, b)
                pr.s.activation(pt[b % 2][:], psS[:], AF.Exp, scale=0.125)
                if b == 3 and jj > 0:
                    ep_post(jj - 1)
                if b >= 1:
                    ctx_mms(jj, b - 1)
            ctx_mms(jj, NB - 1)
            ep_pre(jj)
        ep_post(NCH - 1)
        for u in range(GT):
            t = g * GT + u
            pr.y.dma_start(
                ctx_h.ap()[t],
                cnT[:].rearrange("p (j q) -> p j q", q=512)[:, :, u * 128:(u + 1) * 128])


def emit_start(nc, pr, env):
    ident, onesf, jt, epsc = env["ident"], env["onesf"], env["jt"], env["epsc"]

    def s_clear(eng):
        for h_ in pr.all_sems():
            eng.sem_clear(h_)
    pr.ops["sync"].append((None, s_clear, "RAW"))

    def g_pre(eng):
        eng.memset(env["preg"][:], 0)
    pr.ops["gpsimd"].append((None, g_pre, "RAW"))

    def v_pre(eng):
        for i in range(16):
            eng.memset(env["prev"][:, i * 4:(i + 1) * 4], 0.0)
    pr.ops["vector"].append((None, v_pre, "RAW"))

    def c_pre(eng):
        for i in range(16):
            eng.memzero(env["pres"][:, i * 4:(i + 1) * 4])
    pr.ops["scalar"].append((None, c_pre, "RAW"))

    pr.g.iota(jt[:, 0:128], [[1, 128]], channel_multiplier=0,
              allow_small_or_imprecise_dtypes=True)
    pr.g.iota(jt[:, 128:256], [[0, 128]], channel_multiplier=1,
              allow_small_or_imprecise_dtypes=True)
    pr.v.tensor_tensor(ident[:], jt[:, 0:128], jt[:, 128:256], ALU.is_equal)
    pr.v.memset(onesf[:], 1.0)
    pr.v.memset(epsc[:], float(EPS))
    pr.v.memset(jt[:], 0.0)


def emit_phase1(nc, pr, env, st):
    kres, vres, ident, onesf, cm, jt, epsc = (env[k] for k in
        ["kres", "vres", "ident", "onesf", "cm", "jt", "epsc"])
    x_h, win, bi_h = env["x_h"], env["win"], env["bi_h"]

    def sbt(name, shape, dtype):
        return st.enter_context(nc.sbuf_tensor(name, shape, dtype))

    def pst(name, shape, dtype=F32):
        return st.enter_context(nc.psum_tensor(name, shape, dtype))

    wk_sb = sbt("p1_wk", [128, NCH * D], BF16)
    wv_sb = sbt("p1_wv", [128, NCH * D], BF16)
    bk_c = sbt("p1_bkc", [128, NCH], F32)
    bvb = sbt("p1_bvb", [128, D], F32)
    x_in = [sbt(f"p1_x{i}", [128, D], F32) for i in range(2)]
    stat = sbt("p1_stat", [128, 12], F32)
    mv = [sbt(f"p1_mv{i}", [128, 2], F32) for i in range(2)]
    col = [sbt(f"p1_col{i}", [128, 2], F32) for i in range(2)]
    ln_f = [sbt(f"p1_ln{i}", [128, D], F32) for i in range(2)]
    lnT = sbt("p1_lnT", [128, NCH * 512], BF16)
    bi_sb = sbt("p1_bi", [1, 2048], F32)
    row1 = sbt("p1_row", [1, D], F32)
    mx8 = sbt("p1_mx8", [1, 8], F32)
    psT = pst("p1_psT", [128, 1024])
    psK = [pst(f"p1_psK{i}", [128, 512]) for i in range(2)]
    psV = [pst(f"p1_psV{i}", [128, 512]) for i in range(2)]
    psB = pst("p1_psB", [128, 1024])

    G = NT // GT

    for w_sb, nm in ((wk_sb, "wk"), (wv_sb, "wv")):
        src = win[nm].ap().rearrange("(j p) d -> p j d", p=128)
        pr.g.dma_start(w_sb[:].rearrange("p (j d) -> p j d", d=D), src)
    pr.g.dma_start(bk_c[:], win["bk_c"].ap())
    pr.g.dma_start(bi_sb[:], bi_h.ap())
    pr.g.dma_start(row1[:], win["bv_r"].ap())

    pr.v.max(mx8[:], bi_sb[:])
    pr.v.tensor_scalar(bi_sb[:], bi_sb[:], mx8[0:1, 2:3], None, ALU.is_lt)
    for t in range(16):
        pr.t.transpose(psB[:, t:t + 1], bi_sb[0:1, t * 128:(t + 1) * 128],
                       ident[0:1, 0:1])
    pr.v.tensor_copy(cm[:], psB[:, 0:16])
    pr.t.matmul(psB[:, 0:512], onesf[:], row1[0:1, 0:512])
    pr.t.matmul(psB[:, 512:1024], onesf[:], row1[0:1, 512:1024])
    pr.v.tensor_copy(bvb[:], psB[:])
    pr.v.memset(vres[:].rearrange("p (b o) -> p b o", o=65)[:, :, 64:65], 1.0)

    for g in range(G):
        for u in range(GT):
            t = g * GT + u
            p = t % 2
            xb = x_in[p]
            pr.y.dma_start(xb[:], x_h.ap()[t * TL:(t + 1) * TL, :])
            pr.v.bn_stats(stat[:, 0:6], xb[:, 0:512])
            pr.v.bn_stats(stat[:, 6:12], xb[:, 512:1024])
            pr.v.bn_aggr(mv[p][:], stat[:])
            pr.s.activation(col[p][:, 0:1], mv[p][:, 1:2], AF.Sqrt, bias=epsc[:, 0:1])
            pr.v.reciprocal(col[p][:, 1:2], col[p][:, 0:1])
            pr.v.tensor_scalar(ln_f[p][:], xb[:], mv[p][:, 0:1], col[p][:, 1:2],
                               ALU.subtract, ALU.mult)
            for j in range(NCH):
                pr.t.transpose(psT[:, j * 128:(j + 1) * 128],
                               ln_f[p][:, j * 128:(j + 1) * 128], ident[:])
            pr.v.tensor_copy(
                lnT[:].rearrange("p (j q) -> p j q", q=512)[:, :, u * 128:(u + 1) * 128],
                psT[:].rearrange("p (j q) -> p j q", q=128))
        for i in range(NCH):
            pk = psK[i % 2]
            for j in range(NCH):
                pr.t.matmul(pk[:], wk_sb[:, j * D + i * 128: j * D + (i + 1) * 128],
                            lnT[:, j * 512:(j + 1) * 512],
                            start=(j == 0), stop=(j == NCH - 1))
            pr.s.activation(
                kres[:].rearrange("p (c j q) -> p c j q", j=NCH, q=128)
                [:, g * GT:(g + 1) * GT, i, :],
                pk[:].rearrange("p (u q) -> p u q", q=128),
                AF.Identity, bias=bk_c[:, i:i + 1])
        for u in range(GT):
            t = g * GT + u
            for half in range(2):
                pv = psV[half]
                for j in range(NCH):
                    pr.t.matmul(
                        pv[:],
                        lnT[:, j * 512 + u * 128: j * 512 + (u + 1) * 128],
                        wv_sb[:, j * D + half * 512: j * D + (half + 1) * 512],
                        start=(j == 0), stop=(j == NCH - 1))
            vblk = vres[:, t * 1040:(t + 1) * 1040]
            for half in range(2):
                pr.v.scalar_tensor_tensor(
                    vblk.rearrange("p (h o) -> p h o", o=65)[:, half * 8:(half + 1) * 8, 0:64],
                    psV[half][:].rearrange("p (h o) -> p h o", o=64),
                    1.0,
                    bvb[:, half * 512:(half + 1) * 512].rearrange("p (h o) -> p h o", o=64),
                    ALU.mult, ALU.add)
            if t < 16:
                pr.v.tensor_scalar(vblk, vblk, cm[:, t:t + 1], None, ALU.mult)



def stage_inputs(inputs):
    x = np.ascontiguousarray(np.asarray(inputs["L0_attn_a_x"], np.float32))
    tp = lambda a: np.ascontiguousarray(np.asarray(a, np.float32).T)
    chunked = lambda a: np.ascontiguousarray(np.asarray(a, np.float32).reshape(NCH, 128).T)
    row = lambda a: np.ascontiguousarray(np.asarray(a, np.float32)[None, :])

    common = {
        "wq_t": tp(inputs["L0_attn_a_Wq"]), "wk_t": tp(inputs["L0_attn_a_Wk"]),
        "wv_t": tp(inputs["L0_attn_a_Wv"]), "wo_t": tp(inputs["L0_attn_a_Wo"]),
        "bqw_t": tp(inputs["L0_attn_b_q_w"]), "bkw_t": tp(inputs["L0_attn_b_k_w"]),
        "bvw_t": tp(inputs["L0_attn_b_v_w"]), "bow_t": tp(inputs["L0_attn_b_o_w"]),
        "fr_t": tp(inputs["L0_ffn_W_r"]), "fk_t": tp(inputs["L0_ffn_W_k"]),
        "fv_t": tp(inputs["L0_ffn_W_v"]),
        "bq_c": chunked(inputs["L0_attn_a_bq"]), "bk_c": chunked(inputs["L0_attn_a_bk"]),
        "qb_c": chunked(inputs["L0_attn_b_q_b"]), "kb_c": chunked(inputs["L0_attn_b_k_b"]),
        "mur_c": chunked(inputs["L0_ffn_mu_r"]), "muk_c": chunked(inputs["L0_ffn_mu_k"]),
        "bv_r": row(inputs["L0_attn_a_bv"]), "bo_r": row(inputs["L0_attn_a_bo"]),
        "gam_r": row(inputs["L0_attn_a_gamma"]), "bet_r": row(inputs["L0_attn_a_beta"]),
        "vb_r": row(inputs["L0_attn_b_v_b"]), "ob_r": row(inputs["L0_attn_b_o_b"]),
        "bi": np.ascontiguousarray(np.asarray(inputs["L0_attn_a_band_importance"], np.float32)),
    }
    in_maps = []
    for c in range(8):
        m = dict(common)
        m["x0"] = np.ascontiguousarray(x[c])
        in_maps.append(m)
    return in_maps


def kernel(**inputs):
    in_maps = stage_inputs(inputs)
    nc = build_program(debug=0)
    res = run_bass_kernel_spmd(nc, in_maps, core_ids=list(range(8)))
    return np.stack([np.asarray(res.results[c]["out"], np.float32) for c in range(8)])
